# version 02
# baseline (speedup 1.0000x reference)
import math
import os
import contextlib
import numpy as np
import ml_dtypes
import concourse.bass as bass
import concourse.mybir as mybir
from concourse.bass_utils import run_bass_kernel_spmd
from concourse.alu_op_type import AluOpType as OP

F32 = mybir.dt.float32
BF16 = mybir.dt.bfloat16
AF = mybir.ActivationFunctionType

B, S, D = 4, 2048, 2048
NCORE = 8
TLOC = 1024
NT = TLOC // 128
NF = D // 128
NKC = S // 128
LN_EPS = 1e-5
NEG_BIG = -80.0
BF16NP = ml_dtypes.bfloat16

STAGES = int(os.environ.get("KSTAGES", "99"))



def host_constants(h):
    c = {}
    tpos = np.arange(TLOC, dtype=np.float64) + 1024.0 * h

    tri = np.ones((NKC, 128, 128), dtype=np.float32)
    lt = (np.arange(128)[:, None] <= np.arange(128)[None, :]).astype(np.float32)
    for kc in range(8 * h, 8 * h + 8):
        tri[kc] = lt
    c["tri"] = np.ascontiguousarray(tri.reshape(NKC * 128, 128)).astype(BF16NP)

    bcol = np.zeros((1, NKC), dtype=np.float32)
    for kc in range(NKC):
        if kc * 128 > (1024 * h + TLOC - 1):
            bcol[0, kc] = NEG_BIG
    c["bcol"] = bcol

    sc = np.ones((1, NKC), dtype=np.float32)
    if h == 0:
        for kc in range(1, 9):
            sc[0, kc] = 0.0
    c["sc"] = sc
    sc2 = np.ones((1, NKC), dtype=np.float32)
    if h == 1:
        for kc in range(9, NKC):
            sc2[0, kc] = 0.0
    c["sc2"] = sc2

    j = np.arange(64, dtype=np.float64)
    theta = 1.0 / (10000.0 ** (j / 64.0))
    ang = tpos[None, :] * theta[:, None]
    ca, sa = np.cos(ang), np.sin(ang)
    c["ropeA_c"] = np.concatenate([ca, ca], 0).astype(np.float32)
    c["ropeA_s"] = np.concatenate([-sa, sa], 0).astype(np.float32)

    jg = np.arange(32, dtype=np.float64)
    thg = 1.0 / (10000.0 ** (2.0 * jg / 64.0))
    angg = tpos[None, :] * thg[:, None]
    cg, sg = np.cos(angg), np.sin(angg)
    C2 = np.repeat(cg, 2, axis=0)
    S2 = np.empty((64, TLOC))
    S2[0::2] = -sg
    S2[1::2] = sg
    c["ropeG_c"] = np.concatenate([C2, C2], 0).astype(np.float32)
    c["ropeG_s"] = np.concatenate([S2, S2], 0).astype(np.float32)

    u = np.linspace(0.0, 1.0, 64)
    fr = (8000.0 / 200.0) ** u * (200.0 / 1000.0)
    freq = 200.0 * fr / 1000.0
    angb = tpos[None, :] * freq[:, None]
    cb, sb = np.cos(angb), np.sin(angb)
    qsc4 = 128.0 ** (-0.25)
    c["ropeB_c"] = (np.repeat(cb, 2, axis=0) * qsc4).astype(np.float32)
    SB = np.empty((128, TLOC))
    SB[0::2] = -sb
    SB[1::2] = sb
    c["ropeB_s"] = (SB * qsc4).astype(np.float32)
    c["id128"] = np.eye(128, dtype=np.float32)
    return c


CONST_SPECS = [
    ("tri", [NKC * 128, 128], BF16),
    ("bcol", [1, NKC], F32),
    ("sc", [1, NKC], F32),
    ("sc2", [1, NKC], F32),
    ("ropeA_c", [128, TLOC], F32),
    ("ropeA_s", [128, TLOC], F32),
    ("ropeG_c", [128, TLOC], F32),
    ("ropeG_s", [128, TLOC], F32),
    ("ropeB_c", [128, TLOC], F32),
    ("ropeB_s", [128, TLOC], F32),
    ("id128", [128, 128], F32),
]



class Sem:
    __slots__ = ("h", "n")
    def __init__(self, stack, nc, name):
        self.h = stack.enter_context(nc.semaphore(name))
        self.n = 0
    def inc(self, instr, k=1):
        instr.then_inc(self.h, k)
        self.n += k
        return instr
    def incw(self, eng, instr, k=1):
        instr.then_inc(self.h, k)
        self.n += k
        eng.wait_ge(self.h, self.n)
        return instr


class Ctx:
    def __init__(self, nc, stack):
        self.nc = nc
        self.stack = stack
        names = ["pe", "act", "dve", "cc", "xin", "tab", "p0", "xh", "xt",
                 "w0", "w1", "kv0", "kv1", "csrc", "vr0", "vr1", "vr2", "vr3",
                 "msc", "msb", "msd", "out", "q0", "q1"]
        self.sem = {n: Sem(stack, nc, "s_" + n) for n in names}

    def sb(self, name, shape, dtype):
        return self.stack.enter_context(self.nc.sbuf_tensor(name, shape, dtype))

    def ps(self, name, shape, dtype=F32):
        return self.stack.enter_context(self.nc.psum_tensor(name, shape, dtype))


XOR1_MASK = [i ^ 1 for i in range(32)]


def wait(eng, *sems):
    for s in sems:
        if s.n > 0:
            eng.wait_ge(s.h, s.n)


def mm512(t, ps_ap, lhsT, rhs_ap, start, stop, width=1024):
    mm = None
    for hw in range(0, width, 512):
        w = min(512, width - hw)
        mm = t.matmul(ps_ap[:, hw:hw + w], lhsT, rhs_ap[:, hw:hw + w],
                      start=start, stop=stop)
    return mm


def alloc_scratch(ctx):
    sc = {}
    sc["x"] = [ctx.sb(f"x{i}", [128, D], F32) for i in range(NT)]
    sc["xhat"] = ctx.sb("xhat", [128, 2 * D], BF16)
    sc["xhatT"] = ctx.sb("xhatT", [128, NF * TLOC], BF16)
    sc["qbuf"] = ctx.sb("qbuf", [128, 8 * TLOC], BF16)
    sc["ctxh"] = ctx.sb("ctxh", [128, 8 * TLOC], BF16)
    sc["kbuf"] = [ctx.sb(f"kbuf{i}", [128, 2048], BF16) for i in range(2)]
    sc["vbuf"] = [ctx.sb(f"vbuf{i}", [128, 2048], BF16) for i in range(2)]
    sc["wbuf"] = [ctx.sb(f"wbuf{i}", [128, NF * 256], BF16) for i in range(2)]
    sc["tA"] = ctx.sb("tA", [128, 1024], BF16)
    sc["tB"] = ctx.sb("tB", [128, 1024], BF16)
    sc["tC"] = ctx.sb("tC", [128, 1024], BF16)
    sc["tD"] = ctx.sb("tD", [128, 1024], BF16)
    sc["pT"] = [ctx.sb(f"pT{i}", [128, 1024], BF16) for i in range(2)]
    sc["ones_col_bf"] = ctx.sb("ones_col_bf", [128, 1], BF16)
    sc["ones_row_f"] = ctx.sb("ones_row_f", [1, 128], F32)
    sc["nrm"] = ctx.sb("nrm", [1, 128], F32)
    sc["nrm2"] = ctx.sb("nrm2", [1, 128], F32)
    sc["bcolr"] = ctx.sb("bcolr", [128, NKC], F32)
    sc["bcolr0"] = ctx.sb("bcolr0", [128, NKC], F32)
    sc["scr"] = ctx.sb("scr", [128, NKC], F32)
    sc["scr2"] = ctx.sb("scr2", [128, NKC], F32)
    sc["biasmat"] = ctx.sb("biasmat", [128, 16 * NKC], F32)
    sc["rcp"] = ctx.sb("rcp", [1, 1024], F32)
    sc["smallf"] = ctx.sb("smallf", [128, 64], F32)
    sc["smallg"] = ctx.sb("smallg", [1, 16], F32)
    sc["qsc"] = ctx.sb("qsc", [128, 16], BF16)
    sc["qscf"] = ctx.sb("qscf", [128, 32], F32)
    sc["gts"] = ctx.sb("gts", [128, 64], F32)
    sc["gtsb"] = ctx.sb("gtsb", [128, 64], BF16)
    sc["gtmp"] = ctx.sb("gtmp", [128, 24], F32)
    sc["gtT"] = ctx.sb("gtT", [8, 1024], BF16)
    sc["b2row"] = ctx.sb("b2row", [8, 512], BF16)
    sc["brow"] = sc["b2row"]
    sc["ones_row_bf"] = ctx.sb("ones_row_bf", [1, 128], BF16)
    sc["b1sb"] = ctx.sb("b1sb", [128, 64], F32)
    sc["sb1sb"] = ctx.sb("sb1sb", [128, 16], F32)
    sc["id128_bf"] = ctx.sb("id128_bf", [128, 128], BF16)
    sc["row2"] = ctx.sb("row2", [1, 1024], F32)
    sc["lnhg"] = ctx.sb("lnhg", [128, 2], F32)
    sc["bqsb"] = ctx.sb("bqsb", [128, 16], F32)
    sc["bksb"] = ctx.sb("bksb", [128, 16], F32)
    sc["bvsb"] = ctx.sb("bvsb", [128, 16], F32)
    sc["ropeC_t"] = ctx.sb("ropeC_t", [128, TLOC], BF16)
    sc["ropeS_t"] = ctx.sb("ropeS_t", [128, TLOC], BF16)
    sc["tri_sb"] = ctx.sb("tri_sb", [128, NKC * 128], BF16)
    sc["ps_a"] = ctx.ps("ps_a", [128, 1024])
    sc["ps_b"] = ctx.ps("ps_b", [128, 1024])
    sc["ps_c"] = ctx.ps("ps_c", [128, 1024])
    sc["ps_d"] = ctx.ps("ps_d", [128, 1024])
    return sc



def build_startup(ctx, sc, XIN, CT):
    nc = ctx.nc
    sem = ctx.sem
    with nc.Block("startup") as blk:
        @blk.sync
        def _(sy):
            for i in range(NT):
                sem["xin"].inc(sy.dma_start(sc["x"][i][:], XIN[128 * i:128 * (i + 1), :]), 16)
            sem["xin"].inc(sy.dma_start(
                sc["tri_sb"][:].rearrange("p (kc q) -> p kc q", q=128),
                CT["tri"].rearrange("(kc p) q -> p kc q", p=128)), 16)

        @blk.gpsimd
        def _(g):
            sem["tab"].inc(g.dma_start(sc["id128_bf"][:], CT["id128"][:, :]), 16)
            sem["tab"].inc(g.dma_start(sc["nrm"][0:1, 0:NKC], CT["bcol"][:, :]), 16)
            sem["tab"].inc(g.dma_start(sc["nrm2"][0:1, 0:NKC], CT["sc"][:, :]), 16)
            sem["tab"].inc(g.dma_start(sc["smallg"][0:1, 0:NKC], CT["sc2"][:, :]), 16)

        @blk.vector
        def _(v):
            v.memset(sc["ones_col_bf"][:], 1.0)
            v.memset(sc["ones_row_bf"][:], 1.0)
            v.memset(sc["bcolr0"][:], 0.0)
            sem["dve"].incw(v, v.memset(sc["ones_row_f"][:], 1.0))

        @blk.tensor
        def _(t):
            wait(t, sem["tab"], sem["dve"])
            t.matmul(sc["ps_a"][:, 0:NKC], sc["ones_row_f"][:],
                     sc["nrm"][0:1, 0:NKC], start=True, stop=True)
            t.matmul(sc["ps_b"][:, 0:NKC], sc["ones_row_f"][:],
                     sc["nrm2"][0:1, 0:NKC], start=True, stop=True)
            sem["pe"].incw(t, t.matmul(sc["ps_c"][:, 0:NKC], sc["ones_row_f"][:],
                                   sc["smallg"][0:1, 0:NKC], start=True, stop=True))

        @blk.scalar
        def _(s):
            wait(s, sem["pe"])
            s.copy(sc["bcolr"][:], sc["ps_a"][:, 0:NKC])
            s.copy(sc["scr"][:], sc["ps_b"][:, 0:NKC])
            sem["act"].incw(s, s.copy(sc["scr2"][:], sc["ps_c"][:, 0:NKC]))


def build_loadrope(ctx, sc, CT, cname, sname, tag):
    nc = ctx.nc
    sem = ctx.sem
    with nc.Block(f"lr_{tag}") as blk:
        @blk.gpsimd
        def _(g):
            wait(g, sem["dve"], sem["csrc"])
            sem["tab"].inc(g.dma_start(sc["ropeC_t"][:], CT[cname][:, :]), 16)
            sem["tab"].inc(g.dma_start(sc["ropeS_t"][:], CT[sname][:, :]), 16)


def build_phase0_cast(ctx, pairs, tag):
    nc = ctx.nc
    sem = ctx.sem
    with nc.Block(f"p0_{tag}") as blk:
        @blk.gpsimd
        def _(g):
            for srcap, dstap in pairs:
                sem["p0"].inc(g.dma_start(dstap, srcap), 16)



def build_ln(ctx, sc, tag, want_stats=False, eps=LN_EPS, to_hbm=None):
    nc = ctx.nc
    sem = ctx.sem
    bn = sc["smallf"]
    xhat = sc["xhat"]
    act0 = sem["act"].n
    dve0 = sem["dve"].n
    xh0 = sem["xh"].n
    NV = 9
    with nc.Block(f"ln_{tag}") as blk:
        @blk.vector
        def _(v):
            wait(v, sem["xin"], sem["pe"], sem["act"])

            def vf(instr):
                sem["dve"].inc(instr)
                v.wait_ge(sem["dve"].h, sem["dve"].n)

            for i in range(NT):
                base = sem["dve"].n
                xt = sc["x"][i]
                st = None
                for c4 in range(4):
                    st = v.bn_stats(bn[:, 6 * c4:6 * c4 + 6], xt[:, 512 * c4:512 * (c4 + 1)])
                vf(st)
                vf(v.bn_aggr(bn[:, 24:26], bn[:, 0:24]))
                if want_stats:
                    v.tensor_copy(sc["smallf"][:, 48 + 2 * i:50 + 2 * i], bn[:, 24:26])
                vf(v.tensor_scalar(bn[:, 26:27], bn[:, 25:26], eps, None, OP.add))
                vf(v.tensor_copy(bn[:, 30:31], bn[:, 26:27]))
                v.wait_ge(sem["act"].h, act0 + i + 1)
                vf(v.reciprocal(bn[:, 27:28], bn[:, 30:31]))
                vf(v.tensor_tensor(bn[:, 28:29], bn[:, 24:25], bn[:, 27:28], OP.mult))
                vf(v.tensor_scalar(bn[:, 28:29], bn[:, 28:29], -1.0, None, OP.mult))
                if to_hbm is not None and i >= 2:
                    v.wait_ge(sem["xh"].h, xh0 + i * 16)
                vf(v.tensor_scalar(xhat[:, D * (i % 2):D * (i % 2 + 1)], xt[:],
                                   bn[:, 27:28], bn[:, 28:29], OP.mult, OP.add))
                while sem["dve"].n < base + NV:
                    sem["dve"].inc(v.nop())
                assert sem["dve"].n == base + NV

        @blk.scalar
        def _(s):
            for i in range(NT):
                s.wait_ge(sem["dve"].h, dve0 + NV * i + 4)
                s.sqrt(bn[:, 30:31], bn[:, 30:31]).then_inc(sem["act"].h, 1)
                s.wait_ge(sem["act"].h, act0 + i + 1)
        sem["act"].n = act0 + NT

        if to_hbm is not None:
            @blk.gpsimd
            def _(g):
                for i in range(NT):
                    g.wait_ge(sem["dve"].h, dve0 + NV * i + 8)
                    g.dma_start(to_hbm[128 * i:128 * (i + 1), :],
                                sc["xhat"][:, D * (i % 2):D * (i % 2 + 1)]
                                ).then_inc(sem["xh"].h, 16)
            sem["xh"].n = xh0 + NT * 16


def build_bounce(ctx, sc, XH, tag):
    nc = ctx.nc
    sem = ctx.sem
    with nc.Block(f"tx_{tag}") as blk:
        @blk.sync
        def _(sy):
            wait(sy, sem["xh"], sem["msb"])
            for j in range(NF):
                sem["xt"].inc(sy.dma_start(sc["xhatT"][:, TLOC * j:TLOC * (j + 1)],
                                           XH[:, 128 * j:128 * (j + 1)], transpose=True), 16)



def build_headproj(ctx, sc, WBF, heads, dst_fn, norm_col0, ropeC, ropeS,
                   tag, scale=1.0, csrc_dma=None, war_gap=0, rope_mode="half",
                   qpad=None):
    nc = ctx.nc
    sem = ctx.sem
    ps_a, ps_b, ps_c = sc["ps_a"], sc["ps_b"], sc["ps_c"]
    npair = len(heads) // 2
    pairs = [(heads[2 * i], heads[2 * i + 1]) for i in range(npair)]

    pe0 = sem["pe"].n
    act0 = sem["act"].n
    dve0 = sem["dve"].n
    csrc0 = sem["csrc"].n
    wsem = [sem["w0"], sem["w1"]]
    w0n = [wsem[0].n, wsem[1].n]
    DPH = 3 if rope_mode == "half" else 4
    AP_ = 1 + (4 if qpad is not None else 0)

    with nc.Block(f"hp_{tag}") as blk:
        @blk.sync
        def _(sy):
            wait(sy, sem["p0"], sem["q0"], sem["q1"])
            for j, (ha, hb) in enumerate(pairs):
                if j >= 2:
                    sy.wait_ge(sem["pe"].h, pe0 + (j - 1) * 4)
                for f in range(NF):
                    sy.dma_start(
                        sc["wbuf"][j % 2][:, 256 * f:256 * (f + 1)],
                        WBF[128 * ha:128 * ha + 256, 128 * f:128 * (f + 1)],
                        transpose=True).then_inc(wsem[j % 2].h, 16)

        @blk.scalar
        def _(s):
            wait(s, sem["p0"], sem["q0"], sem["q1"])
            for j, (ha, hb) in enumerate(pairs):
                if j >= 2:
                    s.wait_ge(sem["pe"].h, pe0 + (j - 1) * 4)
                for o in range(2):
                    s.wait_ge(sem["pe"].h, pe0 + 4 * j + o + 1)
                    i_lin = 2 * j + o
                    if i_lin >= 1:
                        s.wait_ge(sem["dve"].h, dve0 + DPH * (i_lin - 1) + DPH - 1)
                    ps = ps_a if o == 0 else ps_b
                    if rope_mode == "half":
                        s.activation(sc["tA"][:], ps[:], AF.Copy, scale=scale)
                        s.activation(sc["tB"][0:64, :], ps[64:128, :], AF.Copy, scale=scale)
                        s.activation(sc["tB"][64:128, :], ps[0:64, :], AF.Copy,
                                     scale=scale).then_inc(sem["act"].h, 1)
                    else:
                        s.activation(sc["tA"][:], ps[:], AF.Copy,
                                     scale=scale).then_inc(sem["act"].h, 1)
                    s.wait_ge(sem["act"].h, act0 + AP_ * i_lin + 1)
                    if qpad is not None:
                        s.wait_ge(sem["dve"].h, dve0 + DPH * i_lin + DPH - 1)
                        stg = dst_fn(i_lin, 0)
                        (apA, kvhA), (apB, kvhB) = qpad(i_lin)
                        s.memzero(apA).then_inc(sem["act"].h, 1)
                        s.wait_ge(sem["act"].h, act0 + AP_ * i_lin + 2)
                        s.copy(apA[kvhA:kvhA + 64, :], stg[0:64, :]
                               ).then_inc(sem["act"].h, 1)
                        s.wait_ge(sem["act"].h, act0 + AP_ * i_lin + 3)
                        s.memzero(apB).then_inc(sem["act"].h, 1)
                        s.wait_ge(sem["act"].h, act0 + AP_ * i_lin + 4)
                        s.copy(apB[kvhB:kvhB + 64, :], stg[64:128, :]
                               ).then_inc(sem["act"].h, 1)
                        s.wait_ge(sem["act"].h, act0 + AP_ * i_lin + 5)

        sem["act"].n = act0 + npair * 2 * AP_
        for j in range(npair):
            wsem[j % 2].n += 16 * 16

        @blk.tensor
        def _(t):
            wait(t, sem["xt"])
            for j, (ha, hb) in enumerate(pairs):
                t.wait_ge(wsem[j % 2].h, w0n[j % 2] + (j // 2 + 1) * 16 * 16)
                for o in range(2):
                    ps = ps_a if o == 0 else ps_b
                    if j >= 1:
                        i_prev = 2 * (j - 1) + o
                        t.wait_ge(sem["act"].h, act0 + AP_ * i_prev + 1)
                    mm = None
                    for f in range(NF):
                        mm = mm512(t, ps, sc["wbuf"][j % 2][:, 256 * f + 128 * o:256 * f + 128 * (o + 1)],
                                   sc["xhatT"][:, TLOC * f:TLOC * (f + 1)],
                                   start=(f == 0), stop=(f == NF - 1))
                    mm.then_inc(sem["pe"].h, 1)
                    t.wait_ge(sem["pe"].h, pe0 + 4 * j + o + 1)
                for o in range(2):
                    t.wait_ge(sem["dve"].h, dve0 + DPH * (2 * j + o) + DPH)
                    mm = mm512(t, ps_c[0:1, :], sc["ones_col_bf"][:],
                               sc["tC"][:], start=True, stop=True)
                    mm.then_inc(sem["pe"].h, 1)
                    t.wait_ge(sem["pe"].h, pe0 + 4 * j + 2 + o + 1)
        sem["pe"].n = pe0 + npair * 4

        @blk.vector
        def _(v):
            wait(v, sem["tab"])
            for j, (ha, hb) in enumerate(pairs):
                for o in range(2):
                    h = (ha, hb)[o]
                    i_lin = 2 * j + o
                    v.wait_ge(sem["act"].h, act0 + AP_ * i_lin + 1)
                    if csrc_dma is not None and i_lin >= 2:
                        v.wait_ge(sem["csrc"].h, csrc0 + i_lin * war_gap * 16)
                    if qpad is not None and i_lin >= 2:
                        v.wait_ge(sem["act"].h, act0 + AP_ * (i_lin - 2) + AP_)
                    dst = dst_fn(i_lin, h)
                    b0 = dve0 + DPH * i_lin
                    if rope_mode == "xor1":
                        v.stream_shuffle(sc["tB"][:], sc["tA"][:], XOR1_MASK
                                         ).then_inc(sem["dve"].h, 1)
                        v.wait_ge(sem["dve"].h, b0 + 1)
                    v.tensor_tensor(sc["tA"][:], sc["tA"][:], ropeC, OP.mult)
                    v.tensor_tensor(sc["tB"][:], sc["tB"][:], ropeS,
                                    OP.mult).then_inc(sem["dve"].h, 1)
                    v.wait_ge(sem["dve"].h, b0 + DPH - 2)
                    v.tensor_tensor(dst, sc["tA"][:], sc["tB"][:], OP.add
                                    ).then_inc(sem["dve"].h, 1)
                    v.wait_ge(sem["dve"].h, b0 + DPH - 1)
                    v.tensor_tensor(sc["tC"][:], dst, dst,
                                    OP.mult).then_inc(sem["dve"].h, 1)
                    v.wait_ge(sem["dve"].h, b0 + DPH)
                    v.wait_ge(sem["pe"].h, pe0 + 4 * j + 2 + o + 1)
                    v.tensor_reduce(sc["nrm"][0:1, norm_col0 + i_lin:norm_col0 + i_lin + 1],
                                    ps_c[0:1, 0:1024], axis=mybir.AxisListType.X,
                                    op=OP.max)
        sem["dve"].n = dve0 + DPH * 2 * npair

        if csrc_dma is not None:
            @blk.gpsimd
            def _(g):
                for j in range(npair):
                    for o in range(2):
                        i_lin = 2 * j + o
                        h = pairs[j][o]
                        g.wait_ge(sem["dve"].h, dve0 + DPH * i_lin + DPH - 1)
                        csrc_dma(g, i_lin, h)
            sem["csrc"].n = csrc0 + npair * 2 * war_gap * 16
    return



def build_vstage(ctx, sc, WVBF, CSRC, tag, ndim=2048, csrc_row0=2048,
                 natural=False, nat_col0=0):
    nc = ctx.nc
    sem = ctx.sem
    ps_a, ps_b = sc["ps_a"], sc["ps_b"]
    n_oc = ndim // 256
    pe0 = sem["pe"].n
    act0 = sem["act"].n
    wsem = [sem["w0"], sem["w1"]]
    w0n = [wsem[0].n, wsem[1].n]
    vrsem = [sem["vr0"], sem["vr1"], sem["vr2"], sem["vr3"]]
    vr0 = [sme.n for sme in vrsem]
    ring = 4
    inc_per = 16 if ndim == 2048 else 32

    with nc.Block(f"vs_{tag}") as blk:
        @blk.sync
        def _(sy):
            for oc in range(n_oc):
                if oc >= 2:
                    sy.wait_ge(sem["pe"].h, pe0 + (oc - 1) * NT)
                for f in range(NF):
                    if natural:
                        sy.dma_start(
                            sc["wbuf"][oc % 2][:, 256 * f:256 * f + 256],
                            WVBF[128 * f:128 * (f + 1),
                                 nat_col0 + 256 * oc:nat_col0 + 256 * (oc + 1)]
                            ).then_inc(wsem[oc % 2].h, 16)
                    else:
                        sy.dma_start(
                            sc["wbuf"][oc % 2][:, 256 * f:256 * f + 256],
                            WVBF[256 * oc:256 * (oc + 1), 128 * f:128 * (f + 1)],
                            transpose=True).then_inc(wsem[oc % 2].h, 16)

        @blk.scalar
        def _(s):
            for oc in range(n_oc):
                if oc >= 2:
                    s.wait_ge(sem["pe"].h, pe0 + (oc - 1) * NT)
                for tb in range(NT):
                    idx = oc * NT + tb
                    r = idx % ring
                    s.wait_ge(sem["pe"].h, pe0 + idx + 1)
                    if idx >= ring:
                        s.wait_ge(vrsem[r].h, vr0[r] + (idx // ring) * inc_per)
                    ps = ps_a if tb % 2 == 0 else ps_b
                    s.copy(sc["tD"][:, 256 * r:256 * (r + 1)],
                           ps[:, 0:256]).then_inc(sem["act"].h, 1)
                    s.wait_ge(sem["act"].h, act0 + idx + 1)
        sem["act"].n = act0 + n_oc * NT
        for oc in range(n_oc):
            wsem[oc % 2].n += 16 * 16

        @blk.tensor
        def _(t):
            for oc in range(n_oc):
                t.wait_ge(wsem[oc % 2].h, w0n[oc % 2] + (oc // 2 + 1) * 16 * 16)
                for tb in range(NT):
                    idx = oc * NT + tb
                    ps = ps_a if tb % 2 == 0 else ps_b
                    if idx >= 2:
                        t.wait_ge(sem["act"].h, act0 + idx - 1)
                    mm = None
                    for f in range(NF):
                        mm = t.matmul(ps[:, 0:256],
                                 sc["xhatT"][:, TLOC * f + 128 * tb:TLOC * f + 128 * (tb + 1)],
                                 sc["wbuf"][oc % 2][:, 256 * f:256 * f + 256],
                                 start=(f == 0), stop=(f == NF - 1))
                    mm.then_inc(sem["pe"].h, 1)
                    t.wait_ge(sem["pe"].h, pe0 + idx + 1)
        sem["pe"].n = pe0 + n_oc * NT

        @blk.gpsimd
        def _(g):
            for oc in range(n_oc):
                d0 = oc * 256
                for tb in range(NT):
                    idx = oc * NT + tb
                    r = idx % ring
                    g.wait_ge(sem["act"].h, act0 + idx + 1)
                    srcap = sc["tD"][:, 256 * r:256 * (r + 1)]
                    if ndim == 2048:
                        dhalf = d0 // 1024
                        dcol = d0 % 1024
                        dst = CSRC[csrc_row0 + 2 * 128 * tb + dhalf:
                                   csrc_row0 + 2 * 128 * (tb + 1) + dhalf:2,
                                   dcol:dcol + 256]
                        g.dma_start(dst, srcap).then_inc(vrsem[r].h, 16)
                    else:
                        dst = CSRC[csrc_row0 + 128 * tb:csrc_row0 + 128 * (tb + 1),
                                   d0:d0 + 256]
                        g.dma_start(dst, srcap).then_inc(vrsem[r].h, 16)
                        dst2 = CSRC[csrc_row0 + 128 * tb:csrc_row0 + 128 * (tb + 1),
                                    512 + d0:512 + d0 + 256]
                        g.dma_start(dst2, srcap).then_inc(vrsem[r].h, 16)
        for idx in range(n_oc * NT):
            vrsem[idx % 4].n += inc_per
    return



def build_gather(ctx, sc, CSRC_t, CDST_t, nheads_k, tag, norm_row=4096):
    nc = ctx.nc
    sem = ctx.sem
    with nc.Block(f"cc_{tag}") as blk:
        @blk.vector
        def _(v):
            wait(v, sem["dve"])
            sem["dve"].inc(v.memset(sc["rcp"][0:1, :], 0.0))
            v.wait_ge(sem["dve"].h, sem["dve"].n)
            sem["dve"].incw(v, v.tensor_copy(sc["rcp"][0:1, 0:nheads_k],
                                         sc["nrm"][0:1, 0:nheads_k]))

        @blk.gpsimd
        def _(g):
            wait(g, sem["dve"])
            sem["csrc"].inc(g.dma_start(
                CSRC_t.ap()[norm_row:norm_row + 1, 0:1024], sc["rcp"][0:1, :]), 16)
            wait(g, sem["csrc"], sem["vr0"], sem["vr1"], sem["vr2"], sem["vr3"])
            g.collective_compute(
                "AllGather",
                mybir.AluOpType.bypass,
                replica_groups=[[0, 1], [2, 3], [4, 5], [6, 7]],
                ins=[CSRC_t.ap().opt()],
                outs=[CDST_t.ap().opt()],
            ).then_inc(sem["cc"].h, 1)
            sem["cc"].n += 1
            g.wait_ge(sem["cc"].h, sem["cc"].n)
    return



def build_bias(ctx, sc, CDST, phase, nh_phase, hd, tag, kn_of_head=None,
               qn_of_head=None, nrow0=4096, nrow1=8193, col0=None,
               bcol_key="bcolr"):
    nc = ctx.nc
    sem = ctx.sem
    if kn_of_head is None:
        kn_of_head = [phase * nh_phase + i for i in range(nh_phase)]
    if qn_of_head is None:
        qn_of_head = [32 + phase * nh_phase + i for i in range(nh_phase)]
    if col0 is None:
        col0 = phase * nh_phase * NKC
    nk = max(kn_of_head) + 1
    with nc.Block(f"bias_{tag}{phase}") as blk:
        @blk.gpsimd
        def _(g):
            wait(g, sem["cc"])
            sem["msc"].inc(g.dma_start(sc["nrm2"][0:1, 0:nk], CDST[nrow0:nrow0 + 1, 0:nk]), 16)
            sem["msc"].inc(g.dma_start(sc["nrm2"][0:1, 32:32 + nk], CDST[nrow1:nrow1 + 1, 0:nk]), 16)

        @blk.vector
        def _(v):
            wait(v, sem["msc"], sem["dve"], sem["pe"])

            def vf(instr):
                sem["dve"].inc(instr)
                v.wait_ge(sem["dve"].h, sem["dve"].n)

            vf(v.tensor_tensor(sc["nrm2"][0:1, 64:64 + nk], sc["nrm2"][0:1, 0:nk],
                               sc["nrm2"][0:1, 32:32 + nk], OP.max))
            cp = None
            for i in range(nh_phase):
                cp = v.tensor_copy(sc["nrm2"][0:1, 96 + i:97 + i],
                                   sc["nrm2"][0:1, 64 + kn_of_head[i]:65 + kn_of_head[i]])
            vf(cp)
            cp = None
            for i in range(nh_phase):
                cp = v.tensor_copy(sc["nrm2"][0:1, 112 + i:113 + i],
                                   sc["nrm"][0:1, qn_of_head[i]:qn_of_head[i] + 1])
            vf(cp)
            vf(v.tensor_tensor(sc["smallg"][0:1, 0:nh_phase],
                               sc["nrm2"][0:1, 112:112 + nh_phase],
                               sc["nrm2"][0:1, 96:96 + nh_phase], OP.mult))

        @blk.scalar
        def _(s):
            wait(s, sem["dve"])
            sem["act"].incw(s, s.sqrt(sc["smallg"][0:1, 0:nh_phase], sc["smallg"][0:1, 0:nh_phase]))

        @blk.vector
        def _(v2):
            wait(v2, sem["act"])
            sem["dve"].incw(v2, v2.tensor_scalar(sc["smallg"][0:1, 0:nh_phase],
                           sc["smallg"][0:1, 0:nh_phase], -1.0 / math.sqrt(hd), None, OP.mult))

        @blk.tensor
        def _(t):
            wait(t, sem["dve"], sem["act"])
            sem["pe"].incw(t, t.matmul(sc["ps_c"][:, 0:nh_phase], sc["ones_row_f"][:],
                                   sc["smallg"][0:1, 0:nh_phase], start=True, stop=True))

        @blk.scalar
        def _(s2):
            wait(s2, sem["pe"])
            sem["act"].incw(s2, s2.copy(sc["smallf"][:, 0:nh_phase], sc["ps_c"][:, 0:nh_phase]))

        @blk.vector
        def _(v3):
            wait(v3, sem["act"])
            ins = None
            for i in range(nh_phase):
                ins = v3.tensor_scalar(
                    sc["biasmat"][:, col0 + i * NKC:col0 + (i + 1) * NKC],
                    sc[bcol_key][:, 0:NKC], sc["smallf"][:, i:i + 1], None, OP.add)
            sem["dve"].incw(v3, ins)
    return



def build_attn_phase(ctx, sc, CDST, phase, heads, hd, tag, bv_sb=None,
                     bv_col0=0, causal=True):
    nc = ctx.nc
    sem = ctx.sem
    ps_a, ps_b, ps_c, ps_d = sc["ps_a"], sc["ps_b"], sc["ps_c"], sc["ps_d"]
    NHP = len(heads)
    pe0 = sem["pe"].n
    act0 = sem["act"].n
    dve0 = sem["dve"].n
    kvsem = [sem["kv0"], sem["kv1"]]
    kv0n = [kvsem[0].n, kvsem[1].n]
    escale = 1.0 / math.sqrt(hd)
    AH = 34 + (1 if bv_sb is not None else 0)

    def pe_s(i, kc):
        return pe0 + 33 * i + (1 if kc == 0 else 2 * kc)

    def pe_cd(i, kc):
        return pe0 + 33 * i + (2 * kc + 3 if kc <= 14 else 32)

    with nc.Block(f"at_{tag}{phase}") as blk:
        @blk.gpsimd
        def _(g):
            wait(g, sem["cc"])
            for i, h in enumerate(heads):
                if i >= 2:
                    g.wait_ge(sem["pe"].h, pe0 + 33 * (i - 1))
                kb = sc["kbuf"][i % 2]
                vb = sc["vbuf"][i % 2]
                for r in range(2):
                    g.dma_start(kb[:, 1024 * r:1024 * (r + 1)],
                                CDST[4097 * r + 128 * h:4097 * r + 128 * (h + 1), 0:1024]
                                ).then_inc(kvsem[i % 2].h, 16)
                    vsrc = CDST[4097 * r + 2048:4097 * r + 4096, :]
                    vsrc = vsrc.rearrange("(kc i two) j -> i kc two j", kc=8, i=128, two=2)
                    dhalf = (h * 128) // 1024
                    dcol = (h * 128) % 1024
                    g.dma_start(
                        vb[:, 1024 * r:1024 * (r + 1)].rearrange("i (kc j) -> i kc j", kc=8),
                        vsrc[:, :, dhalf, dcol:dcol + 128]).then_inc(kvsem[i % 2].h, 16)
        for i in range(NHP):
            kvsem[i % 2].n += 64

        @blk.tensor
        def _(t):
            for i, h in enumerate(heads):
                kb = sc["kbuf"][i % 2]
                vb = sc["vbuf"][i % 2]
                t.wait_ge(kvsem[i % 2].h, kv0n[i % 2] + (i // 2 + 1) * 64)
                if i >= 1:
                    t.wait_ge(sem["act"].h, act0 + 17 * i)
                    t.wait_ge(sem["dve"].h, dve0 + AH * i)
                for kc in range(NKC):
                    ps = ps_a if kc % 2 == 0 else ps_b
                    if kc >= 2:
                        t.wait_ge(sem["act"].h, act0 + 17 * i + (kc - 1))
                    mm = mm512(t, ps, kb[:, 128 * kc:128 * (kc + 1)],
                               sc["qbuf"][:, TLOC * i:TLOC * (i + 1)], start=True, stop=True)
                    mm.then_inc(sem["pe"].h, 1)
                    t.wait_ge(sem["pe"].h, pe_s(i, kc))
                    if kc >= 1:
                        kp = kc - 1
                        t.wait_ge(sem["dve"].h, dve0 + AH * i + 2 * (kp + 1))
                        pt = sc["pT"][kp % 2]
                        mm512(t, ps_c, vb[:, 128 * kp:128 * (kp + 1)], pt[:],
                              start=(kp == 0), stop=False)
                        mm = mm512(t, ps_d[0:1, :], sc["ones_col_bf"][:], pt[:],
                                   start=(kp == 0), stop=False)
                        mm.then_inc(sem["pe"].h, 1)
                        t.wait_ge(sem["pe"].h, pe_cd(i, kp))
                kp = NKC - 1
                t.wait_ge(sem["dve"].h, dve0 + AH * i + 2 * (kp + 1))
                pt = sc["pT"][kp % 2]
                mm512(t, ps_c, vb[:, 128 * kp:128 * (kp + 1)], pt[:],
                      start=False, stop=True)
                mm = mm512(t, ps_d[0:1, :], sc["ones_col_bf"][:], pt[:],
                           start=False, stop=True)
                mm.then_inc(sem["pe"].h, 1)
                t.wait_ge(sem["pe"].h, pe_cd(i, 15))
                t.wait_ge(sem["act"].h, act0 + 17 * i + 16)
                t.wait_ge(sem["dve"].h, dve0 + AH * i + 33)
                mm = mm512(t, ps_a, sc["ones_row_f"][:], sc["rcp"][0:1, :],
                           start=True, stop=True)
                mm.then_inc(sem["pe"].h, 1)
                t.wait_ge(sem["pe"].h, pe0 + 33 * i + 33)
        sem["pe"].n = pe0 + NHP * 33

        @blk.scalar
        def _(s):
            for i, h in enumerate(heads):
                hb = phase * 8 + i
                for kc in range(NKC):
                    ps = ps_a if kc % 2 == 0 else ps_b
                    s.wait_ge(sem["pe"].h, pe_s(i, kc))
                    s.activation(sc["pT"][kc % 2][:], ps[:], AF.Exp,
                                 scale=escale,
                                 bias=sc["biasmat"][:, hb * NKC + kc:hb * NKC + kc + 1]
                                 ).then_inc(sem["act"].h, 1)
                    s.wait_ge(sem["act"].h, act0 + 17 * i + kc + 1)
                s.wait_ge(sem["pe"].h, pe_cd(i, 15))
                s.copy(sc["tD"][:], ps_c[:]).then_inc(sem["act"].h, 1)
                s.wait_ge(sem["act"].h, act0 + 17 * i + 17)
        sem["act"].n = act0 + NHP * 17

        @blk.vector
        def _(v):
            for i, h in enumerate(heads):
                for kc in range(NKC):
                    v.wait_ge(sem["act"].h, act0 + 17 * i + kc + 1)
                    pt = sc["pT"][kc % 2]
                    if causal and 1 <= kc <= 8:
                        sem["dve"].inc(v.tensor_scalar(pt[:, 0:128 * kc], pt[:, 0:128 * kc],
                                       sc["scr"][:, kc:kc + 1], None, OP.mult))
                        v.wait_ge(sem["dve"].h, sem["dve"].n)
                    elif causal and kc >= 9:
                        w = 128 * (kc - 8)
                        sem["dve"].inc(v.tensor_scalar(pt[:, 0:w], pt[:, 0:w],
                                       sc["scr2"][:, kc:kc + 1], None, OP.mult))
                        v.wait_ge(sem["dve"].h, sem["dve"].n)
                    else:
                        sem["dve"].inc(v.nop())
                        v.wait_ge(sem["dve"].h, sem["dve"].n)
                    off = (kc % 8) * 128
                    if causal:
                        sem["dve"].inc(v.tensor_tensor(pt[:, off:off + 128], pt[:, off:off + 128],
                                       sc["tri_sb"][:, 128 * kc:128 * (kc + 1)], OP.mult))
                    else:
                        sem["dve"].inc(v.nop())
                    v.wait_ge(sem["dve"].h, sem["dve"].n)
                v.wait_ge(sem["pe"].h, pe_cd(i, 15))
                sem["dve"].inc(v.reciprocal(sc["rcp"][0:1, :], ps_d[0:1, :]))
                v.wait_ge(sem["dve"].h, sem["dve"].n)
                v.wait_ge(sem["act"].h, act0 + 17 * i + 17)
                v.wait_ge(sem["pe"].h, pe0 + 33 * i + 33)
                sem["dve"].inc(v.tensor_tensor(sc["ctxh"][:, TLOC * i:TLOC * (i + 1)],
                               sc["tD"][:], ps_a[:], OP.mult))
                v.wait_ge(sem["dve"].h, sem["dve"].n)
                if bv_sb is not None:
                    if i == 0:
                        wait(v, sem["msd"])
                    sem["dve"].inc(v.tensor_scalar(
                        sc["ctxh"][:, TLOC * i:TLOC * (i + 1)],
                        sc["ctxh"][:, TLOC * i:TLOC * (i + 1)],
                        bv_sb[:, bv_col0 + phase * 8 + i:bv_col0 + phase * 8 + i + 1],
                        None, OP.add))
                    v.wait_ge(sem["dve"].h, sem["dve"].n)
        assert sem["dve"].n == dve0 + NHP * AH
    return



def build_tokmaj_mm(ctx, sc, lhs_src, nchunk_f, rhs_load, n_oc, oc_w, evac_fn,
                    tag, evac_eng="scalar"):
    nc = ctx.nc
    sem = ctx.sem
    ps_a, ps_b = sc["ps_a"], sc["ps_b"]
    pe0 = sem["pe"].n
    act0 = sem["act"].n
    wsem = [sem["w0"], sem["w1"]]
    w0n = [wsem[0].n, wsem[1].n]
    assert oc_w <= 256

    with nc.Block(f"tm_{tag}") as blk:
        @blk.sync
        def _(sy):
            for oc in range(n_oc):
                if oc >= 2:
                    sy.wait_ge(sem["pe"].h, pe0 + (oc - 1) * NT)
                for f in range(nchunk_f):
                    rhs_load(sy, oc, f,
                             sc["wbuf"][oc % 2][:, 256 * f:256 * f + oc_w]
                             ).then_inc(wsem[oc % 2].h, 16)

        @blk.scalar
        def _(s):
            for oc in range(n_oc):
                if oc >= 2:
                    s.wait_ge(sem["pe"].h, pe0 + (oc - 1) * NT)
                if evac_eng == "scalar":
                    for tb in range(NT):
                        s.wait_ge(sem["pe"].h, pe0 + oc * NT + tb + 1)
                        ps = ps_a if tb % 2 == 0 else ps_b
                        evac_fn(s, ps[:, 0:oc_w], tb, oc).then_inc(sem["act"].h, 1)
                        s.wait_ge(sem["act"].h, act0 + oc * NT + tb + 1)
        if evac_eng == "scalar":
            sem["act"].n = act0 + n_oc * NT
        for oc in range(n_oc):
            wsem[oc % 2].n += nchunk_f * 16

        @blk.tensor
        def _(t):
            for oc in range(n_oc):
                t.wait_ge(wsem[oc % 2].h, w0n[oc % 2] + (oc // 2 + 1) * nchunk_f * 16)
                for tb in range(NT):
                    ps = ps_a if tb % 2 == 0 else ps_b
                    if oc * NT + tb >= 2:
                        t.wait_ge(sem["act"].h, act0 + oc * NT + tb - 1)
                    mm = None
                    for f in range(nchunk_f):
                        mm = t.matmul(ps[:, 0:oc_w], lhs_src(f, tb),
                                 sc["wbuf"][oc % 2][:, 256 * f:256 * f + oc_w],
                                 start=(f == 0), stop=(f == nchunk_f - 1))
                    mm.then_inc(sem["pe"].h, 1)
                    t.wait_ge(sem["pe"].h, pe0 + oc * NT + tb + 1)
        sem["pe"].n = pe0 + n_oc * NT

        if evac_eng == "vector":
            @blk.vector
            def _(v):
                for oc in range(n_oc):
                    for tb in range(NT):
                        v.wait_ge(sem["pe"].h, pe0 + oc * NT + tb + 1)
                        ps = ps_a if tb % 2 == 0 else ps_b
                        evac_fn(v, ps[:, 0:oc_w], tb, oc).then_inc(sem["act"].h, 1)
                        v.wait_ge(sem["act"].h, act0 + oc * NT + tb + 1)
            sem["act"].n = act0 + n_oc * NT
    return


def build_outproj(ctx, sc, WOBF, phase, tag, nchunk=8):
    def rhs_load(eng, oc, f, dest):
        dglob = phase * nchunk + f
        return eng.dma_start(dest,
                             WOBF[256 * oc:256 * (oc + 1), 128 * dglob:128 * (dglob + 1)],
                             transpose=True)

    def lhs(f, tb):
        return sc["ctxh"][:, TLOC * f + 128 * tb:TLOC * f + 128 * (tb + 1)]

    def evac(v, ps_ap, tb, oc):
        xt = sc["x"][tb]
        return v.tensor_tensor(xt[:, 256 * oc:256 * (oc + 1)],
                               xt[:, 256 * oc:256 * (oc + 1)], ps_ap, OP.add)

    build_tokmaj_mm(ctx, sc, lhs, nchunk, rhs_load, 8, 256, evac,
                    tag + f"op{phase}", evac_eng="vector")
    return



def build_attn_a(ctx, sc, WBF, XH, CSRC_t, CDST_t, tag):
    CSRC = CSRC_t.ap()
    CDST = CDST_t.ap()
    if STAGES < 1:
        return
    build_ln(ctx, sc, tag, to_hbm=XH)
    if STAGES < 2:
        return
    build_loadrope(ctx, sc, ctx.CT, "ropeA_c", "ropeA_s", tag)
    build_bounce(ctx, sc, XH, tag)
    if STAGES < 3:
        return

    def kdst(i_lin, h):
        return sc["pT"][i_lin % 2][:]

    def kdma(g, i_lin, h):
        g.dma_start(CSRC[128 * h:128 * (h + 1), 0:1024],
                    sc["pT"][i_lin % 2][:]).then_inc(ctx.sem["csrc"].h, 16)

    build_headproj(ctx, sc, WBF["k"], list(range(16)), kdst, 0,
                   sc["ropeC_t"][:], sc["ropeS_t"][:], tag + "K",
                   csrc_dma=kdma, war_gap=1)
    if STAGES < 4:
        return
    build_vstage(ctx, sc, WBF["v"], CSRC, tag)
    if STAGES < 5:
        return
    build_gather(ctx, sc, CSRC_t, CDST_t, 16, tag)
    if STAGES < 6:
        return

    for phase in range(2):
        heads = list(range(8 * phase, 8 * phase + 8))

        def qdst(i_lin, h):
            return sc["qbuf"][:, TLOC * i_lin:TLOC * (i_lin + 1)]

        build_headproj(ctx, sc, WBF["q"], heads, qdst, 32 + 8 * phase,
                       sc["ropeC_t"][:], sc["ropeS_t"][:], f"{tag}Q{phase}")
        if STAGES < 7:
            return
        build_bias(ctx, sc, CDST, phase, 8, 128, tag)
        if STAGES < 8:
            return
        build_attn_phase(ctx, sc, CDST, phase, heads, 128, tag)
        if STAGES < 9:
            return
        build_outproj(ctx, sc, WBF["o"], phase, tag)
        if STAGES < 10:
            return
    return



def build_store(ctx, sc, XOUT):
    nc = ctx.nc
    sem = ctx.sem
    with nc.Block("store") as blk:
        @blk.sync
        def _(sy):
            wait(sy, sem["dve"], sem["act"], sem["pe"], sem["cc"],
                 sem["xh"], sem["xt"], sem["csrc"], sem["msc"], sem["msb"], sem["msd"],
                 sem["vr0"], sem["vr1"], sem["vr2"], sem["vr3"],
                 sem["w0"], sem["w1"], sem["kv0"], sem["kv1"], sem["p0"],
                 sem["q0"], sem["q1"])
            for i in range(NT):
                sem["out"].inc(sy.dma_start(XOUT[128 * i:128 * (i + 1), :], sc["x"][i][:]), 16)
            wait(sy, sem["out"])



SUBLAYERS = [
    ("attn_a", "L0A"),
    ("ffn_l0", "L0F"),
    ("gqa", "L0B"),
    ("ffn_l0", "L0F"),
    ("attn_a", "L1A"),
    ("moe", "L1F"),
    ("attn_b1", "L1B"),
    ("moe", "L1F"),
]

WEIGHT_INPUTS = {
    "L0A": [("q", "L0_attn_a_Wq", [2048, 2048]), ("k", "L0_attn_a_Wk", [2048, 2048]),
            ("v", "L0_attn_a_Wv", [2048, 2048]), ("o", "L0_attn_a_Wo", [2048, 2048])],
    "L1A": [("q", "L1_attn_a_Wq", [2048, 2048]), ("k", "L1_attn_a_Wk", [2048, 2048]),
            ("v", "L1_attn_a_Wv", [2048, 2048]), ("o", "L1_attn_a_Wo", [2048, 2048])],
    "L0B": [("q", "L0_attn_b_wq", [2048, 2048]), ("k", "L0_attn_b_wk", [512, 2048]),
            ("v", "L0_attn_b_wv", [512, 2048]), ("o", "L0_attn_b_wo", [2048, 2048])],
    "L0F": [("gu", "L0_ffn_w_gate_up", [16384, 2048]), ("pr", "L0_ffn_w_proj", [2048, 8192])],
    "L1F": [("gate", "L1_ffn_gate_w", [2048, 8]), ("w1", "L1_ffn_w1", [8 * 2048, 1024]),
            ("b1", "L1_ffn_b1", [8, 1024]), ("w2", "L1_ffn_w2", [8 * 1024, 2048]),
            ("b2", "L1_ffn_b2", [8, 2048]),
            ("sw1", "L1_ffn_shared_w1", [2048, 2048]), ("sb1", "L1_ffn_shared_b1", [1, 2048]),
            ("sw2", "L1_ffn_shared_w2", [2048, 2048]), ("sb2", "L1_ffn_shared_b2", [1, 2048])],
    "L1B": [("wq", "L1_attn_b_Wq", [2048, 2048]), ("bq", "L1_attn_b_bq", [1, 2048]),
            ("wkv", "L1_attn_b_Wkv", [2048, 4096]), ("bkv", "L1_attn_b_bkv", [1, 4096]),
            ("lnqg", "L1_attn_b_ln_q_g", [1, 2048]), ("lnqb", "L1_attn_b_ln_q_b", [1, 2048]),
            ("lnkg", "L1_attn_b_ln_kv_g", [1, 2048]), ("lnkb", "L1_attn_b_ln_kv_b", [1, 2048]),
            ("lnhg", "L1_attn_b_ln_h_g", [1, 128]), ("lnhb", "L1_attn_b_ln_h_b", [1, 128]),
            ("wo", "L1_attn_b_Wo", [2048, 2048]), ("bo", "L1_attn_b_bo", [1, 2048])],
}


def build_program(sub_idx_list):
    nc = bass.Bass("TRN2", target_bir_lowering=False, debug=False)
    stack = contextlib.ExitStack()
    ctx = Ctx(nc, stack)

    XIN = nc.dram_tensor("XIN", [TLOC, D], F32, kind="ExternalInput").ap()
    XOUT = nc.dram_tensor("XOUT", [TLOC, D], F32, kind="ExternalOutput").ap()
    CT = {}
    for name, shape, dt in CONST_SPECS:
        CT[name] = nc.dram_tensor("c_" + name, shape, dt, kind="ExternalInput").ap()

    kinds = [SUBLAYERS[i] for i in sub_idx_list]
    wnames = sorted({wkey for _, wkey in kinds})
    weights = {}
    input_names = ["XIN"] + ["c_" + n for n, _, _ in CONST_SPECS]
    for wkey in wnames:
        weights[wkey] = {}
        for short, iname, shape in WEIGHT_INPUTS[wkey]:
            weights[wkey][short] = nc.dram_tensor(iname, shape, F32, kind="ExternalInput").ap()
            input_names.append(iname)

    XH = nc.dram_tensor("XH", [TLOC, D], BF16, kind="Internal").ap()

    wbf = {}
    for wkey in wnames:
        if wkey in ("L0A", "L1A", "L0B"):
            wbf[wkey] = {}
            for short, iname, shape in WEIGHT_INPUTS[wkey]:
                wbf[wkey][short] = nc.dram_tensor(
                    f"{wkey}_{short}_bf", shape, BF16, kind="Internal").ap()
        elif wkey == "L1F":
            wbf[wkey] = {}
            for short in ["gate", "w1", "w2", "sw1", "sw2", "b2", "sb2"]:
                shape = dict((s, sh) for s, n, sh in WEIGHT_INPUTS["L1F"])[short]
                wbf[wkey][short] = nc.dram_tensor(
                    f"L1F_{short}_bf", shape, BF16, kind="Internal").ap()
        elif wkey == "L1B":
            wbf[wkey] = {}
            for short in ["wq", "wkv", "wo", "bo"]:
                shape = dict((s, sh) for s, n, sh in WEIGHT_INPUTS["L1B"])[short]
                wbf[wkey][short] = nc.dram_tensor(
                    f"L1B_{short}_bf", shape, BF16, kind="Internal").ap()

    cds = {}
    for i in sub_idx_list:
        kind, wkey = SUBLAYERS[i]
        if kind in ("attn_a", "attn_b1"):
            cds[i] = (
                nc.dram_tensor(f"CSRC{i}", [4097, 1024], BF16, kind="Internal"),
                nc.dram_tensor(f"CDST{i}", [8194, 1024], BF16, kind="Internal"),
            )
        elif kind == "gqa":
            cds[i] = (
                nc.dram_tensor(f"CSRC{i}", [1537, 1024], BF16, kind="Internal"),
                nc.dram_tensor(f"CDST{i}", [3074, 1024], BF16, kind="Internal"),
            )
    have_ffn = any(SUBLAYERS[i][0] == "ffn_l0" for i in sub_idx_list)
    if have_ffn:
        WGU_Q = nc.dram_tensor("WGU_Q", [16384, 2048], BF16, kind="Internal").ap()
        WPR_Q = nc.dram_tensor("WPR_Q", [2048, 8192], BF16, kind="Internal").ap()
        HHBM = nc.dram_tensor("HHBM", [8192, TLOC], BF16, kind="Internal").ap()
    have_b1 = any(SUBLAYERS[i][0] == "attn_b1" for i in sub_idx_list)
    if have_b1:
        XQ = nc.dram_tensor("XQ", [TLOC, D], BF16, kind="Internal").ap()
        XK = nc.dram_tensor("XK", [TLOC, D], BF16, kind="Internal").ap()
        MROW = nc.dram_tensor("MROW", [8, 128], BF16, kind="Internal").ap()

    ctx.CT = CT
    with stack:
        sc = alloc_scratch(ctx)
        build_startup(ctx, sc, XIN, CT)
        pairs = []
        for wkey in wnames:
            if wkey in wbf:
                for short in wbf[wkey]:
                    pairs.append((weights[wkey][short][:, :], wbf[wkey][short][:, :]))
        if pairs:
            build_phase0_cast(ctx, pairs, "w")
        if have_ffn:
            build_quant(ctx, sc, weights["L0F"]["gu"], WGU_Q, 16384, 2048, "gu")
            build_quant(ctx, sc, weights["L0F"]["pr"], WPR_Q, 2048, 8192, "pr")

        for i in sub_idx_list:
            kind, wkey = SUBLAYERS[i]
            if kind == "attn_a":
                build_attn_a(ctx, sc, wbf[wkey], XH, cds[i][0], cds[i][1], f"s{i}")
            elif kind == "ffn_l0":
                build_ffn_l0(ctx, sc, WGU_Q, WPR_Q, XH, HHBM, f"s{i}")
            elif kind == "gqa":
                build_attn_gqa(ctx, sc, wbf[wkey], XH, cds[i][0], cds[i][1], f"s{i}")
            elif kind == "moe":
                Wm = dict(wbf["L1F"])
                Wm["b1f"] = weights["L1F"]["b1"]
                Wm["sb1f"] = weights["L1F"]["sb1"]
                build_moe(ctx, sc, Wm, XH, f"s{i}")
            elif kind == "attn_b1":
                build_attn_b1(ctx, sc, wbf["L1B"], weights["L1B"], XH, XQ, XK,
                              MROW, cds[i][0], cds[i][1], f"s{i}")
            else:
                raise NotImplementedError(kind)
        build_store(ctx, sc, XOUT)
    return nc, input_names


def run_sublayers(sub_idx_list, x_full, inputs):
    nc, input_names = build_program(sub_idx_list)
    in_maps = []
    for c in range(NCORE):
        b, h = c // 2, c % 2
        m = {"XIN": np.ascontiguousarray(x_full[b, 1024 * h:1024 * (h + 1), :])}
        consts = host_constants(h)
        for name, shape, dt in CONST_SPECS:
            m["c_" + name] = np.ascontiguousarray(consts[name].reshape(shape))
        for iname in input_names:
            if iname.startswith(("XIN", "c_")):
                continue
            arr = np.asarray(inputs[iname], dtype=np.float32)
            if arr.ndim == 1:
                arr = arr.reshape(1, -1)
            elif arr.ndim == 3:
                arr = arr.reshape(-1, arr.shape[-1])
            m[iname] = np.ascontiguousarray(arr)
        in_maps.append(m)
    res = run_bass_kernel_spmd(nc, in_maps, core_ids=list(range(NCORE)))
    out = np.empty_like(x_full)
    for c in range(NCORE):
        b, h = c // 2, c % 2
        out[b, 1024 * h:1024 * (h + 1), :] = res.results[c]["XOUT"]
    return out


def kernel(**inputs):
    x = np.ascontiguousarray(np.asarray(inputs["L0_attn_a_hidden_states"], dtype=np.float32))
    for group in [[0], [1], [2], [3], [4], [5], [6], [7]]:
        x = run_sublayers(group, x, inputs)
    return x



def build_quant(ctx, sc, WSRC, WQ, R, Cc, tag):
    nc = ctx.nc
    sem = ctx.sem
    nrt = R // 128
    ncc = Cc // 1024
    qs0 = sem["msc"].n
    qsem = [sem["q0"], sem["q1"]]
    qo0 = [qsem[0].n, qsem[1].n]
    act0 = sem["act"].n
    dve0 = sem["dve"].n
    qsc = sc["qsc"]
    qscf = sc["qscf"]
    with nc.Block(f"qz_{tag}") as blk:
        @blk.gpsimd
        def _(g):
            for it in range(nrt * ncc):
                rt, cc = divmod(it, ncc)
                g.wait_ge(qsem[it % 2].h, qo0[it % 2] + (it // 2) * 16)
                g.dma_start(sc["wbuf"][it % 2][:, 0:1024],
                            WSRC[128 * rt:128 * (rt + 1), 1024 * cc:1024 * (cc + 1)]
                            ).then_inc(sem["msc"].h, 16)
                g.wait_ge(sem["dve"].h, dve0 + (it + 1) * 8)
                g.dma_start(WQ[128 * rt:128 * (rt + 1), 1024 * cc:1024 * (cc + 1)],
                            sc["wbuf"][it % 2][:, 1024:2048]
                            ).then_inc(qsem[it % 2].h, 16)
        sem["msc"].n = qs0 + nrt * ncc * 16
        for it in range(nrt * ncc):
            qsem[it % 2].n += 16

        @blk.scalar
        def _(s):
            for it in range(nrt * ncc):
                s.wait_ge(sem["msc"].h, qs0 + (it + 1) * 16)
                s.wait_ge(sem["dve"].h, dve0 + (8 * (it - 1) + 7 if it >= 1 else 0))
                wb = sc["wbuf"][it % 2][:, 0:1024]
                s.activation(sc["tA"][:], wb, AF.Abs)
                s.activation(sc["tB"][:], wb, AF.Sign).then_inc(sem["act"].h, 1)
                s.wait_ge(sem["act"].h, act0 + it + 1)
        sem["act"].n = act0 + nrt * ncc

        @blk.vector
        def _(v):
            for it in range(nrt * ncc):
                v.wait_ge(sem["act"].h, act0 + it + 1)
                v.wait_ge(qsem[it % 2].h, qo0[it % 2] + (it // 2) * 16)

                def vf(instr):
                    sem["dve"].inc(instr)
                    v.wait_ge(sem["dve"].h, sem["dve"].n)

                base = sem["dve"].n
                whq = sc["wbuf"][it % 2][:, 1024:2048]
                vf(v.tensor_reduce(sc["smallf"][:, 0:16],
                                   sc["tA"][:].rearrange("p (g k) -> p g k", k=64),
                                   axis=mybir.AxisListType.X, op=OP.add))
                vf(v.tensor_scalar(qsc[:], sc["smallf"][:, 0:16], 1.0 / 64.0, None, OP.mult))
                vf(v.tensor_scalar(qsc[:], qsc[:], 1e-8, None, OP.max))
                vf(v.tensor_copy(qscf[:, 0:16], qsc[:]))
                vf(v.tensor_scalar(qscf[:, 16:32], qscf[:, 0:16], 0.5, None, OP.mult))
                cp = None
                for gg in range(16):
                    cp = v.tensor_scalar(sc["tC"][:, 64 * gg:64 * (gg + 1)],
                                         sc["tA"][:, 64 * gg:64 * (gg + 1)],
                                         qscf[:, 16 + gg:17 + gg], None, OP.is_gt)
                vf(cp)
                vf(v.tensor_tensor(sc["tC"][:], sc["tC"][:], sc["tB"][:], OP.mult))
                cp = None
                for gg in range(16):
                    cp = v.tensor_scalar(whq[:, 64 * gg:64 * (gg + 1)],
                                         sc["tC"][:, 64 * gg:64 * (gg + 1)],
                                         qscf[:, gg:gg + 1], None, OP.mult)
                assert sem["dve"].n == base + 7
                vf(cp)
                assert sem["dve"].n == base + 8
        assert sem["dve"].n == dve0 + nrt * ncc * 8
    return



def build_ffn_l0(ctx, sc, WGU_Q, WPR_Q, XH, HHBM, tag):
    nc = ctx.nc
    sem = ctx.sem
    ps_a, ps_b, ps_c, ps_d = sc["ps_a"], sc["ps_b"], sc["ps_c"], sc["ps_d"]
    build_ln(ctx, sc, tag, to_hbm=XH)
    build_bounce(ctx, sc, XH, tag)

    wsem = [sem["w0"], sem["w1"]]
    vrsem = [sem["vr0"], sem["vr1"]]
    NPAIR = 64

    pe0 = sem["pe"].n
    act0 = sem["act"].n
    dve0 = sem["dve"].n
    w0n = [wsem[0].n, wsem[1].n]
    vr0n = [vrsem[0].n, vrsem[1].n]
    with nc.Block(f"f1_{tag}") as blk:
        @blk.sync
        def _(sy):
            wait(sy, sem["p0"], sem["q0"], sem["q1"])
            for i in range(NPAIR):
                if i >= 2:
                    sy.wait_ge(sem["pe"].h, pe0 + (i - 1) * 2)
                for f in range(NF):
                    sy.dma_start(
                        sc["wbuf"][i % 2][:, 256 * f:256 * f + 128],
                        WGU_Q[128 * i:128 * (i + 1), 128 * f:128 * (f + 1)],
                        transpose=True).then_inc(wsem[i % 2].h, 16)
                    sy.dma_start(
                        sc["wbuf"][i % 2][:, 256 * f + 128:256 * f + 256],
                        WGU_Q[128 * (64 + i):128 * (65 + i), 128 * f:128 * (f + 1)],
                        transpose=True).then_inc(wsem[i % 2].h, 16)

        @blk.scalar
        def _(s):
            wait(s, sem["p0"], sem["q0"], sem["q1"])
            for i in range(NPAIR):
                s.wait_ge(sem["pe"].h, pe0 + 2 * i + 1)
                if i >= 1:
                    s.wait_ge(sem["dve"].h, dve0 + 2 * (i - 1) + 1)
                s.activation(sc["tA"][:], ps_a[:], AF.Silu)
                s.copy(sc["tB"][:], ps_b[:]).then_inc(sem["act"].h, 1)
                s.wait_ge(sem["act"].h, act0 + i + 1)
        sem["act"].n = act0 + NPAIR
        for i in range(NPAIR):
            wsem[i % 2].n += 32 * 16

        @blk.tensor
        def _(t):
            wait(t, sem["xt"])
            for i in range(NPAIR):
                t.wait_ge(wsem[i % 2].h, w0n[i % 2] + (i // 2 + 1) * 32 * 16)
                if i >= 1:
                    t.wait_ge(sem["act"].h, act0 + i)
                mm = None
                for f in range(NF):
                    mm512(t, ps_a, sc["wbuf"][i % 2][:, 256 * f:256 * f + 128],
                          sc["xhatT"][:, TLOC * f:TLOC * (f + 1)],
                          start=(f == 0), stop=(f == NF - 1))
                    mm = mm512(t, ps_b, sc["wbuf"][i % 2][:, 256 * f + 128:256 * f + 256],
                               sc["xhatT"][:, TLOC * f:TLOC * (f + 1)],
                               start=(f == 0), stop=(f == NF - 1))
                mm.then_inc(sem["pe"].h, 1)
                t.wait_ge(sem["pe"].h, pe0 + 2 * i + 1)
                t.wait_ge(sem["dve"].h, dve0 + 2 * i + 2)
                mm = mm512(t, ps_c[0:1, :], sc["ones_col_bf"][:], sc["tC"][:],
                           start=(i == 0), stop=(i == NPAIR - 1))
                mm.then_inc(sem["pe"].h, 1)
                t.wait_ge(sem["pe"].h, pe0 + 2 * i + 2)
        sem["pe"].n = pe0 + NPAIR * 2

        @blk.vector
        def _(v):
            for i in range(NPAIR):
                v.wait_ge(sem["act"].h, act0 + i + 1)
                u = i // 2
                if u >= 1:
                    v.wait_ge(vrsem[i % 2].h, vr0n[i % 2] + u * 16)
                ht = sc["pT"][i % 2]
                sem["dve"].inc(v.tensor_tensor(ht[:], sc["tA"][:], sc["tB"][:], OP.mult))
                v.wait_ge(sem["dve"].h, sem["dve"].n)
                sem["dve"].inc(v.tensor_tensor(sc["tC"][:], ht[:], ht[:], OP.mult))
                v.wait_ge(sem["dve"].h, sem["dve"].n)
        assert sem["dve"].n == dve0 + NPAIR * 2

        @blk.gpsimd
        def _(g):
            for i in range(NPAIR):
                g.wait_ge(sem["dve"].h, dve0 + 2 * i + 1)
                g.dma_start(HHBM[128 * i:128 * (i + 1), :],
                            sc["pT"][i % 2][:]).then_inc(vrsem[i % 2].h, 16)
        for i in range(NPAIR):
            vrsem[i % 2].n += 16

    EPS32 = float(np.finfo(np.float32).eps)
    with nc.Block(f"f2_{tag}") as blk:
        @blk.vector
        def _(v):
            wait(v, sem["pe"], sem["dve"])
            sem["dve"].incw(v, v.tensor_scalar(sc["rcp"][0:1, :], ps_c[0:1, :],
                            1.0 / 8192.0, EPS32, OP.mult, OP.add))

        @blk.tensor
        def _(t):
            wait(t, sem["dve"])
            mm = None
            pslist = [ps_a, ps_b, ps_c, ps_d]
            for tb in range(NT):
                mm = t.matmul(pslist[tb // 2][:, 512 * (tb % 2):512 * (tb % 2) + 1],
                              sc["rcp"][0:1, 128 * tb:128 * (tb + 1)],
                              sc["ones_row_f"][0:1, 0:1], start=True, stop=True)
            sem["pe"].incw(t, mm)

        @blk.scalar
        def _(s):
            wait(s, sem["pe"])
            pslist = [ps_a, ps_b, ps_c, ps_d]
            ins = None
            for pi in range(4):
                ins = s.sqrt(sc["smallf"][:, 32 + 2 * pi:34 + 2 * pi],
                             pslist[pi][:].rearrange("p (b c) -> p b c", b=2)[:, :, 0])
            sem["act"].incw(s, ins)

        @blk.vector
        def _(v2):
            wait(v2, sem["act"])
            sem["dve"].incw(v2, v2.reciprocal(sc["smallf"][:, 40:48], sc["smallf"][:, 32:40]))

    pe0 = sem["pe"].n
    act0 = sem["act"].n
    dve0 = sem["dve"].n
    w0n = [wsem[0].n, wsem[1].n]
    vr0n = [vrsem[0].n, vrsem[1].n]
    NH2 = 64

    def psreg(tb):
        base = [ps_a, ps_b, ps_c, ps_d][tb // 2]
        return base[:, 512 * (tb % 2):512 * (tb % 2 + 1)]

    with nc.Block(f"f3_{tag}") as blk:
        @blk.gpsimd
        def _(g):
            wait(g, vrsem[0], vrsem[1])
            for ocg in range(4):
                for hc in range(NH2):
                    it = ocg * NH2 + hc
                    u = it // 2
                    if u >= 1:
                        g.wait_ge(sem["pe"].h, pe0 + (it - 1))
                    g.dma_start(sc["pT"][it % 2][:], HHBM[128 * hc:128 * (hc + 1), :]
                                ).then_inc(vrsem[it % 2].h, 16)
        for it in range(4 * NH2):
            vrsem[it % 2].n += 16

        @blk.sync
        def _(sy):
            for ocg in range(4):
                for hc in range(NH2):
                    it = ocg * NH2 + hc
                    if it >= 2:
                        sy.wait_ge(sem["pe"].h, pe0 + (it - 1))
                    sy.dma_start(sc["wbuf"][it % 2][:, 0:256],
                                 WPR_Q[512 * ocg:512 * ocg + 256,
                                       128 * hc:128 * (hc + 1)],
                                 transpose=True).then_inc(wsem[it % 2].h, 16)
                    sy.dma_start(sc["wbuf"][it % 2][:, 256:512],
                                 WPR_Q[512 * ocg + 256:512 * (ocg + 1),
                                       128 * hc:128 * (hc + 1)],
                                 transpose=True).then_inc(wsem[it % 2].h, 16)
        for it in range(4 * NH2):
            wsem[it % 2].n += 32

        @blk.tensor
        def _(t):
            for ocg in range(4):
                for hc in range(NH2):
                    it = ocg * NH2 + hc
                    t.wait_ge(wsem[it % 2].h, w0n[it % 2] + (it // 2 + 1) * 32)
                    t.wait_ge(vrsem[it % 2].h, vr0n[it % 2] + (it // 2 + 1) * 16)
                    if hc == 0 and ocg >= 1:
                        t.wait_ge(sem["dve"].h, dve0 + ocg * 8)
                    mm = None
                    for tb in range(NT):
                        mm = t.matmul(psreg(tb),
                                      sc["pT"][it % 2][:, 128 * tb:128 * (tb + 1)],
                                      sc["wbuf"][it % 2][:, 0:512],
                                      start=(hc == 0), stop=(hc == NH2 - 1))
                    mm.then_inc(sem["pe"].h, 1)
                    t.wait_ge(sem["pe"].h, pe0 + it + 1)
        sem["pe"].n = pe0 + 4 * NH2

        @blk.vector
        def _(v):
            for ocg in range(4):
                v.wait_ge(sem["pe"].h, pe0 + (ocg + 1) * NH2)
                for tb in range(NT):
                    xt = sc["x"][tb]
                    sem["dve"].inc(v.scalar_tensor_tensor(
                        xt[:, 512 * ocg:512 * (ocg + 1)], psreg(tb),
                        sc["smallf"][:, 40 + tb:41 + tb],
                        xt[:, 512 * ocg:512 * (ocg + 1)], OP.mult, OP.add))
                    v.wait_ge(sem["dve"].h, sem["dve"].n)
        assert sem["dve"].n == dve0 + 4 * 8
    return



def build_attn_phase_gqa(ctx, sc, CDST, phase, tag, rowsz):
    nc = ctx.nc
    sem = ctx.sem
    ps_a, ps_b, ps_c, ps_d = sc["ps_a"], sc["ps_b"], sc["ps_c"], sc["ps_d"]
    NHP = 8
    AH = 34
    pe0 = sem["pe"].n
    act0 = sem["act"].n
    dve0 = sem["dve"].n
    kvsem = [sem["kv0"], sem["kv1"]]
    kv0n = kvsem[phase % 2].n
    escale = 1.0 / math.sqrt(64.0)

    def pe_s(i, kc):
        return pe0 + 33 * i + (1 if kc == 0 else 2 * kc)

    def pe_cd(i, kc):
        return pe0 + 33 * i + (2 * kc + 3 if kc <= 14 else 32)

    kb = sc["kbuf"][phase % 2]
    vb = sc["vbuf"][phase % 2]

    with nc.Block(f"ag_{tag}{phase}") as blk:
        @blk.gpsimd
        def _(g):
            wait(g, sem["cc"])
            if phase >= 2:
                g.wait_ge(sem["pe"].h, pe0)
            for r in range(2):
                g.dma_start(kb[:, 1024 * r:1024 * (r + 1)],
                            CDST[rowsz * r + 128 * phase:rowsz * r + 128 * (phase + 1), 0:1024]
                            ).then_inc(kvsem[phase % 2].h, 16)
                g.dma_start(
                    vb[:, 1024 * r:1024 * (r + 1)].rearrange("i (kc j) -> i kc j", kc=8),
                    CDST[rowsz * r + 512:rowsz * r + 1536, 128 * phase:128 * (phase + 1)]
                    .rearrange("(kc i) j -> i kc j", kc=8)).then_inc(kvsem[phase % 2].h, 16)
        kvsem[phase % 2].n += 64

        @blk.tensor
        def _(t):
            t.wait_ge(kvsem[phase % 2].h, kv0n + 64)
            for i in range(NHP):
                kvh = (i // 4) * 64
                if i >= 1:
                    t.wait_ge(sem["act"].h, act0 + 18 * i)
                    t.wait_ge(sem["dve"].h, dve0 + AH * i)
                for kc in range(NKC):
                    ps = ps_a if kc % 2 == 0 else ps_b
                    if kc >= 2:
                        t.wait_ge(sem["act"].h, act0 + 18 * i + (kc - 1))
                    mm = mm512(t, ps, kb[:, 128 * kc:128 * (kc + 1)],
                               sc["qbuf"][:, TLOC * i:TLOC * (i + 1)],
                               start=True, stop=True)
                    mm.then_inc(sem["pe"].h, 1)
                    t.wait_ge(sem["pe"].h, pe_s(i, kc))
                    if kc >= 1:
                        kp = kc - 1
                        t.wait_ge(sem["dve"].h, dve0 + AH * i + 2 * (kp + 1))
                        pt = sc["pT"][kp % 2]
                        mm512(t, ps_c[0:64, :], vb[:, 128 * kp + kvh:128 * kp + kvh + 64],
                              pt[:], start=(kp == 0), stop=False)
                        mm = mm512(t, ps_d[0:1, :], sc["ones_col_bf"][:], pt[:],
                                   start=(kp == 0), stop=False)
                        mm.then_inc(sem["pe"].h, 1)
                        t.wait_ge(sem["pe"].h, pe_cd(i, kp))
                kp = NKC - 1
                t.wait_ge(sem["dve"].h, dve0 + AH * i + 2 * (kp + 1))
                pt = sc["pT"][kp % 2]
                mm512(t, ps_c[0:64, :], vb[:, 128 * kp + kvh:128 * kp + kvh + 64],
                      pt[:], start=False, stop=True)
                mm = mm512(t, ps_d[0:1, :], sc["ones_col_bf"][:], pt[:],
                           start=False, stop=True)
                mm.then_inc(sem["pe"].h, 1)
                t.wait_ge(sem["pe"].h, pe_cd(i, 15))
                t.wait_ge(sem["act"].h, act0 + 18 * i + 16)
                t.wait_ge(sem["dve"].h, dve0 + AH * i + 33)
                mm = mm512(t, ps_a[0:64, :], sc["ones_row_f"][0:1, 0:64],
                           sc["rcp"][0:1, :], start=True, stop=True)
                mm.then_inc(sem["pe"].h, 1)
                t.wait_ge(sem["pe"].h, pe0 + 33 * i + 33)
        sem["pe"].n = pe0 + NHP * 33

        @blk.scalar
        def _(s):
            for i in range(NHP):
                hb = i
                for kc in range(NKC):
                    ps = ps_a if kc % 2 == 0 else ps_b
                    s.wait_ge(sem["pe"].h, pe_s(i, kc))
                    s.activation(sc["pT"][kc % 2][:], ps[:], AF.Exp,
                                 scale=escale,
                                 bias=sc["biasmat"][:, hb * NKC + kc:hb * NKC + kc + 1]
                                 ).then_inc(sem["act"].h, 1)
                    s.wait_ge(sem["act"].h, act0 + 18 * i + kc + 1)
                s.wait_ge(sem["pe"].h, pe_cd(i, 15))
                s.copy(sc["tD"][0:64, :], ps_c[0:64, :]).then_inc(sem["act"].h, 1)
                s.wait_ge(sem["act"].h, act0 + 18 * i + 17)
                s.wait_ge(sem["dve"].h, dve0 + AH * i + 34)
                qt, qr = i // 2, (i % 2) * 64
                s.copy(sc["ctxh"][qr:qr + 64, TLOC * qt:TLOC * (qt + 1)],
                       sc["tC"][0:64, :]).then_inc(sem["act"].h, 1)
                s.wait_ge(sem["act"].h, act0 + 18 * i + 18)
        sem["act"].n = act0 + NHP * 18

        @blk.vector
        def _(v):
            for i in range(NHP):
                for kc in range(NKC):
                    v.wait_ge(sem["act"].h, act0 + 18 * i + kc + 1)
                    pt = sc["pT"][kc % 2]
                    if 1 <= kc <= 8:
                        sem["dve"].inc(v.tensor_scalar(pt[:, 0:128 * kc], pt[:, 0:128 * kc],
                                       sc["scr"][:, kc:kc + 1], None, OP.mult))
                        v.wait_ge(sem["dve"].h, sem["dve"].n)
                    elif kc >= 9:
                        w = 128 * (kc - 8)
                        sem["dve"].inc(v.tensor_scalar(pt[:, 0:w], pt[:, 0:w],
                                       sc["scr2"][:, kc:kc + 1], None, OP.mult))
                        v.wait_ge(sem["dve"].h, sem["dve"].n)
                    else:
                        sem["dve"].inc(v.nop())
                        v.wait_ge(sem["dve"].h, sem["dve"].n)
                    off = (kc % 8) * 128
                    sem["dve"].inc(v.tensor_tensor(pt[:, off:off + 128], pt[:, off:off + 128],
                                   sc["tri_sb"][:, 128 * kc:128 * (kc + 1)], OP.mult))
                    v.wait_ge(sem["dve"].h, sem["dve"].n)
                v.wait_ge(sem["pe"].h, pe_cd(i, 15))
                sem["dve"].inc(v.reciprocal(sc["rcp"][0:1, :], ps_d[0:1, :]))
                v.wait_ge(sem["dve"].h, sem["dve"].n)
                v.wait_ge(sem["act"].h, act0 + 18 * i + 17)
                v.wait_ge(sem["pe"].h, pe0 + 33 * i + 33)
                sem["dve"].inc(v.tensor_tensor(sc["tC"][0:64, :], sc["tD"][0:64, :],
                               ps_a[0:64, :], OP.mult))
                v.wait_ge(sem["dve"].h, sem["dve"].n)
        assert sem["dve"].n == dve0 + NHP * AH
    return



def build_attn_gqa(ctx, sc, WBF, XH, CSRC_t, CDST_t, tag):
    nc = ctx.nc
    sem = ctx.sem
    ROWSZ = 1537
    CSRC = CSRC_t.ap()
    CDST = CDST_t.ap()
    build_ln(ctx, sc, tag, to_hbm=XH)
    build_loadrope(ctx, sc, ctx.CT, "ropeG_c", "ropeG_s", tag)
    build_bounce(ctx, sc, XH, tag)

    def kdst(i_lin, h):
        return sc["pT"][i_lin % 2][:]

    def kdma(g, i_lin, h):
        g.dma_start(CSRC[128 * h:128 * (h + 1), 0:1024],
                    sc["pT"][i_lin % 2][:]).then_inc(ctx.sem["csrc"].h, 16)

    build_headproj(ctx, sc, WBF["k"], list(range(4)), kdst, 0,
                   sc["ropeC_t"][:], sc["ropeS_t"][:], tag + "K",
                   csrc_dma=kdma, war_gap=1, rope_mode="xor1")
    build_vstage(ctx, sc, WBF["v"], CSRC, tag, ndim=512, csrc_row0=512)
    build_gather(ctx, sc, CSRC_t, CDST_t, 4, tag, norm_row=1536)

    for phase in range(4):
        heads = list(range(4 * phase, 4 * phase + 4))

        def qdst(i_lin, h):
            return sc["pT"][i_lin % 2][:]

        def qpad(i_lin):
            kvh = 0 if i_lin < 2 else 64
            qa = sc["qbuf"][:, TLOC * (2 * i_lin):TLOC * (2 * i_lin + 1)]
            qb = sc["qbuf"][:, TLOC * (2 * i_lin + 1):TLOC * (2 * i_lin + 2)]
            return (qa, kvh), (qb, kvh)

        build_headproj(ctx, sc, WBF["q"], heads, qdst, 32 + 4 * phase,
                       sc["ropeC_t"][:], sc["ropeS_t"][:], f"{tag}Q{phase}",
                       rope_mode="xor1", qpad=qpad)
        build_bias(ctx, sc, CDST, phase, 8, 64, tag,
                   kn_of_head=[phase] * 8,
                   qn_of_head=[32 + 4 * phase + i // 2 for i in range(8)],
                   nrow0=1536, nrow1=1537 + 1536, col0=0)
        build_attn_phase_gqa(ctx, sc, CDST, phase, tag, 1537)
        build_outproj(ctx, sc, WBF["o"], phase, tag, nchunk=4)
    return



def build_moe(ctx, sc, W, XH, tag):
    nc = ctx.nc
    sem = ctx.sem
    ps_a, ps_b, ps_c, ps_d = sc["ps_a"], sc["ps_b"], sc["ps_c"], sc["ps_d"]
    build_ln(ctx, sc, tag, to_hbm=XH)
    build_bounce(ctx, sc, XH, tag)
    wsem = [sem["w0"], sem["w1"]]
    pslist = [ps_a, ps_b, ps_c, ps_d]

    def psb(tb, width):
        return pslist[tb // 2][:, 512 * (tb % 2):512 * (tb % 2) + width]

    pe0 = sem["pe"].n
    dve0 = sem["dve"].n
    act0 = sem["act"].n
    msc0 = sem["msc"].n
    gts, gtsb, gtmp, gtT = sc["gts"], sc["gtsb"], sc["gtmp"], sc["gtT"]
    with nc.Block(f"mg_{tag}") as blk:
        @blk.sync
        def _(sy):
            wait(sy, sem["q0"], sem["q1"], sem["p0"])
            sem["msb"].inc(sy.dma_start(
                sc["wbuf"][0][:, 0:128].rearrange("p (c e) -> p c e", e=8),
                W["gate"].rearrange("(c p) e -> p c e", p=128)), 16)

        @blk.tensor
        def _(t):
            wait(t, sem["msb"], sem["xt"], sem["act"], sem["dve"])
            for tb in range(NT):
                mm = None
                for f in range(NF):
                    mm = t.matmul(psb(tb, 8),
                                  sc["xhatT"][:, TLOC * f + 128 * tb:TLOC * f + 128 * (tb + 1)],
                                  sc["wbuf"][0][:, 8 * f:8 * f + 8],
                                  start=(f == 0), stop=(f == NF - 1))
                mm.then_inc(sem["pe"].h, 1)
                t.wait_ge(sem["pe"].h, pe0 + tb + 1)
            for tb in range(NT):
                t.wait_ge(sem["dve"].h, dve0 + 12 * (tb + 1))
                mm = t.matmul(psb(tb, 128)[0:8, :], gtsb[:, 8 * tb:8 * tb + 8],
                              sc["id128_bf"][:], start=True, stop=True)
                mm.then_inc(sem["pe"].h, 1)
                t.wait_ge(sem["pe"].h, pe0 + NT + tb + 1)
        sem["pe"].n = pe0 + 2 * NT

        @blk.vector
        def _(v):
            def vf(instr):
                sem["dve"].inc(instr)
                v.wait_ge(sem["dve"].h, sem["dve"].n)
            for tb in range(NT):
                v.wait_ge(sem["pe"].h, pe0 + tb + 1)
                gl = gts[:, 8 * tb:8 * tb + 8]
                sm = sc["smallf"]
                vf(v.tensor_copy(gl, psb(tb, 8)))
                vf(v.tensor_reduce(sm[:, 0:1], gl, axis=mybir.AxisListType.X, op=OP.max))
                vf(v.tensor_scalar(gtmp[:, 0:8], gl, sm[:, 0:1], None, OP.is_ge))
                vf(v.scalar_tensor_tensor(gtmp[:, 8:16], gtmp[:, 0:8], -1e30, gl,
                                          OP.mult, OP.add))
                vf(v.tensor_reduce(sm[:, 1:2], gtmp[:, 8:16], axis=mybir.AxisListType.X, op=OP.max))
                vf(v.tensor_tensor(sm[:, 2:3], sm[:, 0:1], sm[:, 1:2], OP.subtract))
                v.wait_ge(sem["act"].h, act0 + tb + 1)
                vf(v.tensor_scalar(sm[:, 4:5], sm[:, 3:4], -1.0, 1.0, OP.mult, OP.add))
                vf(v.tensor_scalar(gtmp[:, 16:24], gtmp[:, 8:16], sm[:, 1:2], None, OP.is_ge))
                vf(v.tensor_scalar(gtmp[:, 0:8], gtmp[:, 0:8], sm[:, 3:4], None, OP.mult))
                vf(v.tensor_scalar(gtmp[:, 16:24], gtmp[:, 16:24], sm[:, 4:5], None, OP.mult))
                vf(v.tensor_tensor(gl, gtmp[:, 0:8], gtmp[:, 16:24], OP.add))
                vf(v.tensor_copy(gtsb[:, 8 * tb:8 * tb + 8], gl))
        assert sem["dve"].n == dve0 + 12 * NT

        @blk.scalar
        def _(s):
            for tb in range(NT):
                s.wait_ge(sem["dve"].h, dve0 + 12 * tb + 6)
                s.activation(sc["smallf"][:, 3:4], sc["smallf"][:, 2:3],
                             AF.Sigmoid).then_inc(sem["act"].h, 1)
                s.wait_ge(sem["act"].h, act0 + tb + 1)
            for tb in range(NT):
                s.wait_ge(sem["pe"].h, pe0 + NT + tb + 1)
                s.copy(gtT[:, 128 * tb:128 * (tb + 1)],
                       psb(tb, 128)[0:8, :]).then_inc(sem["act"].h, 1)
                s.wait_ge(sem["act"].h, act0 + NT + tb + 1)
        sem["act"].n = act0 + 2 * NT

    def hstage(src_w, n_tiles, row0, bias_sb, bias_col0, dst_fn, stag):
        pe0 = sem["pe"].n
        act0 = sem["act"].n
        w0n = [wsem[0].n, wsem[1].n]
        with nc.Block(f"mh_{stag}") as blk:
            @blk.sync
            def _(sy):
                wait(sy, sem["q0"], sem["q1"])
                for i in range(n_tiles):
                    if i >= 2:
                        sy.wait_ge(sem["pe"].h, pe0 + i - 1)
                    for f in range(NF):
                        if f % 2 == 0:
                            sy.dma_start(
                                sc["wbuf"][i % 2][:, 128 * f:128 * (f + 1)],
                                src_w[128 * f:128 * (f + 1), row0 + 128 * i:row0 + 128 * (i + 1)]
                                ).then_inc(wsem[i % 2].h, 16)

            @blk.scalar
            def _(s):
                for i in range(n_tiles):
                    if i >= 2:
                        s.wait_ge(sem["pe"].h, pe0 + i - 1)
                    for f in range(NF):
                        if f % 2 == 1:
                            s.dma_start(
                                sc["wbuf"][i % 2][:, 128 * f:128 * (f + 1)],
                                src_w[128 * f:128 * (f + 1), row0 + 128 * i:row0 + 128 * (i + 1)]
                                ).then_inc(wsem[i % 2].h, 16)
                    s.wait_ge(sem["pe"].h, pe0 + i + 1)
                    ps = ps_a if i % 2 == 0 else ps_b
                    s.activation(dst_fn(i), ps[:], AF.Gelu,
                                 bias=bias_sb[:, bias_col0 + i:bias_col0 + i + 1]
                                 ).then_inc(sem["act"].h, 1)
                    s.wait_ge(sem["act"].h, act0 + i + 1)
            for i in range(n_tiles):
                wsem[i % 2].n += 16 * 16

            @blk.tensor
            def _(t):
                for i in range(n_tiles):
                    t.wait_ge(wsem[i % 2].h, w0n[i % 2] + (i // 2 + 1) * 16 * 16)
                    if i >= 2:
                        t.wait_ge(sem["act"].h, act0 + i - 1)
                    ps = ps_a if i % 2 == 0 else ps_b
                    mm = None
                    for f in range(NF):
                        mm = mm512(t, ps, sc["wbuf"][i % 2][:, 128 * f:128 * (f + 1)],
                                   sc["xhatT"][:, TLOC * f:TLOC * (f + 1)],
                                   start=(f == 0), stop=(f == NF - 1))
                    mm.then_inc(sem["pe"].h, 1)
                    t.wait_ge(sem["pe"].h, pe0 + i + 1)
            sem["pe"].n = pe0 + n_tiles
        sem["act"].n = act0 + n_tiles

    def ostage(src_w, nhc, wrow0, hsrc_fn, brow_src, evac_kind, e_idx, stag):
        pe0 = sem["pe"].n
        dve0 = sem["dve"].n
        msc0 = sem["msc"].n
        w0n = [wsem[0].n, wsem[1].n]
        per = nhc + (1 if brow_src is not None else 0)

        def mm_done(it):
            o2, h2 = divmod(it, nhc)
            return pe0 + o2 * per + h2 + 1

        with nc.Block(f"mo_{stag}") as blk:
            @blk.sync
            def _(sy):
                wait(sy, sem["q0"], sem["q1"])
                for ocg in range(4):
                    for hc in range(nhc):
                        it = ocg * nhc + hc
                        if it >= 2:
                            sy.wait_ge(sem["pe"].h, mm_done(it - 2))
                        sy.dma_start(sc["wbuf"][it % 2][:, 0:256],
                                     src_w[wrow0 + 128 * hc:wrow0 + 128 * (hc + 1),
                                           512 * ocg:512 * ocg + 256]
                                     ).then_inc(wsem[it % 2].h, 16)

            @blk.scalar
            def _(s):
                for ocg in range(4):
                    for hc in range(nhc):
                        it = ocg * nhc + hc
                        if it >= 2:
                            s.wait_ge(sem["pe"].h, mm_done(it - 2))
                        s.dma_start(sc["wbuf"][it % 2][:, 256:512],
                                    src_w[wrow0 + 128 * hc:wrow0 + 128 * (hc + 1),
                                          512 * ocg + 256:512 * (ocg + 1)]
                                    ).then_inc(wsem[it % 2].h, 16)
            for it in range(4 * nhc):
                wsem[it % 2].n += 32

            @blk.gpsimd
            def _(g):
                if brow_src is not None:
                    wait(g, sem["pe"])
                    for ocg in range(4):
                        if ocg >= 1:
                            g.wait_ge(sem["pe"].h, pe0 + ocg * per)
                        g.dma_start(sc["b2row"][0:1, 0:512],
                                    brow_src[0:1, 512 * ocg:512 * (ocg + 1)]
                                    ).then_inc(sem["msc"].h, 16)
            if brow_src is not None:
                sem["msc"].n += 4 * 16

            @blk.tensor
            def _(t):
                for ocg in range(4):
                    for hc in range(nhc):
                        it = ocg * nhc + hc
                        t.wait_ge(wsem[it % 2].h, w0n[it % 2] + (it // 2 + 1) * 32)
                        if hc == 0 and ocg >= 1:
                            t.wait_ge(sem["dve"].h, dve0 + ocg * 8)
                        mm = None
                        for tb in range(NT):
                            mm = t.matmul(psb(tb, 512), hsrc_fn(hc, tb),
                                          sc["wbuf"][it % 2][:, 0:512],
                                          start=(hc == 0), stop=(hc == nhc - 1 and brow_src is None))
                        mm.then_inc(sem["pe"].h, 1)
                        t.wait_ge(sem["pe"].h, mm_done(it))
                    if brow_src is not None:
                        t.wait_ge(sem["msc"].h, msc0 + (ocg + 1) * 16)
                        mm = None
                        for tb in range(NT):
                            mm = t.matmul(psb(tb, 512), sc["ones_row_bf"][:],
                                          sc["b2row"][0:1, 0:512], start=False, stop=True)
                        mm.then_inc(sem["pe"].h, 1)
                        t.wait_ge(sem["pe"].h, pe0 + (ocg + 1) * per)
            sem["pe"].n = pe0 + 4 * per

            @blk.vector
            def _(v):
                for ocg in range(4):
                    v.wait_ge(sem["pe"].h, pe0 + (ocg + 1) * per)
                    for tb in range(NT):
                        xt = sc["x"][tb]
                        if evac_kind == "add":
                            sem["dve"].inc(v.tensor_tensor(
                                xt[:, 512 * ocg:512 * (ocg + 1)],
                                xt[:, 512 * ocg:512 * (ocg + 1)], psb(tb, 512), OP.add))
                        else:
                            sem["dve"].inc(v.scalar_tensor_tensor(
                                xt[:, 512 * ocg:512 * (ocg + 1)], psb(tb, 512),
                                gts[:, 8 * tb + e_idx:8 * tb + e_idx + 1],
                                xt[:, 512 * ocg:512 * (ocg + 1)], OP.mult, OP.add))
                        v.wait_ge(sem["dve"].h, sem["dve"].n)
            assert sem["dve"].n == dve0 + 4 * 8

    with nc.Block(f"mb_{tag}") as blk:
        @blk.sync
        def _(sy):
            sem["msb"].inc(sy.dma_start(
                sc["b1sb"][:, 0:64].rearrange("p (e ht) -> p e ht", e=8),
                W["b1f"].rearrange("e (ht p) -> p e ht", p=128),
                allow_slow_non_contiguous=True), 16)
            sem["msb"].inc(sy.dma_start(
                sc["sb1sb"][:, 0:16].rearrange("p c -> p c"),
                W["sb1f"].rearrange("o (c p) -> p (o c)", p=128),
                allow_slow_non_contiguous=True), 16)
            wait(sy, sem["msb"])

    def shdst(i):
        if i < 8:
            return sc["ctxh"][:, TLOC * i:TLOC * (i + 1)]
        return sc["qbuf"][:, TLOC * (i - 8):TLOC * (i - 7)]

    hstage(W["sw1"], 16, 0, sc["sb1sb"], 0, shdst, tag + "sh")

    def sh_src(hc, tb):
        base = shdst(hc)
        return base[:, 128 * tb:128 * (tb + 1)]

    ostage(W["sw2"], 16, 0, sh_src, W["sb2"], "add", 0, tag + "so")

    def edst(i):
        return sc["qbuf"][:, TLOC * i:TLOC * (i + 1)]

    def e_src(hc, tb):
        return sc["qbuf"][:, TLOC * hc + 128 * tb:TLOC * hc + 128 * (tb + 1)]

    for e in range(8):
        hstage(W["w1"][2048 * e:2048 * (e + 1), :], 8, 0, sc["b1sb"], 8 * e,
               edst, f"{tag}e{e}h")
        ostage(W["w2"][1024 * e:1024 * (e + 1), :], 8, 0, e_src, None,
               "gate", e, f"{tag}e{e}o")

    pe0b = sem["pe"].n
    dve0b = sem["dve"].n
    msc0b = sem["msc"].n
    with nc.Block(f"mb2_{tag}") as blk:
        @blk.gpsimd
        def _(g):
            wait(g, sem["pe"])
            for ocg in range(4):
                if ocg >= 1:
                    g.wait_ge(sem["pe"].h, pe0b + ocg * NT)
                g.dma_start(sc["b2row"][:, :],
                            W["b2"][:, 512 * ocg:512 * (ocg + 1)]
                            ).then_inc(sem["msc"].h, 16)
        sem["msc"].n += 4 * 16

        @blk.tensor
        def _(t):
            wait(t, sem["act"], sem["dve"])
            for ocg in range(4):
                t.wait_ge(sem["msc"].h, msc0b + (ocg + 1) * 16)
                if ocg >= 1:
                    t.wait_ge(sem["dve"].h, dve0b + ocg * NT)
                for tb in range(NT):
                    mm = t.matmul(psb(tb, 512), sc["gtT"][:, 128 * tb:128 * (tb + 1)],
                                  sc["b2row"][:, :], start=True, stop=True)
                    mm.then_inc(sem["pe"].h, 1)
                    t.wait_ge(sem["pe"].h, pe0b + ocg * NT + tb + 1)
        sem["pe"].n = pe0b + 4 * NT

        @blk.vector
        def _(v):
            for ocg in range(4):
                for tb in range(NT):
                    v.wait_ge(sem["pe"].h, pe0b + ocg * NT + tb + 1)
                    xt = sc["x"][tb]
                    sem["dve"].inc(v.tensor_tensor(
                        xt[:, 512 * ocg:512 * (ocg + 1)],
                        xt[:, 512 * ocg:512 * (ocg + 1)], psb(tb, 512), OP.add))
                    v.wait_ge(sem["dve"].h, sem["dve"].n)
        assert sem["dve"].n == dve0b + 4 * NT
    return



def build_headproj_l1b(ctx, sc, WNAT, wcol0, heads, dst_fn, norm_col0,
                       bias_sb, bias_col0, tag, csrc_dma=None, war_gap=0):
    nc = ctx.nc
    sem = ctx.sem
    ps_a, ps_b, ps_c, ps_d = sc["ps_a"], sc["ps_b"], sc["ps_c"], sc["ps_d"]
    npair = len(heads) // 2
    pairs = [(heads[2 * i], heads[2 * i + 1]) for i in range(npair)]

    pe0 = sem["pe"].n
    act0 = sem["act"].n
    dve0 = sem["dve"].n
    csrc0 = sem["csrc"].n
    wsem = [sem["w0"], sem["w1"]]
    w0n = [wsem[0].n, wsem[1].n]
    PPP = 4 + 8
    PPH = 4
    DL = 14
    rcp = sc["rcp"]
    row2 = sc["row2"]

    def pe_h(i_lin, k):
        j, o = divmod(i_lin, 2)
        return pe0 + 12 * j + 2 + 5 * o + k

    with nc.Block(f"hb_{tag}") as blk:
        @blk.sync
        def _(sy):
            wait(sy, sem["q0"], sem["q1"], sem["p0"], sem["msd"])
            for j, (ha, hb) in enumerate(pairs):
                if j >= 2:
                    sy.wait_ge(sem["pe"].h, pe0 + (j - 1) * 12)
                for f in range(NF):
                    if f % 2 == 0:
                        sy.dma_start(
                            sc["wbuf"][j % 2][:, 256 * f:256 * (f + 1)],
                            WNAT[128 * f:128 * (f + 1),
                                 wcol0 + 256 * j:wcol0 + 256 * (j + 1)]
                            ).then_inc(wsem[j % 2].h, 16)

        @blk.scalar
        def _(s):
            wait(s, sem["q0"], sem["q1"], sem["p0"])
            for j, (ha, hb) in enumerate(pairs):
                if j >= 2:
                    s.wait_ge(sem["pe"].h, pe0 + (j - 1) * 12)
                for f in range(NF):
                    if f % 2 == 1:
                        s.dma_start(
                            sc["wbuf"][j % 2][:, 256 * f:256 * (f + 1)],
                            WNAT[128 * f:128 * (f + 1),
                                 wcol0 + 256 * j:wcol0 + 256 * (j + 1)]
                            ).then_inc(wsem[j % 2].h, 16)
                for o in range(2):
                    i_lin = 2 * j + o
                    s.wait_ge(sem["pe"].h, pe0 + 12 * j + o + 1)
                    if i_lin >= 1:
                        s.wait_ge(sem["dve"].h, dve0 + DL * (i_lin - 1) + 3)
                    if i_lin == 0:
                        wait(s, sem["msd"])
                    ps = ps_a if o == 0 else ps_b
                    h = (ha, hb)[o]
                    s.activation(sc["tA"][:], ps[:], AF.Identity,
                                 bias=bias_sb[:, bias_col0 + i_lin:bias_col0 + i_lin + 1]
                                 ).then_inc(sem["act"].h, 1)
                    s.wait_ge(sem["act"].h, act0 + 2 * i_lin + 1)
                    s.wait_ge(sem["dve"].h, dve0 + DL * i_lin + 8)
                    s.sqrt(row2[0:1, :], row2[0:1, :]).then_inc(sem["act"].h, 1)
                    s.wait_ge(sem["act"].h, act0 + 2 * i_lin + 2)
        sem["act"].n = act0 + npair * 4
        for j in range(npair):
            wsem[j % 2].n += 16 * 16

        @blk.tensor
        def _(t):
            wait(t, sem["xt"])
            for j, (ha, hb) in enumerate(pairs):
                t.wait_ge(wsem[j % 2].h, w0n[j % 2] + (j // 2 + 1) * 16 * 16)
                for o in range(2):
                    ps = ps_a if o == 0 else ps_b
                    if j >= 1:
                        t.wait_ge(sem["act"].h, act0 + 2 * (2 * (j - 1) + o) + 1)
                    mm = None
                    for f in range(NF):
                        mm = mm512(t, ps, sc["wbuf"][j % 2][:, 256 * f + 128 * o:256 * f + 128 * (o + 1)],
                                   sc["xhatT"][:, TLOC * f:TLOC * (f + 1)],
                                   start=(f == 0), stop=(f == NF - 1))
                    mm.then_inc(sem["pe"].h, 1)
                    t.wait_ge(sem["pe"].h, pe0 + 12 * j + o + 1)
                for o in range(2):
                    i_lin = 2 * j + o
                    t.wait_ge(sem["dve"].h, dve0 + DL * i_lin + 3)
                    mm = mm512(t, ps_c[0:1, :], sc["ones_col_bf"][:], sc["tB"][:],
                               start=True, stop=True)
                    mm.then_inc(sem["pe"].h, 1)
                    t.wait_ge(sem["pe"].h, pe_h(i_lin, 1))
                    t.wait_ge(sem["dve"].h, dve0 + DL * i_lin + 4)
                    mm = mm512(t, ps_d[0:1, :], sc["ones_col_bf"][:], sc["tC"][:],
                               start=True, stop=True)
                    mm.then_inc(sem["pe"].h, 1)
                    t.wait_ge(sem["pe"].h, pe_h(i_lin, 2))
                    t.wait_ge(sem["dve"].h, dve0 + DL * i_lin + 8)
                    mm = mm512(t, ps_c, sc["ones_row_f"][:], rcp[0:1, :],
                               start=True, stop=True)
                    mm.then_inc(sem["pe"].h, 1)
                    t.wait_ge(sem["pe"].h, pe_h(i_lin, 3))
                    t.wait_ge(sem["dve"].h, dve0 + DL * i_lin + 9)
                    mm = mm512(t, ps_d, sc["ones_row_f"][:], row2[0:1, :],
                               start=True, stop=True)
                    mm.then_inc(sem["pe"].h, 1)
                    t.wait_ge(sem["pe"].h, pe_h(i_lin, 4))
                    t.wait_ge(sem["dve"].h, dve0 + DL * i_lin + 13)
                    mm = mm512(t, ps_c[0:1, :], sc["ones_col_bf"][:], sc["tC"][:],
                               start=True, stop=True)
                    mm.then_inc(sem["pe"].h, 1)
                    t.wait_ge(sem["pe"].h, pe_h(i_lin, 5))
        sem["pe"].n = pe0 + npair * PPP

        @blk.vector
        def _(v):
            wait(v, sem["msd"])
            def vf(instr):
                sem["dve"].inc(instr)
                v.wait_ge(sem["dve"].h, sem["dve"].n)
            for j, (ha, hb) in enumerate(pairs):
                for o in range(2):
                    h = (ha, hb)[o]
                    i_lin = 2 * j + o
                    base = sem["dve"].n
                    v.wait_ge(sem["act"].h, act0 + 2 * i_lin + 1)
                    if csrc_dma is not None and i_lin >= 2:
                        v.wait_ge(sem["csrc"].h, csrc0 + i_lin * war_gap * 16)
                    dst = dst_fn(i_lin, h)
                    vf(v.stream_shuffle(sc["tB"][:], sc["tA"][:], XOR1_MASK))
                    v.tensor_tensor(sc["tA"][:], sc["tA"][:], sc["ropeC_t"][:], OP.mult)
                    vf(v.tensor_tensor(sc["tB"][:], sc["tB"][:], sc["ropeS_t"][:], OP.mult))
                    vf(v.tensor_tensor(sc["tB"][:], sc["tA"][:], sc["tB"][:], OP.add))
                    vf(v.tensor_tensor(sc["tC"][:], sc["tB"][:], sc["tB"][:], OP.mult))
                    v.wait_ge(sem["pe"].h, pe_h(i_lin, 1))
                    vf(v.tensor_scalar(rcp[0:1, :], ps_c[0:1, :], 1.0 / 128.0, None, OP.mult))
                    v.wait_ge(sem["pe"].h, pe_h(i_lin, 2))
                    vf(v.tensor_scalar(row2[0:1, :], ps_d[0:1, :], 1.0 / 128.0, None, OP.mult))
                    vf(v.tensor_tensor(sc["ps_c"][0:1, :], rcp[0:1, :], rcp[0:1, :], OP.mult))
                    vf(v.scalar_tensor_tensor(row2[0:1, :], row2[0:1, :], 1e-5,
                                              ps_c[0:1, :], OP.add, OP.subtract))
                    v.wait_ge(sem["act"].h, act0 + 2 * i_lin + 2)
                    vf(v.reciprocal(row2[0:1, :], row2[0:1, :]))
                    v.wait_ge(sem["pe"].h, pe_h(i_lin, 3))
                    vf(v.tensor_tensor(sc["tB"][:], sc["tB"][:], ps_c[:], OP.subtract))
                    v.wait_ge(sem["pe"].h, pe_h(i_lin, 4))
                    vf(v.tensor_tensor(sc["tB"][:], sc["tB"][:], ps_d[:], OP.mult))
                    vf(v.tensor_scalar(dst, sc["tB"][:], sc["lnhg"][:, 0:1],
                                       sc["lnhg"][:, 1:2], OP.mult, OP.add))
                    vf(v.tensor_tensor(sc["tC"][:], dst, dst, OP.mult))
                    v.wait_ge(sem["pe"].h, pe_h(i_lin, 5))
                    vf(v.tensor_reduce(sc["nrm"][0:1, norm_col0 + i_lin:norm_col0 + i_lin + 1],
                                       ps_c[0:1, 0:1024], axis=mybir.AxisListType.X,
                                       op=OP.max))
                    assert sem["dve"].n == base + DL
        assert sem["dve"].n == dve0 + DL * 2 * npair

        if csrc_dma is not None:
            @blk.gpsimd
            def _(g):
                for j in range(npair):
                    for o in range(2):
                        i_lin = 2 * j + o
                        h = pairs[j][o]
                        g.wait_ge(sem["dve"].h, dve0 + DL * i_lin + 12)
                        csrc_dma(g, i_lin, h)
            sem["csrc"].n = csrc0 + npair * 2 * war_gap * 16
    return



def build_attn_b1(ctx, sc, W, Wf, XH, XQ, XK, MROW, CSRC_t, CDST_t, tag):
    nc = ctx.nc
    sem = ctx.sem
    ps_a, ps_b, ps_c, ps_d = sc["ps_a"], sc["ps_b"], sc["ps_c"], sc["ps_d"]
    CSRC = CSRC_t.ap()
    CDST = CDST_t.ap()
    build_ln(ctx, sc, tag, to_hbm=XH)
    build_loadrope(ctx, sc, ctx.CT, "ropeB_c", "ropeB_s", tag)

    reps = [("lnqg", sc["kbuf"][0]), ("lnqb", sc["kbuf"][1]),
            ("lnkg", sc["vbuf"][0]), ("lnkb", sc["vbuf"][1])]
    pe0 = sem["pe"].n
    act0 = sem["act"].n
    msb0 = sem["msb"].n
    with nc.Block(f"bp_{tag}") as blk:
        @blk.sync
        def _(sy):
            wait(sy, sem["pe"], sem["act"], sem["dve"])
            for ri, (nm, dstt) in enumerate(reps):
                for hf in range(2):
                    it = 2 * ri + hf
                    if it >= 1:
                        sy.wait_ge(sem["pe"].h, pe0 + it)
                    sy.dma_start(sc["rcp"][0:1, :],
                                 Wf[nm][0:1, 1024 * hf:1024 * (hf + 1)]
                                 ).then_inc(sem["msb"].h, 16)
            sy.dma_start(sc["bqsb"][:, 0:16],
                         Wf["bq"].rearrange("o (c p) -> p (o c)", p=128),
                         allow_slow_non_contiguous=True).then_inc(sem["msd"].h, 16)
            sy.dma_start(sc["bksb"][:, 0:16],
                         Wf["bkv"][0:1, 0:2048].rearrange("o (c p) -> p (o c)", p=128),
                         allow_slow_non_contiguous=True).then_inc(sem["msd"].h, 16)
            sy.dma_start(sc["bvsb"][:, 0:16],
                         Wf["bkv"][0:1, 2048:4096].rearrange("o (c p) -> p (o c)", p=128),
                         allow_slow_non_contiguous=True).then_inc(sem["msd"].h, 16)
            sy.dma_start(sc["lnhg"][:, 0:1],
                         Wf["lnhg"].rearrange("o p -> p o"),
                         allow_slow_non_contiguous=True).then_inc(sem["msd"].h, 16)
            sy.dma_start(sc["lnhg"][:, 1:2],
                         Wf["lnhb"].rearrange("o p -> p o"),
                         allow_slow_non_contiguous=True).then_inc(sem["msd"].h, 16)
        sem["msb"].n = msb0 + 8 * 16
        sem["msd"].n += 5 * 16

        @blk.tensor
        def _(t):
            for it in range(8):
                t.wait_ge(sem["msb"].h, msb0 + (it + 1) * 16)
                if it >= 2:
                    t.wait_ge(sem["act"].h, act0 + it - 1)
                mm = mm512(t, ps_a if it % 2 == 0 else ps_b,
                           sc["ones_row_f"][:], sc["rcp"][0:1, :],
                           start=True, stop=True)
                mm.then_inc(sem["pe"].h, 1)
                t.wait_ge(sem["pe"].h, pe0 + it + 1)
        sem["pe"].n = pe0 + 8

        @blk.scalar
        def _(s):
            for it in range(8):
                ri, hf = divmod(it, 2)
                s.wait_ge(sem["pe"].h, pe0 + it + 1)
                dstt = reps[ri][1]
                s.copy(dstt[:, 1024 * hf:1024 * (hf + 1)],
                       (ps_a if it % 2 == 0 else ps_b)[:]).then_inc(sem["act"].h, 1)
                s.wait_ge(sem["act"].h, act0 + it + 1)
        sem["act"].n = act0 + 8

    pe0 = sem["pe"].n
    act0 = sem["act"].n
    dve0 = sem["dve"].n
    msc0 = sem["msc"].n
    xqs = [sem["q0"], sem["q1"]]
    xks = [sem["vr0"], sem["vr1"]]
    xq0 = [xqs[0].n, xqs[1].n]
    xk0 = [xks[0].n, xks[1].n]
    NVT = 27
    bn = sc["smallf"]
    with nc.Block(f"px_{tag}") as blk:
        @blk.gpsimd
        def _(g):
            wait(g, sem["xh"], sem["act"])
            for i in range(NT):
                if i >= 2:
                    g.wait_ge(sem["dve"].h, dve0 + (i - 1) * NVT)
                g.dma_start(sc["xhat"][:, D * (i % 2):D * (i % 2 + 1)],
                            XH[128 * i:128 * (i + 1), :]).then_inc(sem["msc"].h, 16)
                g.wait_ge(sem["dve"].h, dve0 + i * NVT + NVT)
                g.dma_start(XQ[128 * i:128 * (i + 1), :],
                            sc["ctxh"][:, 2048 * (i % 2):2048 * (i % 2 + 1)]
                            ).then_inc(xqs[i % 2].h, 16)
                g.dma_start(XK[128 * i:128 * (i + 1), :],
                            sc["ctxh"][:, 4096 + 2048 * (i % 2):4096 + 2048 * (i % 2 + 1)]
                            ).then_inc(xks[i % 2].h, 16)
        sem["msc"].n = msc0 + NT * 16
        for p in range(2):
            xqs[p].n = xq0[p] + (NT // 2) * 16
            xks[p].n = xk0[p] + (NT // 2) * 16

        @blk.vector
        def _(v):
            def vf(instr):
                sem["dve"].inc(instr)
                v.wait_ge(sem["dve"].h, sem["dve"].n)
            wait(v, sem["act"])
            for i in range(NT):
                base = sem["dve"].n
                v.wait_ge(sem["msc"].h, msc0 + (i + 1) * 16)
                if i >= 2:
                    p = i % 2
                    v.wait_ge(xqs[p].h, xq0[p] + (i // 2) * 16)
                    v.wait_ge(xks[p].h, xk0[p] + (i // 2) * 16)
                xt = sc["xhat"][:, D * (i % 2):D * (i % 2 + 1)]
                st = None
                for c4 in range(4):
                    st = v.bn_stats(bn[:, 6 * c4:6 * c4 + 6], xt[:, 512 * c4:512 * (c4 + 1)])
                vf(st)
                vf(v.bn_aggr(bn[:, 24:26], bn[:, 0:24]))
                vf(v.tensor_scalar(bn[:, 26:27], bn[:, 25:26], 1e-5, None, OP.add))
                vf(v.tensor_copy(bn[:, 30:31], bn[:, 26:27]))
                v.wait_ge(sem["act"].h, act0 + 2 * i + 1)
                vf(v.reciprocal(bn[:, 27:28], bn[:, 30:31]))
                vf(v.tensor_tensor(bn[:, 28:29], bn[:, 24:25], bn[:, 27:28], OP.mult))
                vf(v.tensor_scalar(bn[:, 28:29], bn[:, 28:29], -1.0, None, OP.mult))
                vf(v.tensor_tensor(bn[:, 31:32], bn[:, 24:25], bn[:, 24:25], OP.mult))
                vf(v.tensor_tensor(bn[:, 31:32], bn[:, 31:32], bn[:, 25:26], OP.add))
                vf(v.tensor_scalar(bn[:, 31:32], bn[:, 31:32], 2048.0, None, OP.mult))
                v.wait_ge(sem["act"].h, act0 + 2 * i + 2)
                vf(v.tensor_copy(sc["qsc"][:, i:i + 1], bn[:, 31:32]))
                for kind in range(2):
                    grep = (sc["kbuf"] if kind == 0 else sc["vbuf"])[0]
                    brep = (sc["kbuf"] if kind == 0 else sc["vbuf"])[1]
                    dst0 = 2048 * (i % 2) if kind == 0 else 4096 + 2048 * (i % 2)
                    for hf in range(2):
                        vf(v.tensor_scalar(sc["tA"][:], xt[:, 1024 * hf:1024 * (hf + 1)],
                                           bn[:, 27:28], bn[:, 28:29], OP.mult, OP.add))
                        vf(v.tensor_tensor(sc["tA"][:], sc["tA"][:],
                                           grep[:, 1024 * hf:1024 * (hf + 1)], OP.mult))
                        vf(v.tensor_tensor(sc["ctxh"][:, dst0 + 1024 * hf:dst0 + 1024 * (hf + 1)],
                                           sc["tA"][:], brep[:, 1024 * hf:1024 * (hf + 1)],
                                           OP.add))
                assert sem["dve"].n == base + 23, sem["dve"].n - base
                while sem["dve"].n < base + NVT:
                    sem["dve"].inc(v.nop())
                    v.wait_ge(sem["dve"].h, sem["dve"].n)

        @blk.scalar
        def _(s):
            for i in range(NT):
                s.wait_ge(sem["dve"].h, dve0 + i * NVT + 4)
                s.sqrt(bn[:, 30:31], bn[:, 30:31]).then_inc(sem["act"].h, 1)
                s.wait_ge(sem["act"].h, act0 + 2 * i + 1)
                s.wait_ge(sem["dve"].h, dve0 + i * NVT + 10)
                s.sqrt(bn[:, 31:32], bn[:, 31:32]).then_inc(sem["act"].h, 1)
                s.wait_ge(sem["act"].h, act0 + 2 * i + 2)
        sem["act"].n = act0 + 2 * NT

    pe0 = sem["pe"].n
    act0 = sem["act"].n
    dve0 = sem["dve"].n
    msc0 = sem["msc"].n
    with nc.Block(f"pm_{tag}") as blk:
        @blk.tensor
        def _(t):
            wait(t, sem["dve"], sem["act"])
            mm = t.matmul(ps_a[0:8, 0:128], sc["qsc"][:, 0:8], sc["id128_bf"][:],
                          start=True, stop=True)
            mm.then_inc(sem["pe"].h, 1)
            t.wait_ge(sem["pe"].h, pe0 + 1)
            t.wait_ge(sem["msc"].h, msc0 + 32)
            mm = mm512(t, ps_c, sc["ones_row_f"][:], sc["rcp"][0:1, :],
                       start=True, stop=True)
            mm.then_inc(sem["pe"].h, 1)
            t.wait_ge(sem["pe"].h, pe0 + 2)
        sem["pe"].n = pe0 + 2

        @blk.scalar
        def _(s):
            s.wait_ge(sem["pe"].h, pe0 + 1)
            s.copy(sc["b2row"][0:8, 0:128], ps_a[0:8, 0:128]).then_inc(sem["act"].h, 1)
            s.wait_ge(sem["act"].h, act0 + 1)
        sem["act"].n = act0 + 1

        @blk.gpsimd
        def _(g):
            g.wait_ge(sem["act"].h, act0 + 1)
            g.dma_start(MROW[:, :], sc["b2row"][0:8, 0:128]).then_inc(sem["msc"].h, 16)
            g.wait_ge(sem["msc"].h, msc0 + 16)
            g.dma_start(sc["rcp"][0:1, :],
                        MROW.rearrange("a b -> (a b)")).then_inc(sem["msc"].h, 16)
        sem["msc"].n = msc0 + 32

        @blk.vector
        def _(v):
            v.wait_ge(sem["pe"].h, pe0 + 2)
            wait(v, sem["tab"])
            sem["dve"].inc(v.tensor_tensor(sc["ropeC_t"][:], sc["ropeC_t"][:],
                                           ps_c[:], OP.mult))
            v.wait_ge(sem["dve"].h, sem["dve"].n)
            sem["dve"].inc(v.tensor_tensor(sc["ropeS_t"][:], sc["ropeS_t"][:],
                                           ps_c[:], OP.mult))
            v.wait_ge(sem["dve"].h, sem["dve"].n)

    with nc.Block(f"xk_{tag}") as blk:
        @blk.sync
        def _(sy):
            wait(sy, sem["vr0"], sem["vr1"], sem["dve"])
            for j in range(NF):
                sem["xt"].inc(sy.dma_start(sc["xhatT"][:, TLOC * j:TLOC * (j + 1)],
                                           XK[:, 128 * j:128 * (j + 1)], transpose=True), 16)

    def kdst(i_lin, h):
        return sc["pT"][i_lin % 2][:]

    def kdma(g, i_lin, h):
        g.dma_start(CSRC[128 * h:128 * (h + 1), 0:1024],
                    sc["pT"][i_lin % 2][:]).then_inc(ctx.sem["csrc"].h, 16)

    build_headproj_l1b(ctx, sc, W["wkv"], 0, list(range(16)), kdst, 0,
                       sc["bksb"], 0, tag + "K", csrc_dma=kdma, war_gap=1)
    build_vstage(ctx, sc, W["wkv"], CSRC, tag, natural=True, nat_col0=2048)
    build_gather(ctx, sc, CSRC_t, CDST_t, 16, tag)

    with nc.Block(f"xq_{tag}") as blk:
        @blk.sync
        def _(sy):
            wait(sy, sem["q0"], sem["q1"], sem["dve"], sem["pe"])
            for j in range(NF):
                sem["xt"].inc(sy.dma_start(sc["xhatT"][:, TLOC * j:TLOC * (j + 1)],
                                           XQ[:, 128 * j:128 * (j + 1)], transpose=True), 16)

    for phase in range(2):
        heads = list(range(8 * phase, 8 * phase + 8))

        def qdst(i_lin, h):
            return sc["qbuf"][:, TLOC * i_lin:TLOC * (i_lin + 1)]

        build_headproj_l1b(ctx, sc, W["wq"], 1024 * phase, heads, qdst,
                           32 + 8 * phase, sc["bqsb"], 8 * phase, f"{tag}Q{phase}")
        build_bias(ctx, sc, CDST, phase, 8, 128, tag, bcol_key="bcolr0")
        build_attn_phase(ctx, sc, CDST, phase, heads, 128, tag,
                         bv_sb=sc["bvsb"], bv_col0=0, causal=False)
        build_outproj_nat(ctx, sc, W["wo"], phase, tag)

    pe0 = sem["pe"].n
    dve0 = sem["dve"].n
    msc0 = sem["msc"].n
    with nc.Block(f"bo_{tag}") as blk:
        @blk.gpsimd
        def _(g):
            wait(g, sem["pe"])
            for ocg in range(4):
                if ocg >= 1:
                    g.wait_ge(sem["pe"].h, pe0 + ocg)
                g.dma_start(sc["b2row"][0:1, 0:512],
                            W["bo"][0:1, 512 * ocg:512 * (ocg + 1)]).then_inc(sem["msc"].h, 16)
        sem["msc"].n = msc0 + 4 * 16

        @blk.tensor
        def _(t):
            wait(t, sem["dve"], sem["act"])
            for ocg in range(4):
                t.wait_ge(sem["msc"].h, msc0 + (ocg + 1) * 16)
                if ocg >= 1:
                    t.wait_ge(sem["dve"].h, dve0 + ocg * NT)
                mm = mm512(t, ps_a[:, 0:512], sc["ones_row_bf"][:],
                           sc["b2row"][0:1, 0:512], start=True, stop=True, width=512)
                mm.then_inc(sem["pe"].h, 1)
                t.wait_ge(sem["pe"].h, pe0 + ocg + 1)
        sem["pe"].n = pe0 + 4

        @blk.vector
        def _(v):
            for ocg in range(4):
                v.wait_ge(sem["pe"].h, pe0 + ocg + 1)
                for tb in range(NT):
                    xt = sc["x"][tb]
                    sem["dve"].inc(v.tensor_tensor(
                        xt[:, 512 * ocg:512 * (ocg + 1)],
                        xt[:, 512 * ocg:512 * (ocg + 1)], ps_a[:, 0:512], OP.add))
                    v.wait_ge(sem["dve"].h, sem["dve"].n)
        assert sem["dve"].n == dve0 + 4 * NT
    return


def build_outproj_nat(ctx, sc, WO, phase, tag):
    def rhs_load(eng, oc, f, dest):
        dglob = phase * 8 + f
        return eng.dma_start(dest,
                             WO[128 * dglob:128 * (dglob + 1), 256 * oc:256 * (oc + 1)])

    def lhs(f, tb):
        return sc["ctxh"][:, TLOC * f + 128 * tb:TLOC * f + 128 * (tb + 1)]

    def evac(v, ps_ap, tb, oc):
        xt = sc["x"][tb]
        return v.tensor_tensor(xt[:, 256 * oc:256 * (oc + 1)],
                               xt[:, 256 * oc:256 * (oc + 1)], ps_ap, OP.add)

    build_tokmaj_mm(ctx, sc, lhs, 8, rhs_load, 8, 256, evac,
                    tag + "on" + str(phase), evac_eng="vector")
    return


# version 03
# speedup vs baseline: 1.2148x; 1.2148x over previous
import math
import os
import contextlib
import numpy as np
import ml_dtypes
import concourse.bass as bass
import concourse.mybir as mybir
from concourse.bass_utils import run_bass_kernel_spmd
from concourse.alu_op_type import AluOpType as OP

F32 = mybir.dt.float32
BF16 = mybir.dt.bfloat16
AF = mybir.ActivationFunctionType

B, S, D = 4, 2048, 2048
NCORE = 8
TLOC = 1024
NT = TLOC // 128
NF = D // 128
NKC = S // 128
LN_EPS = 1e-5
NEG_BIG = -80.0
BF16NP = ml_dtypes.bfloat16

STAGES = int(os.environ.get("KSTAGES", "99"))



def host_constants(h):
    c = {}
    tpos = np.arange(TLOC, dtype=np.float64) + 1024.0 * h

    tri = np.ones((NKC, 128, 128), dtype=np.float32)
    lt = (np.arange(128)[:, None] <= np.arange(128)[None, :]).astype(np.float32)
    for kc in range(8 * h, 8 * h + 8):
        tri[kc] = lt
    c["tri"] = np.ascontiguousarray(tri.reshape(NKC * 128, 128)).astype(BF16NP)

    bcol = np.zeros((1, NKC), dtype=np.float32)
    for kc in range(NKC):
        if kc * 128 > (1024 * h + TLOC - 1):
            bcol[0, kc] = NEG_BIG
    c["bcol"] = bcol

    sc = np.ones((1, NKC), dtype=np.float32)
    if h == 0:
        for kc in range(1, 9):
            sc[0, kc] = 0.0
    c["sc"] = sc
    sc2 = np.ones((1, NKC), dtype=np.float32)
    if h == 1:
        for kc in range(9, NKC):
            sc2[0, kc] = 0.0
    c["sc2"] = sc2

    j = np.arange(64, dtype=np.float64)
    theta = 1.0 / (10000.0 ** (j / 64.0))
    ang = tpos[None, :] * theta[:, None]
    ca, sa = np.cos(ang), np.sin(ang)
    c["ropeA_c"] = np.concatenate([ca, ca], 0).astype(np.float32)
    c["ropeA_s"] = np.concatenate([-sa, sa], 0).astype(np.float32)

    jg = np.arange(32, dtype=np.float64)
    thg = 1.0 / (10000.0 ** (2.0 * jg / 64.0))
    angg = tpos[None, :] * thg[:, None]
    cg, sg = np.cos(angg), np.sin(angg)
    C2 = np.repeat(cg, 2, axis=0)
    S2 = np.empty((64, TLOC))
    S2[0::2] = -sg
    S2[1::2] = sg
    c["ropeG_c"] = np.concatenate([C2, C2], 0).astype(np.float32)
    c["ropeG_s"] = np.concatenate([S2, S2], 0).astype(np.float32)

    u = np.linspace(0.0, 1.0, 64)
    fr = (8000.0 / 200.0) ** u * (200.0 / 1000.0)
    freq = 200.0 * fr / 1000.0
    angb = tpos[None, :] * freq[:, None]
    cb, sb = np.cos(angb), np.sin(angb)
    qsc4 = 128.0 ** (-0.25)
    c["ropeB_c"] = (np.repeat(cb, 2, axis=0) * qsc4).astype(np.float32)
    SB = np.empty((128, TLOC))
    SB[0::2] = -sb
    SB[1::2] = sb
    c["ropeB_s"] = (SB * qsc4).astype(np.float32)
    c["id128"] = np.eye(128, dtype=np.float32)
    return c


CONST_SPECS = [
    ("tri", [NKC * 128, 128], BF16),
    ("bcol", [1, NKC], F32),
    ("sc", [1, NKC], F32),
    ("sc2", [1, NKC], F32),
    ("ropeA_c", [128, TLOC], F32),
    ("ropeA_s", [128, TLOC], F32),
    ("ropeG_c", [128, TLOC], F32),
    ("ropeG_s", [128, TLOC], F32),
    ("ropeB_c", [128, TLOC], F32),
    ("ropeB_s", [128, TLOC], F32),
    ("id128", [128, 128], F32),
]



class Sem:
    __slots__ = ("h", "n")
    def __init__(self, stack, nc, name):
        self.h = stack.enter_context(nc.semaphore(name))
        self.n = 0
    def inc(self, instr, k=1):
        instr.then_inc(self.h, k)
        self.n += k
        return instr
    def incw(self, eng, instr, k=1):
        instr.then_inc(self.h, k)
        self.n += k
        eng.wait_ge(self.h, self.n)
        return instr


class Ctx:
    def __init__(self, nc, stack):
        self.nc = nc
        self.stack = stack
        names = ["pe", "act", "dve", "cc", "xin", "tab", "p0", "xh", "xt",
                 "w0", "w1", "kv0", "kv1", "csrc", "vr0", "vr1", "vr2", "vr3",
                 "msc", "msb", "msd", "out", "q0", "q1"]
        self.sem = {n: Sem(stack, nc, "s_" + n) for n in names}

    def sb(self, name, shape, dtype):
        return self.stack.enter_context(self.nc.sbuf_tensor(name, shape, dtype))

    def ps(self, name, shape, dtype=F32):
        return self.stack.enter_context(self.nc.psum_tensor(name, shape, dtype))


XOR1_MASK = [i ^ 1 for i in range(32)]


def wait(eng, *sems):
    for s in sems:
        if s.n > 0:
            eng.wait_ge(s.h, s.n)


def mm512(t, ps_ap, lhsT, rhs_ap, start, stop, width=1024):
    mm = None
    for hw in range(0, width, 512):
        w = min(512, width - hw)
        mm = t.matmul(ps_ap[:, hw:hw + w], lhsT, rhs_ap[:, hw:hw + w],
                      start=start, stop=stop)
    return mm


def alloc_scratch(ctx):
    sc = {}
    sc["x"] = [ctx.sb(f"x{i}", [128, D], F32) for i in range(NT)]
    sc["xhat"] = ctx.sb("xhat", [128, 2 * D], BF16)
    sc["xhatT"] = ctx.sb("xhatT", [128, NF * TLOC], BF16)
    sc["qbuf"] = ctx.sb("qbuf", [128, 8 * TLOC], BF16)
    sc["ctxh"] = ctx.sb("ctxh", [128, 8 * TLOC], BF16)
    sc["kbuf"] = [ctx.sb(f"kbuf{i}", [128, 2048], BF16) for i in range(2)]
    sc["vbuf"] = [ctx.sb(f"vbuf{i}", [128, 2048], BF16) for i in range(2)]
    sc["wbuf"] = [ctx.sb(f"wbuf{i}", [128, NF * 256], BF16) for i in range(2)]
    sc["tA"] = ctx.sb("tA", [128, 1024], BF16)
    sc["tB"] = ctx.sb("tB", [128, 1024], BF16)
    sc["tC"] = ctx.sb("tC", [128, 1024], BF16)
    sc["tD"] = ctx.sb("tD", [128, 1024], BF16)
    sc["pT"] = [ctx.sb(f"pT{i}", [128, 1024], BF16) for i in range(2)]
    sc["ones_col_bf"] = ctx.sb("ones_col_bf", [128, 1], BF16)
    sc["ones_row_f"] = ctx.sb("ones_row_f", [1, 128], F32)
    sc["nrm"] = ctx.sb("nrm", [1, 128], F32)
    sc["nrm2"] = ctx.sb("nrm2", [1, 128], F32)
    sc["bcolr"] = ctx.sb("bcolr", [128, NKC], F32)
    sc["bcolr0"] = ctx.sb("bcolr0", [128, NKC], F32)
    sc["scr"] = ctx.sb("scr", [128, NKC], F32)
    sc["scr2"] = ctx.sb("scr2", [128, NKC], F32)
    sc["biasmat"] = ctx.sb("biasmat", [128, 16 * NKC], F32)
    sc["rcp"] = ctx.sb("rcp", [1, 1024], F32)
    sc["smallf"] = ctx.sb("smallf", [128, 64], F32)
    sc["smallg"] = ctx.sb("smallg", [1, 16], F32)
    sc["qsc"] = ctx.sb("qsc", [128, 16], BF16)
    sc["qscf"] = ctx.sb("qscf", [128, 32], F32)
    sc["gts"] = ctx.sb("gts", [128, 64], F32)
    sc["gtsb"] = ctx.sb("gtsb", [128, 64], BF16)
    sc["gtmp"] = ctx.sb("gtmp", [128, 24], F32)
    sc["gtT"] = ctx.sb("gtT", [8, 1024], BF16)
    sc["b2row"] = ctx.sb("b2row", [8, 512], BF16)
    sc["brow"] = sc["b2row"]
    sc["ones_row_bf"] = ctx.sb("ones_row_bf", [1, 128], BF16)
    sc["b1sb"] = ctx.sb("b1sb", [128, 64], F32)
    sc["sb1sb"] = ctx.sb("sb1sb", [128, 16], F32)
    sc["id128_bf"] = ctx.sb("id128_bf", [128, 128], BF16)
    sc["row2"] = ctx.sb("row2", [1, 1024], F32)
    sc["lnhg"] = ctx.sb("lnhg", [128, 2], F32)
    sc["bqsb"] = ctx.sb("bqsb", [128, 16], F32)
    sc["bksb"] = ctx.sb("bksb", [128, 16], F32)
    sc["bvsb"] = ctx.sb("bvsb", [128, 16], F32)
    sc["ropeC_t"] = ctx.sb("ropeC_t", [128, TLOC], BF16)
    sc["ropeS_t"] = ctx.sb("ropeS_t", [128, TLOC], BF16)
    sc["tri_sb"] = ctx.sb("tri_sb", [128, NKC * 128], BF16)
    sc["ps_a"] = ctx.ps("ps_a", [128, 1024])
    sc["ps_b"] = ctx.ps("ps_b", [128, 1024])
    sc["ps_c"] = ctx.ps("ps_c", [128, 1024])
    sc["ps_d"] = ctx.ps("ps_d", [128, 1024])
    return sc



def build_startup(ctx, sc, XIN, CT):
    nc = ctx.nc
    sem = ctx.sem
    with nc.Block("startup") as blk:
        @blk.sync
        def _(sy):
            for i in range(NT):
                sem["xin"].inc(sy.dma_start(sc["x"][i][:], XIN[128 * i:128 * (i + 1), :]), 16)
            sem["xin"].inc(sy.dma_start(
                sc["tri_sb"][:].rearrange("p (kc q) -> p kc q", q=128),
                CT["tri"].rearrange("(kc p) q -> p kc q", p=128)), 16)

        @blk.gpsimd
        def _(g):
            sem["tab"].inc(g.dma_start(sc["id128_bf"][:], CT["id128"][:, :]), 16)
            sem["tab"].inc(g.dma_start(sc["nrm"][0:1, 0:NKC], CT["bcol"][:, :]), 16)
            sem["tab"].inc(g.dma_start(sc["nrm2"][0:1, 0:NKC], CT["sc"][:, :]), 16)
            sem["tab"].inc(g.dma_start(sc["smallg"][0:1, 0:NKC], CT["sc2"][:, :]), 16)

        @blk.vector
        def _(v):
            v.memset(sc["ones_col_bf"][:], 1.0)
            v.memset(sc["ones_row_bf"][:], 1.0)
            v.memset(sc["bcolr0"][:], 0.0)
            sem["dve"].incw(v, v.memset(sc["ones_row_f"][:], 1.0))

        @blk.tensor
        def _(t):
            wait(t, sem["tab"], sem["dve"])
            t.matmul(sc["ps_a"][:, 0:NKC], sc["ones_row_f"][:],
                     sc["nrm"][0:1, 0:NKC], start=True, stop=True)
            t.matmul(sc["ps_b"][:, 0:NKC], sc["ones_row_f"][:],
                     sc["nrm2"][0:1, 0:NKC], start=True, stop=True)
            sem["pe"].incw(t, t.matmul(sc["ps_c"][:, 0:NKC], sc["ones_row_f"][:],
                                   sc["smallg"][0:1, 0:NKC], start=True, stop=True))

        @blk.scalar
        def _(s):
            wait(s, sem["pe"])
            s.copy(sc["bcolr"][:], sc["ps_a"][:, 0:NKC])
            s.copy(sc["scr"][:], sc["ps_b"][:, 0:NKC])
            sem["act"].incw(s, s.copy(sc["scr2"][:], sc["ps_c"][:, 0:NKC]))


def build_loadrope(ctx, sc, CT, cname, sname, tag):
    nc = ctx.nc
    sem = ctx.sem
    with nc.Block(f"lr_{tag}") as blk:
        @blk.gpsimd
        def _(g):
            wait(g, sem["dve"], sem["csrc"])
            sem["tab"].inc(g.dma_start(sc["ropeC_t"][:], CT[cname][:, :]), 16)
            sem["tab"].inc(g.dma_start(sc["ropeS_t"][:], CT[sname][:, :]), 16)


def build_phase0_cast(ctx, pairs, tag):
    nc = ctx.nc
    sem = ctx.sem
    with nc.Block(f"p0_{tag}") as blk:
        @blk.gpsimd
        def _(g):
            for srcap, dstap in pairs:
                sem["p0"].inc(g.dma_start(dstap, srcap), 16)



def build_ln(ctx, sc, tag, want_stats=False, eps=LN_EPS, to_hbm=None):
    nc = ctx.nc
    sem = ctx.sem
    bn = sc["smallf"]
    xhat = sc["xhat"]
    act0 = sem["act"].n
    dve0 = sem["dve"].n
    xh0 = sem["xh"].n
    NV = 9
    with nc.Block(f"ln_{tag}") as blk:
        @blk.vector
        def _(v):
            wait(v, sem["xin"], sem["pe"], sem["act"])

            def vf(instr):
                sem["dve"].inc(instr)
                v.wait_ge(sem["dve"].h, sem["dve"].n)

            for i in range(NT):
                base = sem["dve"].n
                xt = sc["x"][i]
                st = None
                for c4 in range(4):
                    st = v.bn_stats(bn[:, 6 * c4:6 * c4 + 6], xt[:, 512 * c4:512 * (c4 + 1)])
                vf(st)
                vf(v.bn_aggr(bn[:, 24:26], bn[:, 0:24]))
                if want_stats:
                    v.tensor_copy(sc["smallf"][:, 48 + 2 * i:50 + 2 * i], bn[:, 24:26])
                vf(v.tensor_scalar(bn[:, 26:27], bn[:, 25:26], eps, None, OP.add))
                vf(v.tensor_copy(bn[:, 30:31], bn[:, 26:27]))
                v.wait_ge(sem["act"].h, act0 + i + 1)
                vf(v.reciprocal(bn[:, 27:28], bn[:, 30:31]))
                vf(v.tensor_tensor(bn[:, 28:29], bn[:, 24:25], bn[:, 27:28], OP.mult))
                vf(v.tensor_scalar(bn[:, 28:29], bn[:, 28:29], -1.0, None, OP.mult))
                if to_hbm is not None and i >= 2:
                    v.wait_ge(sem["xh"].h, xh0 + i * 16)
                vf(v.tensor_scalar(xhat[:, D * (i % 2):D * (i % 2 + 1)], xt[:],
                                   bn[:, 27:28], bn[:, 28:29], OP.mult, OP.add))
                while sem["dve"].n < base + NV:
                    sem["dve"].inc(v.nop())
                assert sem["dve"].n == base + NV

        @blk.scalar
        def _(s):
            for i in range(NT):
                s.wait_ge(sem["dve"].h, dve0 + NV * i + 4)
                s.sqrt(bn[:, 30:31], bn[:, 30:31]).then_inc(sem["act"].h, 1)
                s.wait_ge(sem["act"].h, act0 + i + 1)
        sem["act"].n = act0 + NT

        if to_hbm is not None:
            @blk.gpsimd
            def _(g):
                for i in range(NT):
                    g.wait_ge(sem["dve"].h, dve0 + NV * i + 8)
                    g.dma_start(to_hbm[128 * i:128 * (i + 1), :],
                                sc["xhat"][:, D * (i % 2):D * (i % 2 + 1)]
                                ).then_inc(sem["xh"].h, 16)
            sem["xh"].n = xh0 + NT * 16


def build_bounce(ctx, sc, XH, tag):
    nc = ctx.nc
    sem = ctx.sem
    with nc.Block(f"tx_{tag}") as blk:
        @blk.sync
        def _(sy):
            wait(sy, sem["xh"], sem["msb"])
            for j in range(NF):
                sem["xt"].inc(sy.dma_start(sc["xhatT"][:, TLOC * j:TLOC * (j + 1)],
                                           XH[:, 128 * j:128 * (j + 1)], transpose=True), 16)



def build_headproj(ctx, sc, WBF, heads, dst_fn, norm_col0, ropeC, ropeS,
                   tag, scale=1.0, csrc_dma=None, war_gap=0, rope_mode="half",
                   qpad=None):
    nc = ctx.nc
    sem = ctx.sem
    ps_a, ps_b, ps_c = sc["ps_a"], sc["ps_b"], sc["ps_c"]
    npair = len(heads) // 2
    pairs = [(heads[2 * i], heads[2 * i + 1]) for i in range(npair)]

    pe0 = sem["pe"].n
    act0 = sem["act"].n
    dve0 = sem["dve"].n
    csrc0 = sem["csrc"].n
    wsem = [sem["w0"], sem["w1"]]
    w0n = [wsem[0].n, wsem[1].n]
    DPH = 3 if rope_mode == "half" else 4
    AP_ = 1 + (4 if qpad is not None else 0)

    with nc.Block(f"hp_{tag}") as blk:
        @blk.sync
        def _(sy):
            wait(sy, sem["p0"], sem["q0"], sem["q1"])
            for j, (ha, hb) in enumerate(pairs):
                if j >= 2:
                    sy.wait_ge(sem["pe"].h, pe0 + (j - 1) * 4)
                for f in range(NF):
                    sy.dma_start(
                        sc["wbuf"][j % 2][:, 256 * f:256 * (f + 1)],
                        WBF[128 * ha:128 * ha + 256, 128 * f:128 * (f + 1)],
                        transpose=True).then_inc(wsem[j % 2].h, 16)

        @blk.scalar
        def _(s):
            wait(s, sem["p0"], sem["q0"], sem["q1"])
            for j, (ha, hb) in enumerate(pairs):
                if j >= 2:
                    s.wait_ge(sem["pe"].h, pe0 + (j - 1) * 4)
                for o in range(2):
                    s.wait_ge(sem["pe"].h, pe0 + 4 * j + o + 1)
                    i_lin = 2 * j + o
                    if i_lin >= 1:
                        s.wait_ge(sem["dve"].h, dve0 + DPH * (i_lin - 1) + DPH - 1)
                    ps = ps_a if o == 0 else ps_b
                    if rope_mode == "half":
                        s.activation(sc["tA"][:], ps[:], AF.Copy, scale=scale)
                        s.activation(sc["tB"][0:64, :], ps[64:128, :], AF.Copy, scale=scale)
                        s.activation(sc["tB"][64:128, :], ps[0:64, :], AF.Copy,
                                     scale=scale).then_inc(sem["act"].h, 1)
                    else:
                        s.activation(sc["tA"][:], ps[:], AF.Copy,
                                     scale=scale).then_inc(sem["act"].h, 1)
                    s.wait_ge(sem["act"].h, act0 + AP_ * i_lin + 1)
                    if qpad is not None:
                        s.wait_ge(sem["dve"].h, dve0 + DPH * i_lin + DPH - 1)
                        stg = dst_fn(i_lin, 0)
                        (apA, kvhA), (apB, kvhB) = qpad(i_lin)
                        s.memzero(apA).then_inc(sem["act"].h, 1)
                        s.wait_ge(sem["act"].h, act0 + AP_ * i_lin + 2)
                        s.copy(apA[kvhA:kvhA + 64, :], stg[0:64, :]
                               ).then_inc(sem["act"].h, 1)
                        s.wait_ge(sem["act"].h, act0 + AP_ * i_lin + 3)
                        s.memzero(apB).then_inc(sem["act"].h, 1)
                        s.wait_ge(sem["act"].h, act0 + AP_ * i_lin + 4)
                        s.copy(apB[kvhB:kvhB + 64, :], stg[64:128, :]
                               ).then_inc(sem["act"].h, 1)
                        s.wait_ge(sem["act"].h, act0 + AP_ * i_lin + 5)

        sem["act"].n = act0 + npair * 2 * AP_
        for j in range(npair):
            wsem[j % 2].n += 16 * 16

        @blk.tensor
        def _(t):
            wait(t, sem["xt"])
            for j, (ha, hb) in enumerate(pairs):
                t.wait_ge(wsem[j % 2].h, w0n[j % 2] + (j // 2 + 1) * 16 * 16)
                for o in range(2):
                    ps = ps_a if o == 0 else ps_b
                    if j >= 1:
                        i_prev = 2 * (j - 1) + o
                        t.wait_ge(sem["act"].h, act0 + AP_ * i_prev + 1)
                    mm = None
                    for f in range(NF):
                        mm = mm512(t, ps, sc["wbuf"][j % 2][:, 256 * f + 128 * o:256 * f + 128 * (o + 1)],
                                   sc["xhatT"][:, TLOC * f:TLOC * (f + 1)],
                                   start=(f == 0), stop=(f == NF - 1))
                    mm.then_inc(sem["pe"].h, 1)
                    t.wait_ge(sem["pe"].h, pe0 + 4 * j + o + 1)
                for o in range(2):
                    t.wait_ge(sem["dve"].h, dve0 + DPH * (2 * j + o) + DPH)
                    mm = mm512(t, ps_c[0:1, :], sc["ones_col_bf"][:],
                               sc["tC"][:], start=True, stop=True)
                    mm.then_inc(sem["pe"].h, 1)
                    t.wait_ge(sem["pe"].h, pe0 + 4 * j + 2 + o + 1)
        sem["pe"].n = pe0 + npair * 4

        @blk.vector
        def _(v):
            wait(v, sem["tab"])
            for j, (ha, hb) in enumerate(pairs):
                for o in range(2):
                    h = (ha, hb)[o]
                    i_lin = 2 * j + o
                    v.wait_ge(sem["act"].h, act0 + AP_ * i_lin + 1)
                    if csrc_dma is not None and i_lin >= 2:
                        v.wait_ge(sem["csrc"].h, csrc0 + i_lin * war_gap * 16)
                    if qpad is not None and i_lin >= 2:
                        v.wait_ge(sem["act"].h, act0 + AP_ * (i_lin - 2) + AP_)
                    dst = dst_fn(i_lin, h)
                    b0 = dve0 + DPH * i_lin
                    if rope_mode == "xor1":
                        v.stream_shuffle(sc["tB"][:], sc["tA"][:], XOR1_MASK
                                         ).then_inc(sem["dve"].h, 1)
                        v.wait_ge(sem["dve"].h, b0 + 1)
                    v.tensor_tensor(sc["tA"][:], sc["tA"][:], ropeC, OP.mult)
                    v.tensor_tensor(sc["tB"][:], sc["tB"][:], ropeS,
                                    OP.mult).then_inc(sem["dve"].h, 1)
                    v.wait_ge(sem["dve"].h, b0 + DPH - 2)
                    v.tensor_tensor(dst, sc["tA"][:], sc["tB"][:], OP.add
                                    ).then_inc(sem["dve"].h, 1)
                    v.wait_ge(sem["dve"].h, b0 + DPH - 1)
                    v.tensor_tensor(sc["tC"][:], dst, dst,
                                    OP.mult).then_inc(sem["dve"].h, 1)
                    v.wait_ge(sem["dve"].h, b0 + DPH)
                    v.wait_ge(sem["pe"].h, pe0 + 4 * j + 2 + o + 1)
                    v.tensor_reduce(sc["nrm"][0:1, norm_col0 + i_lin:norm_col0 + i_lin + 1],
                                    ps_c[0:1, 0:1024], axis=mybir.AxisListType.X,
                                    op=OP.max)
        sem["dve"].n = dve0 + DPH * 2 * npair

        if csrc_dma is not None:
            @blk.gpsimd
            def _(g):
                for j in range(npair):
                    for o in range(2):
                        i_lin = 2 * j + o
                        h = pairs[j][o]
                        g.wait_ge(sem["dve"].h, dve0 + DPH * i_lin + DPH - 1)
                        csrc_dma(g, i_lin, h)
            sem["csrc"].n = csrc0 + npair * 2 * war_gap * 16
    return



def build_vstage(ctx, sc, WVBF, CSRC, tag, ndim=2048, csrc_row0=2048,
                 natural=False, nat_col0=0):
    nc = ctx.nc
    sem = ctx.sem
    ps_a, ps_b = sc["ps_a"], sc["ps_b"]
    n_oc = ndim // 256
    pe0 = sem["pe"].n
    act0 = sem["act"].n
    wsem = [sem["w0"], sem["w1"]]
    w0n = [wsem[0].n, wsem[1].n]
    vrsem = [sem["vr0"], sem["vr1"], sem["vr2"], sem["vr3"]]
    vr0 = [sme.n for sme in vrsem]
    ring = 4
    inc_per = 16 if ndim == 2048 else 32

    with nc.Block(f"vs_{tag}") as blk:
        @blk.sync
        def _(sy):
            for oc in range(n_oc):
                if oc >= 2:
                    sy.wait_ge(sem["pe"].h, pe0 + (oc - 1) * NT)
                for f in range(NF):
                    if natural:
                        sy.dma_start(
                            sc["wbuf"][oc % 2][:, 256 * f:256 * f + 256],
                            WVBF[128 * f:128 * (f + 1),
                                 nat_col0 + 256 * oc:nat_col0 + 256 * (oc + 1)]
                            ).then_inc(wsem[oc % 2].h, 16)
                    else:
                        sy.dma_start(
                            sc["wbuf"][oc % 2][:, 256 * f:256 * f + 256],
                            WVBF[256 * oc:256 * (oc + 1), 128 * f:128 * (f + 1)],
                            transpose=True).then_inc(wsem[oc % 2].h, 16)

        @blk.scalar
        def _(s):
            for oc in range(n_oc):
                if oc >= 2:
                    s.wait_ge(sem["pe"].h, pe0 + (oc - 1) * NT)
                for tb in range(NT):
                    idx = oc * NT + tb
                    r = idx % ring
                    s.wait_ge(sem["pe"].h, pe0 + idx + 1)
                    if idx >= ring:
                        s.wait_ge(vrsem[r].h, vr0[r] + (idx // ring) * inc_per)
                    ps = ps_a if tb % 2 == 0 else ps_b
                    s.copy(sc["tD"][:, 256 * r:256 * (r + 1)],
                           ps[:, 0:256]).then_inc(sem["act"].h, 1)
                    s.wait_ge(sem["act"].h, act0 + idx + 1)
        sem["act"].n = act0 + n_oc * NT
        for oc in range(n_oc):
            wsem[oc % 2].n += 16 * 16

        @blk.tensor
        def _(t):
            for oc in range(n_oc):
                t.wait_ge(wsem[oc % 2].h, w0n[oc % 2] + (oc // 2 + 1) * 16 * 16)
                for tb in range(NT):
                    idx = oc * NT + tb
                    ps = ps_a if tb % 2 == 0 else ps_b
                    if idx >= 2:
                        t.wait_ge(sem["act"].h, act0 + idx - 1)
                    mm = None
                    for f in range(NF):
                        mm = t.matmul(ps[:, 0:256],
                                 sc["xhatT"][:, TLOC * f + 128 * tb:TLOC * f + 128 * (tb + 1)],
                                 sc["wbuf"][oc % 2][:, 256 * f:256 * f + 256],
                                 start=(f == 0), stop=(f == NF - 1))
                    mm.then_inc(sem["pe"].h, 1)
                    t.wait_ge(sem["pe"].h, pe0 + idx + 1)
        sem["pe"].n = pe0 + n_oc * NT

        @blk.gpsimd
        def _(g):
            for oc in range(n_oc):
                d0 = oc * 256
                for tb in range(NT):
                    idx = oc * NT + tb
                    r = idx % ring
                    g.wait_ge(sem["act"].h, act0 + idx + 1)
                    srcap = sc["tD"][:, 256 * r:256 * (r + 1)]
                    if ndim == 2048:
                        dhalf = d0 // 1024
                        dcol = d0 % 1024
                        dst = CSRC[csrc_row0 + 2 * 128 * tb + dhalf:
                                   csrc_row0 + 2 * 128 * (tb + 1) + dhalf:2,
                                   dcol:dcol + 256]
                        g.dma_start(dst, srcap).then_inc(vrsem[r].h, 16)
                    else:
                        dst = CSRC[csrc_row0 + 128 * tb:csrc_row0 + 128 * (tb + 1),
                                   d0:d0 + 256]
                        g.dma_start(dst, srcap).then_inc(vrsem[r].h, 16)
                        dst2 = CSRC[csrc_row0 + 128 * tb:csrc_row0 + 128 * (tb + 1),
                                    512 + d0:512 + d0 + 256]
                        g.dma_start(dst2, srcap).then_inc(vrsem[r].h, 16)
        for idx in range(n_oc * NT):
            vrsem[idx % 4].n += inc_per
    return



def build_gather(ctx, sc, CSRC_t, CDST_t, nheads_k, tag, norm_row=4096):
    nc = ctx.nc
    sem = ctx.sem
    with nc.Block(f"cc_{tag}") as blk:
        @blk.vector
        def _(v):
            wait(v, sem["dve"])
            sem["dve"].inc(v.memset(sc["rcp"][0:1, :], 0.0))
            v.wait_ge(sem["dve"].h, sem["dve"].n)
            sem["dve"].incw(v, v.tensor_copy(sc["rcp"][0:1, 0:nheads_k],
                                         sc["nrm"][0:1, 0:nheads_k]))

        @blk.gpsimd
        def _(g):
            wait(g, sem["dve"])
            sem["csrc"].inc(g.dma_start(
                CSRC_t.ap()[norm_row:norm_row + 1, 0:1024], sc["rcp"][0:1, :]), 16)
            wait(g, sem["csrc"], sem["vr0"], sem["vr1"], sem["vr2"], sem["vr3"])
            g.collective_compute(
                "AllGather",
                mybir.AluOpType.bypass,
                replica_groups=[[0, 1], [2, 3], [4, 5], [6, 7]],
                ins=[CSRC_t.ap().opt()],
                outs=[CDST_t.ap().opt()],
            ).then_inc(sem["cc"].h, 1)
            sem["cc"].n += 1
            g.wait_ge(sem["cc"].h, sem["cc"].n)
    return



def build_bias(ctx, sc, CDST, phase, nh_phase, hd, tag, kn_of_head=None,
               qn_of_head=None, nrow0=4096, nrow1=8193, col0=None,
               bcol_key="bcolr"):
    nc = ctx.nc
    sem = ctx.sem
    if kn_of_head is None:
        kn_of_head = [phase * nh_phase + i for i in range(nh_phase)]
    if qn_of_head is None:
        qn_of_head = [32 + phase * nh_phase + i for i in range(nh_phase)]
    if col0 is None:
        col0 = phase * nh_phase * NKC
    nk = max(kn_of_head) + 1
    with nc.Block(f"bias_{tag}{phase}") as blk:
        @blk.gpsimd
        def _(g):
            wait(g, sem["cc"])
            sem["msc"].inc(g.dma_start(sc["nrm2"][0:1, 0:nk], CDST[nrow0:nrow0 + 1, 0:nk]), 16)
            sem["msc"].inc(g.dma_start(sc["nrm2"][0:1, 32:32 + nk], CDST[nrow1:nrow1 + 1, 0:nk]), 16)

        @blk.vector
        def _(v):
            wait(v, sem["msc"], sem["dve"], sem["pe"])

            def vf(instr):
                sem["dve"].inc(instr)
                v.wait_ge(sem["dve"].h, sem["dve"].n)

            vf(v.tensor_tensor(sc["nrm2"][0:1, 64:64 + nk], sc["nrm2"][0:1, 0:nk],
                               sc["nrm2"][0:1, 32:32 + nk], OP.max))
            cp = None
            for i in range(nh_phase):
                cp = v.tensor_copy(sc["nrm2"][0:1, 96 + i:97 + i],
                                   sc["nrm2"][0:1, 64 + kn_of_head[i]:65 + kn_of_head[i]])
            vf(cp)
            cp = None
            for i in range(nh_phase):
                cp = v.tensor_copy(sc["nrm2"][0:1, 112 + i:113 + i],
                                   sc["nrm"][0:1, qn_of_head[i]:qn_of_head[i] + 1])
            vf(cp)
            vf(v.tensor_tensor(sc["smallg"][0:1, 0:nh_phase],
                               sc["nrm2"][0:1, 112:112 + nh_phase],
                               sc["nrm2"][0:1, 96:96 + nh_phase], OP.mult))

        @blk.scalar
        def _(s):
            wait(s, sem["dve"])
            sem["act"].incw(s, s.sqrt(sc["smallg"][0:1, 0:nh_phase], sc["smallg"][0:1, 0:nh_phase]))

        @blk.vector
        def _(v2):
            wait(v2, sem["act"])
            sem["dve"].incw(v2, v2.tensor_scalar(sc["smallg"][0:1, 0:nh_phase],
                           sc["smallg"][0:1, 0:nh_phase], -1.0 / math.sqrt(hd), None, OP.mult))

        @blk.tensor
        def _(t):
            wait(t, sem["dve"], sem["act"])
            sem["pe"].incw(t, t.matmul(sc["ps_c"][:, 0:nh_phase], sc["ones_row_f"][:],
                                   sc["smallg"][0:1, 0:nh_phase], start=True, stop=True))

        @blk.scalar
        def _(s2):
            wait(s2, sem["pe"])
            sem["act"].incw(s2, s2.copy(sc["smallf"][:, 0:nh_phase], sc["ps_c"][:, 0:nh_phase]))

        @blk.vector
        def _(v3):
            wait(v3, sem["act"])
            ins = None
            for i in range(nh_phase):
                ins = v3.tensor_scalar(
                    sc["biasmat"][:, col0 + i * NKC:col0 + (i + 1) * NKC],
                    sc[bcol_key][:, 0:NKC], sc["smallf"][:, i:i + 1], None, OP.add)
            sem["dve"].incw(v3, ins)
    return



def build_attn_phase(ctx, sc, CDST, phase, heads, hd, tag, bv_sb=None,
                     bv_col0=0, causal=True):
    nc = ctx.nc
    sem = ctx.sem
    ps_a, ps_b, ps_c, ps_d = sc["ps_a"], sc["ps_b"], sc["ps_c"], sc["ps_d"]
    NHP = len(heads)
    pe0 = sem["pe"].n
    act0 = sem["act"].n
    dve0 = sem["dve"].n
    kvsem = [sem["kv0"], sem["kv1"]]
    kv0n = [kvsem[0].n, kvsem[1].n]
    escale = 1.0 / math.sqrt(hd)
    AH = 34 + (1 if bv_sb is not None else 0)

    def pe_s(i, kc):
        return pe0 + 33 * i + (1 if kc == 0 else 2 * kc)

    def pe_cd(i, kc):
        return pe0 + 33 * i + (2 * kc + 3 if kc <= 14 else 32)

    with nc.Block(f"at_{tag}{phase}") as blk:
        @blk.gpsimd
        def _(g):
            wait(g, sem["cc"])
            for i, h in enumerate(heads):
                if i >= 2:
                    g.wait_ge(sem["pe"].h, pe0 + 33 * (i - 1))
                kb = sc["kbuf"][i % 2]
                vb = sc["vbuf"][i % 2]
                for r in range(2):
                    g.dma_start(kb[:, 1024 * r:1024 * (r + 1)],
                                CDST[4097 * r + 128 * h:4097 * r + 128 * (h + 1), 0:1024]
                                ).then_inc(kvsem[i % 2].h, 16)
                    vsrc = CDST[4097 * r + 2048:4097 * r + 4096, :]
                    vsrc = vsrc.rearrange("(kc i two) j -> i kc two j", kc=8, i=128, two=2)
                    dhalf = (h * 128) // 1024
                    dcol = (h * 128) % 1024
                    g.dma_start(
                        vb[:, 1024 * r:1024 * (r + 1)].rearrange("i (kc j) -> i kc j", kc=8),
                        vsrc[:, :, dhalf, dcol:dcol + 128]).then_inc(kvsem[i % 2].h, 16)
        for i in range(NHP):
            kvsem[i % 2].n += 64

        @blk.tensor
        def _(t):
            for i, h in enumerate(heads):
                kb = sc["kbuf"][i % 2]
                vb = sc["vbuf"][i % 2]
                t.wait_ge(kvsem[i % 2].h, kv0n[i % 2] + (i // 2 + 1) * 64)
                if i >= 1:
                    t.wait_ge(sem["act"].h, act0 + 17 * i)
                    t.wait_ge(sem["dve"].h, dve0 + AH * i)
                for kc in range(NKC):
                    ps = ps_a if kc % 2 == 0 else ps_b
                    if kc >= 2:
                        t.wait_ge(sem["act"].h, act0 + 17 * i + (kc - 1))
                    mm = mm512(t, ps, kb[:, 128 * kc:128 * (kc + 1)],
                               sc["qbuf"][:, TLOC * i:TLOC * (i + 1)], start=True, stop=True)
                    mm.then_inc(sem["pe"].h, 1)
                    t.wait_ge(sem["pe"].h, pe_s(i, kc))
                    if kc >= 1:
                        kp = kc - 1
                        t.wait_ge(sem["dve"].h, dve0 + AH * i + 2 * (kp + 1))
                        pt = sc["pT"][kp % 2]
                        mm512(t, ps_c, vb[:, 128 * kp:128 * (kp + 1)], pt[:],
                              start=(kp == 0), stop=False)
                        mm = mm512(t, ps_d[0:1, :], sc["ones_col_bf"][:], pt[:],
                                   start=(kp == 0), stop=False)
                        mm.then_inc(sem["pe"].h, 1)
                        t.wait_ge(sem["pe"].h, pe_cd(i, kp))
                kp = NKC - 1
                t.wait_ge(sem["dve"].h, dve0 + AH * i + 2 * (kp + 1))
                pt = sc["pT"][kp % 2]
                mm512(t, ps_c, vb[:, 128 * kp:128 * (kp + 1)], pt[:],
                      start=False, stop=True)
                mm = mm512(t, ps_d[0:1, :], sc["ones_col_bf"][:], pt[:],
                           start=False, stop=True)
                mm.then_inc(sem["pe"].h, 1)
                t.wait_ge(sem["pe"].h, pe_cd(i, 15))
                t.wait_ge(sem["act"].h, act0 + 17 * i + 16)
                t.wait_ge(sem["dve"].h, dve0 + AH * i + 33)
                mm = mm512(t, ps_a, sc["ones_row_f"][:], sc["rcp"][0:1, :],
                           start=True, stop=True)
                mm.then_inc(sem["pe"].h, 1)
                t.wait_ge(sem["pe"].h, pe0 + 33 * i + 33)
        sem["pe"].n = pe0 + NHP * 33

        @blk.scalar
        def _(s):
            for i, h in enumerate(heads):
                hb = phase * 8 + i
                for kc in range(NKC):
                    ps = ps_a if kc % 2 == 0 else ps_b
                    s.wait_ge(sem["pe"].h, pe_s(i, kc))
                    s.activation(sc["pT"][kc % 2][:], ps[:], AF.Exp,
                                 scale=escale,
                                 bias=sc["biasmat"][:, hb * NKC + kc:hb * NKC + kc + 1]
                                 ).then_inc(sem["act"].h, 1)
                    s.wait_ge(sem["act"].h, act0 + 17 * i + kc + 1)
                s.wait_ge(sem["pe"].h, pe_cd(i, 15))
                s.copy(sc["tD"][:], ps_c[:]).then_inc(sem["act"].h, 1)
                s.wait_ge(sem["act"].h, act0 + 17 * i + 17)
        sem["act"].n = act0 + NHP * 17

        @blk.vector
        def _(v):
            for i, h in enumerate(heads):
                for kc in range(NKC):
                    v.wait_ge(sem["act"].h, act0 + 17 * i + kc + 1)
                    pt = sc["pT"][kc % 2]
                    if causal and 1 <= kc <= 8:
                        sem["dve"].inc(v.tensor_scalar(pt[:, 0:128 * kc], pt[:, 0:128 * kc],
                                       sc["scr"][:, kc:kc + 1], None, OP.mult))
                        v.wait_ge(sem["dve"].h, sem["dve"].n)
                    elif causal and kc >= 9:
                        w = 128 * (kc - 8)
                        sem["dve"].inc(v.tensor_scalar(pt[:, 0:w], pt[:, 0:w],
                                       sc["scr2"][:, kc:kc + 1], None, OP.mult))
                        v.wait_ge(sem["dve"].h, sem["dve"].n)
                    else:
                        sem["dve"].inc(v.nop())
                        v.wait_ge(sem["dve"].h, sem["dve"].n)
                    off = (kc % 8) * 128
                    if causal:
                        sem["dve"].inc(v.tensor_tensor(pt[:, off:off + 128], pt[:, off:off + 128],
                                       sc["tri_sb"][:, 128 * kc:128 * (kc + 1)], OP.mult))
                    else:
                        sem["dve"].inc(v.nop())
                    v.wait_ge(sem["dve"].h, sem["dve"].n)
                v.wait_ge(sem["pe"].h, pe_cd(i, 15))
                sem["dve"].inc(v.reciprocal(sc["rcp"][0:1, :], ps_d[0:1, :]))
                v.wait_ge(sem["dve"].h, sem["dve"].n)
                v.wait_ge(sem["act"].h, act0 + 17 * i + 17)
                v.wait_ge(sem["pe"].h, pe0 + 33 * i + 33)
                sem["dve"].inc(v.tensor_tensor(sc["ctxh"][:, TLOC * i:TLOC * (i + 1)],
                               sc["tD"][:], ps_a[:], OP.mult))
                v.wait_ge(sem["dve"].h, sem["dve"].n)
                if bv_sb is not None:
                    if i == 0:
                        wait(v, sem["msd"])
                    sem["dve"].inc(v.tensor_scalar(
                        sc["ctxh"][:, TLOC * i:TLOC * (i + 1)],
                        sc["ctxh"][:, TLOC * i:TLOC * (i + 1)],
                        bv_sb[:, bv_col0 + phase * 8 + i:bv_col0 + phase * 8 + i + 1],
                        None, OP.add))
                    v.wait_ge(sem["dve"].h, sem["dve"].n)
        assert sem["dve"].n == dve0 + NHP * AH
    return



def build_tokmaj_mm(ctx, sc, lhs_src, nchunk_f, rhs_load, n_oc, oc_w, evac_fn,
                    tag, evac_eng="scalar"):
    nc = ctx.nc
    sem = ctx.sem
    ps_a, ps_b = sc["ps_a"], sc["ps_b"]
    pe0 = sem["pe"].n
    act0 = sem["act"].n
    wsem = [sem["w0"], sem["w1"]]
    w0n = [wsem[0].n, wsem[1].n]
    assert oc_w <= 256

    with nc.Block(f"tm_{tag}") as blk:
        @blk.sync
        def _(sy):
            for oc in range(n_oc):
                if oc >= 2:
                    sy.wait_ge(sem["pe"].h, pe0 + (oc - 1) * NT)
                for f in range(nchunk_f):
                    rhs_load(sy, oc, f,
                             sc["wbuf"][oc % 2][:, 256 * f:256 * f + oc_w]
                             ).then_inc(wsem[oc % 2].h, 16)

        @blk.scalar
        def _(s):
            for oc in range(n_oc):
                if oc >= 2:
                    s.wait_ge(sem["pe"].h, pe0 + (oc - 1) * NT)
                if evac_eng == "scalar":
                    for tb in range(NT):
                        s.wait_ge(sem["pe"].h, pe0 + oc * NT + tb + 1)
                        ps = ps_a if tb % 2 == 0 else ps_b
                        evac_fn(s, ps[:, 0:oc_w], tb, oc).then_inc(sem["act"].h, 1)
                        s.wait_ge(sem["act"].h, act0 + oc * NT + tb + 1)
        if evac_eng == "scalar":
            sem["act"].n = act0 + n_oc * NT
        for oc in range(n_oc):
            wsem[oc % 2].n += nchunk_f * 16

        @blk.tensor
        def _(t):
            for oc in range(n_oc):
                t.wait_ge(wsem[oc % 2].h, w0n[oc % 2] + (oc // 2 + 1) * nchunk_f * 16)
                for tb in range(NT):
                    ps = ps_a if tb % 2 == 0 else ps_b
                    if oc * NT + tb >= 2:
                        t.wait_ge(sem["act"].h, act0 + oc * NT + tb - 1)
                    mm = None
                    for f in range(nchunk_f):
                        mm = t.matmul(ps[:, 0:oc_w], lhs_src(f, tb),
                                 sc["wbuf"][oc % 2][:, 256 * f:256 * f + oc_w],
                                 start=(f == 0), stop=(f == nchunk_f - 1))
                    mm.then_inc(sem["pe"].h, 1)
                    t.wait_ge(sem["pe"].h, pe0 + oc * NT + tb + 1)
        sem["pe"].n = pe0 + n_oc * NT

        if evac_eng == "vector":
            @blk.vector
            def _(v):
                for oc in range(n_oc):
                    for tb in range(NT):
                        v.wait_ge(sem["pe"].h, pe0 + oc * NT + tb + 1)
                        ps = ps_a if tb % 2 == 0 else ps_b
                        evac_fn(v, ps[:, 0:oc_w], tb, oc).then_inc(sem["act"].h, 1)
                        v.wait_ge(sem["act"].h, act0 + oc * NT + tb + 1)
            sem["act"].n = act0 + n_oc * NT
    return


def build_outproj(ctx, sc, WOBF, phase, tag, nchunk=8):
    def rhs_load(eng, oc, f, dest):
        dglob = phase * nchunk + f
        return eng.dma_start(dest,
                             WOBF[256 * oc:256 * (oc + 1), 128 * dglob:128 * (dglob + 1)],
                             transpose=True)

    def lhs(f, tb):
        return sc["ctxh"][:, TLOC * f + 128 * tb:TLOC * f + 128 * (tb + 1)]

    def evac(v, ps_ap, tb, oc):
        xt = sc["x"][tb]
        return v.tensor_tensor(xt[:, 256 * oc:256 * (oc + 1)],
                               xt[:, 256 * oc:256 * (oc + 1)], ps_ap, OP.add)

    build_tokmaj_mm(ctx, sc, lhs, nchunk, rhs_load, 8, 256, evac,
                    tag + f"op{phase}", evac_eng="vector")
    return



def build_attn_a(ctx, sc, WBF, XH, CSRC_t, CDST_t, tag):
    CSRC = CSRC_t.ap()
    CDST = CDST_t.ap()
    if STAGES < 1:
        return
    build_ln(ctx, sc, tag, to_hbm=XH)
    if STAGES < 2:
        return
    build_loadrope(ctx, sc, ctx.CT, "ropeA_c", "ropeA_s", tag)
    build_bounce(ctx, sc, XH, tag)
    if STAGES < 3:
        return

    def kdst(i_lin, h):
        return sc["pT"][i_lin % 2][:]

    def kdma(g, i_lin, h):
        g.dma_start(CSRC[128 * h:128 * (h + 1), 0:1024],
                    sc["pT"][i_lin % 2][:]).then_inc(ctx.sem["csrc"].h, 16)

    build_headproj(ctx, sc, WBF["k"], list(range(16)), kdst, 0,
                   sc["ropeC_t"][:], sc["ropeS_t"][:], tag + "K",
                   csrc_dma=kdma, war_gap=1)
    if STAGES < 4:
        return
    build_vstage(ctx, sc, WBF["v"], CSRC, tag)
    if STAGES < 5:
        return
    build_gather(ctx, sc, CSRC_t, CDST_t, 16, tag)
    if STAGES < 6:
        return

    for phase in range(2):
        heads = list(range(8 * phase, 8 * phase + 8))

        def qdst(i_lin, h):
            return sc["qbuf"][:, TLOC * i_lin:TLOC * (i_lin + 1)]

        build_headproj(ctx, sc, WBF["q"], heads, qdst, 32 + 8 * phase,
                       sc["ropeC_t"][:], sc["ropeS_t"][:], f"{tag}Q{phase}")
        if STAGES < 7:
            return
        build_bias(ctx, sc, CDST, phase, 8, 128, tag)
        if STAGES < 8:
            return
        build_attn_phase(ctx, sc, CDST, phase, heads, 128, tag)
        if STAGES < 9:
            return
        build_outproj(ctx, sc, WBF["o"], phase, tag)
        if STAGES < 10:
            return
    return



def build_store(ctx, sc, XOUT):
    nc = ctx.nc
    sem = ctx.sem
    with nc.Block("store") as blk:
        @blk.sync
        def _(sy):
            wait(sy, sem["dve"], sem["act"], sem["pe"], sem["cc"],
                 sem["xh"], sem["xt"], sem["csrc"], sem["msc"], sem["msb"], sem["msd"],
                 sem["vr0"], sem["vr1"], sem["vr2"], sem["vr3"],
                 sem["w0"], sem["w1"], sem["kv0"], sem["kv1"], sem["p0"],
                 sem["q0"], sem["q1"])
            for i in range(NT):
                sem["out"].inc(sy.dma_start(XOUT[128 * i:128 * (i + 1), :], sc["x"][i][:]), 16)
            wait(sy, sem["out"])



SUBLAYERS = [
    ("attn_a", "L0A"),
    ("ffn_l0", "L0F"),
    ("gqa", "L0B"),
    ("ffn_l0", "L0F"),
    ("attn_a", "L1A"),
    ("moe", "L1F"),
    ("attn_b1", "L1B"),
    ("moe", "L1F"),
]

WEIGHT_INPUTS = {
    "L0A": [("q", "L0_attn_a_Wq", [2048, 2048]), ("k", "L0_attn_a_Wk", [2048, 2048]),
            ("v", "L0_attn_a_Wv", [2048, 2048]), ("o", "L0_attn_a_Wo", [2048, 2048])],
    "L1A": [("q", "L1_attn_a_Wq", [2048, 2048]), ("k", "L1_attn_a_Wk", [2048, 2048]),
            ("v", "L1_attn_a_Wv", [2048, 2048]), ("o", "L1_attn_a_Wo", [2048, 2048])],
    "L0B": [("q", "L0_attn_b_wq", [2048, 2048]), ("k", "L0_attn_b_wk", [512, 2048]),
            ("v", "L0_attn_b_wv", [512, 2048]), ("o", "L0_attn_b_wo", [2048, 2048])],
    "L0F": [("gu", "L0_ffn_w_gate_up", [16384, 2048]), ("pr", "L0_ffn_w_proj", [2048, 8192])],
    "L1F": [("gate", "L1_ffn_gate_w", [2048, 8]), ("w1", "L1_ffn_w1", [8 * 2048, 1024]),
            ("b1", "L1_ffn_b1", [8, 1024]), ("w2", "L1_ffn_w2", [8 * 1024, 2048]),
            ("b2", "L1_ffn_b2", [8, 2048]),
            ("sw1", "L1_ffn_shared_w1", [2048, 2048]), ("sb1", "L1_ffn_shared_b1", [1, 2048]),
            ("sw2", "L1_ffn_shared_w2", [2048, 2048]), ("sb2", "L1_ffn_shared_b2", [1, 2048])],
    "L1B": [("wq", "L1_attn_b_Wq", [2048, 2048]), ("bq", "L1_attn_b_bq", [1, 2048]),
            ("wkv", "L1_attn_b_Wkv", [2048, 4096]), ("bkv", "L1_attn_b_bkv", [1, 4096]),
            ("lnqg", "L1_attn_b_ln_q_g", [1, 2048]), ("lnqb", "L1_attn_b_ln_q_b", [1, 2048]),
            ("lnkg", "L1_attn_b_ln_kv_g", [1, 2048]), ("lnkb", "L1_attn_b_ln_kv_b", [1, 2048]),
            ("lnhg", "L1_attn_b_ln_h_g", [1, 128]), ("lnhb", "L1_attn_b_ln_h_b", [1, 128]),
            ("wo", "L1_attn_b_Wo", [2048, 2048]), ("bo", "L1_attn_b_bo", [1, 2048])],
}


def build_program(sub_idx_list):
    nc = bass.Bass("TRN2", target_bir_lowering=False, debug=False)
    stack = contextlib.ExitStack()
    ctx = Ctx(nc, stack)

    XIN = nc.dram_tensor("XIN", [TLOC, D], F32, kind="ExternalInput").ap()
    XOUT = nc.dram_tensor("XOUT", [TLOC, D], F32, kind="ExternalOutput").ap()
    CT = {}
    for name, shape, dt in CONST_SPECS:
        CT[name] = nc.dram_tensor("c_" + name, shape, dt, kind="ExternalInput").ap()

    kinds = [SUBLAYERS[i] for i in sub_idx_list]
    wnames = sorted({wkey for _, wkey in kinds})
    weights = {}
    input_names = ["XIN"] + ["c_" + n for n, _, _ in CONST_SPECS]
    for wkey in wnames:
        weights[wkey] = {}
        for short, iname, shape in WEIGHT_INPUTS[wkey]:
            weights[wkey][short] = nc.dram_tensor(iname, shape, F32, kind="ExternalInput").ap()
            input_names.append(iname)

    XH = nc.dram_tensor("XH", [TLOC, D], BF16, kind="Internal").ap()

    wbf = {}
    for wkey in wnames:
        if wkey in ("L0A", "L1A", "L0B"):
            wbf[wkey] = {}
            for short, iname, shape in WEIGHT_INPUTS[wkey]:
                wbf[wkey][short] = nc.dram_tensor(
                    f"{wkey}_{short}_bf", shape, BF16, kind="Internal").ap()
        elif wkey == "L1F":
            wbf[wkey] = {}
            for short in ["gate", "w1", "w2", "sw1", "sw2", "b2", "sb2"]:
                shape = dict((s, sh) for s, n, sh in WEIGHT_INPUTS["L1F"])[short]
                wbf[wkey][short] = nc.dram_tensor(
                    f"L1F_{short}_bf", shape, BF16, kind="Internal").ap()
        elif wkey == "L1B":
            wbf[wkey] = {}
            for short in ["wq", "wkv", "wo", "bo"]:
                shape = dict((s, sh) for s, n, sh in WEIGHT_INPUTS["L1B"])[short]
                wbf[wkey][short] = nc.dram_tensor(
                    f"L1B_{short}_bf", shape, BF16, kind="Internal").ap()

    cds = {}
    for i in sub_idx_list:
        kind, wkey = SUBLAYERS[i]
        if kind in ("attn_a", "attn_b1"):
            cds[i] = (
                nc.dram_tensor(f"CSRC{i}", [4097, 1024], BF16, kind="Internal"),
                nc.dram_tensor(f"CDST{i}", [8194, 1024], BF16, kind="Internal"),
            )
        elif kind == "gqa":
            cds[i] = (
                nc.dram_tensor(f"CSRC{i}", [1537, 1024], BF16, kind="Internal"),
                nc.dram_tensor(f"CDST{i}", [3074, 1024], BF16, kind="Internal"),
            )
    have_ffn = any(SUBLAYERS[i][0] == "ffn_l0" for i in sub_idx_list)
    if have_ffn:
        WGU_Q = nc.dram_tensor("WGU_Q", [16384, 2048], BF16, kind="Internal").ap()
        WPR_Q = nc.dram_tensor("WPR_Q", [2048, 8192], BF16, kind="Internal").ap()
        HHBM = nc.dram_tensor("HHBM", [8192, TLOC], BF16, kind="Internal").ap()
    have_b1 = any(SUBLAYERS[i][0] == "attn_b1" for i in sub_idx_list)
    if have_b1:
        XQ = nc.dram_tensor("XQ", [TLOC, D], BF16, kind="Internal").ap()
        XK = nc.dram_tensor("XK", [TLOC, D], BF16, kind="Internal").ap()
        MROW = nc.dram_tensor("MROW", [8, 128], BF16, kind="Internal").ap()

    ctx.CT = CT
    with stack:
        sc = alloc_scratch(ctx)
        build_startup(ctx, sc, XIN, CT)
        pairs = []
        for wkey in wnames:
            if wkey in wbf:
                for short in wbf[wkey]:
                    pairs.append((weights[wkey][short][:, :], wbf[wkey][short][:, :]))
        if pairs:
            build_phase0_cast(ctx, pairs, "w")
        if have_ffn:
            build_quant(ctx, sc, weights["L0F"]["gu"], WGU_Q, 16384, 2048, "gu")
            build_quant(ctx, sc, weights["L0F"]["pr"], WPR_Q, 2048, 8192, "pr")

        for i in sub_idx_list:
            kind, wkey = SUBLAYERS[i]
            if kind == "attn_a":
                build_attn_a(ctx, sc, wbf[wkey], XH, cds[i][0], cds[i][1], f"s{i}")
            elif kind == "ffn_l0":
                build_ffn_l0(ctx, sc, WGU_Q, WPR_Q, XH, HHBM, f"s{i}")
            elif kind == "gqa":
                build_attn_gqa(ctx, sc, wbf[wkey], XH, cds[i][0], cds[i][1], f"s{i}")
            elif kind == "moe":
                Wm = dict(wbf["L1F"])
                Wm["b1f"] = weights["L1F"]["b1"]
                Wm["sb1f"] = weights["L1F"]["sb1"]
                build_moe(ctx, sc, Wm, XH, f"s{i}")
            elif kind == "attn_b1":
                build_attn_b1(ctx, sc, wbf["L1B"], weights["L1B"], XH, XQ, XK,
                              MROW, cds[i][0], cds[i][1], f"s{i}")
            else:
                raise NotImplementedError(kind)
        build_store(ctx, sc, XOUT)
    return nc, input_names


def run_sublayers(sub_idx_list, x_full, inputs):
    nc, input_names = build_program(sub_idx_list)
    in_maps = []
    for c in range(NCORE):
        b, h = c // 2, c % 2
        m = {"XIN": np.ascontiguousarray(x_full[b, 1024 * h:1024 * (h + 1), :])}
        consts = host_constants(h)
        for name, shape, dt in CONST_SPECS:
            m["c_" + name] = np.ascontiguousarray(consts[name].reshape(shape))
        for iname in input_names:
            if iname.startswith(("XIN", "c_")):
                continue
            arr = np.asarray(inputs[iname], dtype=np.float32)
            if arr.ndim == 1:
                arr = arr.reshape(1, -1)
            elif arr.ndim == 3:
                arr = arr.reshape(-1, arr.shape[-1])
            m[iname] = np.ascontiguousarray(arr)
        in_maps.append(m)
    res = run_bass_kernel_spmd(nc, in_maps, core_ids=list(range(NCORE)))
    out = np.empty_like(x_full)
    for c in range(NCORE):
        b, h = c // 2, c % 2
        out[b, 1024 * h:1024 * (h + 1), :] = res.results[c]["XOUT"]
    return out


def kernel(**inputs):
    x = np.ascontiguousarray(np.asarray(inputs["L0_attn_a_hidden_states"], dtype=np.float32))
    for group in [[0, 1, 2, 3, 4, 5, 6, 7]]:
        x = run_sublayers(group, x, inputs)
    return x



def build_quant(ctx, sc, WSRC, WQ, R, Cc, tag):
    nc = ctx.nc
    sem = ctx.sem
    nrt = R // 128
    ncc = Cc // 1024
    qs0 = sem["msc"].n
    qsem = [sem["q0"], sem["q1"]]
    qo0 = [qsem[0].n, qsem[1].n]
    act0 = sem["act"].n
    dve0 = sem["dve"].n
    qsc = sc["qsc"]
    qscf = sc["qscf"]
    with nc.Block(f"qz_{tag}") as blk:
        @blk.gpsimd
        def _(g):
            for it in range(nrt * ncc):
                rt, cc = divmod(it, ncc)
                g.wait_ge(qsem[it % 2].h, qo0[it % 2] + (it // 2) * 16)
                g.dma_start(sc["wbuf"][it % 2][:, 0:1024],
                            WSRC[128 * rt:128 * (rt + 1), 1024 * cc:1024 * (cc + 1)]
                            ).then_inc(sem["msc"].h, 16)
                g.wait_ge(sem["dve"].h, dve0 + (it + 1) * 8)
                g.dma_start(WQ[128 * rt:128 * (rt + 1), 1024 * cc:1024 * (cc + 1)],
                            sc["wbuf"][it % 2][:, 1024:2048]
                            ).then_inc(qsem[it % 2].h, 16)
        sem["msc"].n = qs0 + nrt * ncc * 16
        for it in range(nrt * ncc):
            qsem[it % 2].n += 16

        @blk.scalar
        def _(s):
            for it in range(nrt * ncc):
                s.wait_ge(sem["msc"].h, qs0 + (it + 1) * 16)
                s.wait_ge(sem["dve"].h, dve0 + (8 * (it - 1) + 7 if it >= 1 else 0))
                wb = sc["wbuf"][it % 2][:, 0:1024]
                s.activation(sc["tA"][:], wb, AF.Abs)
                s.activation(sc["tB"][:], wb, AF.Sign).then_inc(sem["act"].h, 1)
                s.wait_ge(sem["act"].h, act0 + it + 1)
        sem["act"].n = act0 + nrt * ncc

        @blk.vector
        def _(v):
            for it in range(nrt * ncc):
                v.wait_ge(sem["act"].h, act0 + it + 1)
                v.wait_ge(qsem[it % 2].h, qo0[it % 2] + (it // 2) * 16)

                def vf(instr):
                    sem["dve"].inc(instr)
                    v.wait_ge(sem["dve"].h, sem["dve"].n)

                base = sem["dve"].n
                whq = sc["wbuf"][it % 2][:, 1024:2048]
                vf(v.tensor_reduce(sc["smallf"][:, 0:16],
                                   sc["tA"][:].rearrange("p (g k) -> p g k", k=64),
                                   axis=mybir.AxisListType.X, op=OP.add))
                vf(v.tensor_scalar(qsc[:], sc["smallf"][:, 0:16], 1.0 / 64.0, None, OP.mult))
                vf(v.tensor_scalar(qsc[:], qsc[:], 1e-8, None, OP.max))
                vf(v.tensor_copy(qscf[:, 0:16], qsc[:]))
                vf(v.tensor_scalar(qscf[:, 16:32], qscf[:, 0:16], 0.5, None, OP.mult))
                cp = None
                for gg in range(16):
                    cp = v.tensor_scalar(sc["tC"][:, 64 * gg:64 * (gg + 1)],
                                         sc["tA"][:, 64 * gg:64 * (gg + 1)],
                                         qscf[:, 16 + gg:17 + gg], None, OP.is_gt)
                vf(cp)
                vf(v.tensor_tensor(sc["tC"][:], sc["tC"][:], sc["tB"][:], OP.mult))
                cp = None
                for gg in range(16):
                    cp = v.tensor_scalar(whq[:, 64 * gg:64 * (gg + 1)],
                                         sc["tC"][:, 64 * gg:64 * (gg + 1)],
                                         qscf[:, gg:gg + 1], None, OP.mult)
                assert sem["dve"].n == base + 7
                vf(cp)
                assert sem["dve"].n == base + 8
        assert sem["dve"].n == dve0 + nrt * ncc * 8
    return



def build_ffn_l0(ctx, sc, WGU_Q, WPR_Q, XH, HHBM, tag):
    nc = ctx.nc
    sem = ctx.sem
    ps_a, ps_b, ps_c, ps_d = sc["ps_a"], sc["ps_b"], sc["ps_c"], sc["ps_d"]
    build_ln(ctx, sc, tag, to_hbm=XH)
    build_bounce(ctx, sc, XH, tag)

    wsem = [sem["w0"], sem["w1"]]
    vrsem = [sem["vr0"], sem["vr1"]]
    NPAIR = 64

    pe0 = sem["pe"].n
    act0 = sem["act"].n
    dve0 = sem["dve"].n
    w0n = [wsem[0].n, wsem[1].n]
    vr0n = [vrsem[0].n, vrsem[1].n]
    with nc.Block(f"f1_{tag}") as blk:
        @blk.sync
        def _(sy):
            wait(sy, sem["p0"], sem["q0"], sem["q1"])
            for i in range(NPAIR):
                if i >= 2:
                    sy.wait_ge(sem["pe"].h, pe0 + (i - 1) * 2)
                for f in range(NF):
                    sy.dma_start(
                        sc["wbuf"][i % 2][:, 256 * f:256 * f + 128],
                        WGU_Q[128 * i:128 * (i + 1), 128 * f:128 * (f + 1)],
                        transpose=True).then_inc(wsem[i % 2].h, 16)
                    sy.dma_start(
                        sc["wbuf"][i % 2][:, 256 * f + 128:256 * f + 256],
                        WGU_Q[128 * (64 + i):128 * (65 + i), 128 * f:128 * (f + 1)],
                        transpose=True).then_inc(wsem[i % 2].h, 16)

        @blk.scalar
        def _(s):
            wait(s, sem["p0"], sem["q0"], sem["q1"])
            for i in range(NPAIR):
                s.wait_ge(sem["pe"].h, pe0 + 2 * i + 1)
                if i >= 1:
                    s.wait_ge(sem["dve"].h, dve0 + 2 * (i - 1) + 1)
                s.activation(sc["tA"][:], ps_a[:], AF.Silu)
                s.copy(sc["tB"][:], ps_b[:]).then_inc(sem["act"].h, 1)
                s.wait_ge(sem["act"].h, act0 + i + 1)
        sem["act"].n = act0 + NPAIR
        for i in range(NPAIR):
            wsem[i % 2].n += 32 * 16

        @blk.tensor
        def _(t):
            wait(t, sem["xt"])
            for i in range(NPAIR):
                t.wait_ge(wsem[i % 2].h, w0n[i % 2] + (i // 2 + 1) * 32 * 16)
                if i >= 1:
                    t.wait_ge(sem["act"].h, act0 + i)
                mm = None
                for f in range(NF):
                    mm512(t, ps_a, sc["wbuf"][i % 2][:, 256 * f:256 * f + 128],
                          sc["xhatT"][:, TLOC * f:TLOC * (f + 1)],
                          start=(f == 0), stop=(f == NF - 1))
                    mm = mm512(t, ps_b, sc["wbuf"][i % 2][:, 256 * f + 128:256 * f + 256],
                               sc["xhatT"][:, TLOC * f:TLOC * (f + 1)],
                               start=(f == 0), stop=(f == NF - 1))
                mm.then_inc(sem["pe"].h, 1)
                t.wait_ge(sem["pe"].h, pe0 + 2 * i + 1)
                t.wait_ge(sem["dve"].h, dve0 + 2 * i + 2)
                mm = mm512(t, ps_c[0:1, :], sc["ones_col_bf"][:], sc["tC"][:],
                           start=(i == 0), stop=(i == NPAIR - 1))
                mm.then_inc(sem["pe"].h, 1)
                t.wait_ge(sem["pe"].h, pe0 + 2 * i + 2)
        sem["pe"].n = pe0 + NPAIR * 2

        @blk.vector
        def _(v):
            for i in range(NPAIR):
                v.wait_ge(sem["act"].h, act0 + i + 1)
                u = i // 2
                if u >= 1:
                    v.wait_ge(vrsem[i % 2].h, vr0n[i % 2] + u * 16)
                ht = sc["pT"][i % 2]
                sem["dve"].inc(v.tensor_tensor(ht[:], sc["tA"][:], sc["tB"][:], OP.mult))
                v.wait_ge(sem["dve"].h, sem["dve"].n)
                sem["dve"].inc(v.tensor_tensor(sc["tC"][:], ht[:], ht[:], OP.mult))
                v.wait_ge(sem["dve"].h, sem["dve"].n)
        assert sem["dve"].n == dve0 + NPAIR * 2

        @blk.gpsimd
        def _(g):
            for i in range(NPAIR):
                g.wait_ge(sem["dve"].h, dve0 + 2 * i + 1)
                g.dma_start(HHBM[128 * i:128 * (i + 1), :],
                            sc["pT"][i % 2][:]).then_inc(vrsem[i % 2].h, 16)
        for i in range(NPAIR):
            vrsem[i % 2].n += 16

    EPS32 = float(np.finfo(np.float32).eps)
    with nc.Block(f"f2_{tag}") as blk:
        @blk.vector
        def _(v):
            wait(v, sem["pe"], sem["dve"])
            sem["dve"].incw(v, v.tensor_scalar(sc["rcp"][0:1, :], ps_c[0:1, :],
                            1.0 / 8192.0, EPS32, OP.mult, OP.add))

        @blk.tensor
        def _(t):
            wait(t, sem["dve"])
            mm = None
            pslist = [ps_a, ps_b, ps_c, ps_d]
            for tb in range(NT):
                mm = t.matmul(pslist[tb // 2][:, 512 * (tb % 2):512 * (tb % 2) + 1],
                              sc["rcp"][0:1, 128 * tb:128 * (tb + 1)],
                              sc["ones_row_f"][0:1, 0:1], start=True, stop=True)
            sem["pe"].incw(t, mm)

        @blk.scalar
        def _(s):
            wait(s, sem["pe"])
            pslist = [ps_a, ps_b, ps_c, ps_d]
            ins = None
            for pi in range(4):
                ins = s.sqrt(sc["smallf"][:, 32 + 2 * pi:34 + 2 * pi],
                             pslist[pi][:].rearrange("p (b c) -> p b c", b=2)[:, :, 0])
            sem["act"].incw(s, ins)

        @blk.vector
        def _(v2):
            wait(v2, sem["act"])
            sem["dve"].incw(v2, v2.reciprocal(sc["smallf"][:, 40:48], sc["smallf"][:, 32:40]))

    pe0 = sem["pe"].n
    act0 = sem["act"].n
    dve0 = sem["dve"].n
    w0n = [wsem[0].n, wsem[1].n]
    vr0n = [vrsem[0].n, vrsem[1].n]
    NH2 = 64

    def psreg(tb):
        base = [ps_a, ps_b, ps_c, ps_d][tb // 2]
        return base[:, 512 * (tb % 2):512 * (tb % 2 + 1)]

    with nc.Block(f"f3_{tag}") as blk:
        @blk.gpsimd
        def _(g):
            wait(g, vrsem[0], vrsem[1])
            for ocg in range(4):
                for hc in range(NH2):
                    it = ocg * NH2 + hc
                    u = it // 2
                    if u >= 1:
                        g.wait_ge(sem["pe"].h, pe0 + (it - 1))
                    g.dma_start(sc["pT"][it % 2][:], HHBM[128 * hc:128 * (hc + 1), :]
                                ).then_inc(vrsem[it % 2].h, 16)
        for it in range(4 * NH2):
            vrsem[it % 2].n += 16

        @blk.sync
        def _(sy):
            for ocg in range(4):
                for hc in range(NH2):
                    it = ocg * NH2 + hc
                    if it >= 2:
                        sy.wait_ge(sem["pe"].h, pe0 + (it - 1))
                    sy.dma_start(sc["wbuf"][it % 2][:, 0:256],
                                 WPR_Q[512 * ocg:512 * ocg + 256,
                                       128 * hc:128 * (hc + 1)],
                                 transpose=True).then_inc(wsem[it % 2].h, 16)
                    sy.dma_start(sc["wbuf"][it % 2][:, 256:512],
                                 WPR_Q[512 * ocg + 256:512 * (ocg + 1),
                                       128 * hc:128 * (hc + 1)],
                                 transpose=True).then_inc(wsem[it % 2].h, 16)
        for it in range(4 * NH2):
            wsem[it % 2].n += 32

        @blk.tensor
        def _(t):
            for ocg in range(4):
                for hc in range(NH2):
                    it = ocg * NH2 + hc
                    t.wait_ge(wsem[it % 2].h, w0n[it % 2] + (it // 2 + 1) * 32)
                    t.wait_ge(vrsem[it % 2].h, vr0n[it % 2] + (it // 2 + 1) * 16)
                    if hc == 0 and ocg >= 1:
                        t.wait_ge(sem["dve"].h, dve0 + ocg * 8)
                    mm = None
                    for tb in range(NT):
                        mm = t.matmul(psreg(tb),
                                      sc["pT"][it % 2][:, 128 * tb:128 * (tb + 1)],
                                      sc["wbuf"][it % 2][:, 0:512],
                                      start=(hc == 0), stop=(hc == NH2 - 1))
                    mm.then_inc(sem["pe"].h, 1)
                    t.wait_ge(sem["pe"].h, pe0 + it + 1)
        sem["pe"].n = pe0 + 4 * NH2

        @blk.vector
        def _(v):
            for ocg in range(4):
                v.wait_ge(sem["pe"].h, pe0 + (ocg + 1) * NH2)
                for tb in range(NT):
                    xt = sc["x"][tb]
                    sem["dve"].inc(v.scalar_tensor_tensor(
                        xt[:, 512 * ocg:512 * (ocg + 1)], psreg(tb),
                        sc["smallf"][:, 40 + tb:41 + tb],
                        xt[:, 512 * ocg:512 * (ocg + 1)], OP.mult, OP.add))
                    v.wait_ge(sem["dve"].h, sem["dve"].n)
        assert sem["dve"].n == dve0 + 4 * 8
    return



def build_attn_phase_gqa(ctx, sc, CDST, phase, tag, rowsz):
    nc = ctx.nc
    sem = ctx.sem
    ps_a, ps_b, ps_c, ps_d = sc["ps_a"], sc["ps_b"], sc["ps_c"], sc["ps_d"]
    NHP = 8
    AH = 34
    pe0 = sem["pe"].n
    act0 = sem["act"].n
    dve0 = sem["dve"].n
    kvsem = [sem["kv0"], sem["kv1"]]
    kv0n = kvsem[phase % 2].n
    escale = 1.0 / math.sqrt(64.0)

    def pe_s(i, kc):
        return pe0 + 33 * i + (1 if kc == 0 else 2 * kc)

    def pe_cd(i, kc):
        return pe0 + 33 * i + (2 * kc + 3 if kc <= 14 else 32)

    kb = sc["kbuf"][phase % 2]
    vb = sc["vbuf"][phase % 2]

    with nc.Block(f"ag_{tag}{phase}") as blk:
        @blk.gpsimd
        def _(g):
            wait(g, sem["cc"])
            if phase >= 2:
                g.wait_ge(sem["pe"].h, pe0)
            for r in range(2):
                g.dma_start(kb[:, 1024 * r:1024 * (r + 1)],
                            CDST[rowsz * r + 128 * phase:rowsz * r + 128 * (phase + 1), 0:1024]
                            ).then_inc(kvsem[phase % 2].h, 16)
                g.dma_start(
                    vb[:, 1024 * r:1024 * (r + 1)].rearrange("i (kc j) -> i kc j", kc=8),
                    CDST[rowsz * r + 512:rowsz * r + 1536, 128 * phase:128 * (phase + 1)]
                    .rearrange("(kc i) j -> i kc j", kc=8)).then_inc(kvsem[phase % 2].h, 16)
        kvsem[phase % 2].n += 64

        @blk.tensor
        def _(t):
            t.wait_ge(kvsem[phase % 2].h, kv0n + 64)
            for i in range(NHP):
                kvh = (i // 4) * 64
                if i >= 1:
                    t.wait_ge(sem["act"].h, act0 + 18 * i)
                    t.wait_ge(sem["dve"].h, dve0 + AH * i)
                for kc in range(NKC):
                    ps = ps_a if kc % 2 == 0 else ps_b
                    if kc >= 2:
                        t.wait_ge(sem["act"].h, act0 + 18 * i + (kc - 1))
                    mm = mm512(t, ps, kb[:, 128 * kc:128 * (kc + 1)],
                               sc["qbuf"][:, TLOC * i:TLOC * (i + 1)],
                               start=True, stop=True)
                    mm.then_inc(sem["pe"].h, 1)
                    t.wait_ge(sem["pe"].h, pe_s(i, kc))
                    if kc >= 1:
                        kp = kc - 1
                        t.wait_ge(sem["dve"].h, dve0 + AH * i + 2 * (kp + 1))
                        pt = sc["pT"][kp % 2]
                        mm512(t, ps_c[0:64, :], vb[:, 128 * kp + kvh:128 * kp + kvh + 64],
                              pt[:], start=(kp == 0), stop=False)
                        mm = mm512(t, ps_d[0:1, :], sc["ones_col_bf"][:], pt[:],
                                   start=(kp == 0), stop=False)
                        mm.then_inc(sem["pe"].h, 1)
                        t.wait_ge(sem["pe"].h, pe_cd(i, kp))
                kp = NKC - 1
                t.wait_ge(sem["dve"].h, dve0 + AH * i + 2 * (kp + 1))
                pt = sc["pT"][kp % 2]
                mm512(t, ps_c[0:64, :], vb[:, 128 * kp + kvh:128 * kp + kvh + 64],
                      pt[:], start=False, stop=True)
                mm = mm512(t, ps_d[0:1, :], sc["ones_col_bf"][:], pt[:],
                           start=False, stop=True)
                mm.then_inc(sem["pe"].h, 1)
                t.wait_ge(sem["pe"].h, pe_cd(i, 15))
                t.wait_ge(sem["act"].h, act0 + 18 * i + 16)
                t.wait_ge(sem["dve"].h, dve0 + AH * i + 33)
                mm = mm512(t, ps_a[0:64, :], sc["ones_row_f"][0:1, 0:64],
                           sc["rcp"][0:1, :], start=True, stop=True)
                mm.then_inc(sem["pe"].h, 1)
                t.wait_ge(sem["pe"].h, pe0 + 33 * i + 33)
        sem["pe"].n = pe0 + NHP * 33

        @blk.scalar
        def _(s):
            for i in range(NHP):
                hb = i
                for kc in range(NKC):
                    ps = ps_a if kc % 2 == 0 else ps_b
                    s.wait_ge(sem["pe"].h, pe_s(i, kc))
                    s.activation(sc["pT"][kc % 2][:], ps[:], AF.Exp,
                                 scale=escale,
                                 bias=sc["biasmat"][:, hb * NKC + kc:hb * NKC + kc + 1]
                                 ).then_inc(sem["act"].h, 1)
                    s.wait_ge(sem["act"].h, act0 + 18 * i + kc + 1)
                s.wait_ge(sem["pe"].h, pe_cd(i, 15))
                s.copy(sc["tD"][0:64, :], ps_c[0:64, :]).then_inc(sem["act"].h, 1)
                s.wait_ge(sem["act"].h, act0 + 18 * i + 17)
                s.wait_ge(sem["dve"].h, dve0 + AH * i + 34)
                qt, qr = i // 2, (i % 2) * 64
                s.copy(sc["ctxh"][qr:qr + 64, TLOC * qt:TLOC * (qt + 1)],
                       sc["tC"][0:64, :]).then_inc(sem["act"].h, 1)
                s.wait_ge(sem["act"].h, act0 + 18 * i + 18)
        sem["act"].n = act0 + NHP * 18

        @blk.vector
        def _(v):
            for i in range(NHP):
                for kc in range(NKC):
                    v.wait_ge(sem["act"].h, act0 + 18 * i + kc + 1)
                    pt = sc["pT"][kc % 2]
                    if 1 <= kc <= 8:
                        sem["dve"].inc(v.tensor_scalar(pt[:, 0:128 * kc], pt[:, 0:128 * kc],
                                       sc["scr"][:, kc:kc + 1], None, OP.mult))
                        v.wait_ge(sem["dve"].h, sem["dve"].n)
                    elif kc >= 9:
                        w = 128 * (kc - 8)
                        sem["dve"].inc(v.tensor_scalar(pt[:, 0:w], pt[:, 0:w],
                                       sc["scr2"][:, kc:kc + 1], None, OP.mult))
                        v.wait_ge(sem["dve"].h, sem["dve"].n)
                    else:
                        sem["dve"].inc(v.nop())
                        v.wait_ge(sem["dve"].h, sem["dve"].n)
                    off = (kc % 8) * 128
                    sem["dve"].inc(v.tensor_tensor(pt[:, off:off + 128], pt[:, off:off + 128],
                                   sc["tri_sb"][:, 128 * kc:128 * (kc + 1)], OP.mult))
                    v.wait_ge(sem["dve"].h, sem["dve"].n)
                v.wait_ge(sem["pe"].h, pe_cd(i, 15))
                sem["dve"].inc(v.reciprocal(sc["rcp"][0:1, :], ps_d[0:1, :]))
                v.wait_ge(sem["dve"].h, sem["dve"].n)
                v.wait_ge(sem["act"].h, act0 + 18 * i + 17)
                v.wait_ge(sem["pe"].h, pe0 + 33 * i + 33)
                sem["dve"].inc(v.tensor_tensor(sc["tC"][0:64, :], sc["tD"][0:64, :],
                               ps_a[0:64, :], OP.mult))
                v.wait_ge(sem["dve"].h, sem["dve"].n)
        assert sem["dve"].n == dve0 + NHP * AH
    return



def build_attn_gqa(ctx, sc, WBF, XH, CSRC_t, CDST_t, tag):
    nc = ctx.nc
    sem = ctx.sem
    ROWSZ = 1537
    CSRC = CSRC_t.ap()
    CDST = CDST_t.ap()
    build_ln(ctx, sc, tag, to_hbm=XH)
    build_loadrope(ctx, sc, ctx.CT, "ropeG_c", "ropeG_s", tag)
    build_bounce(ctx, sc, XH, tag)

    def kdst(i_lin, h):
        return sc["pT"][i_lin % 2][:]

    def kdma(g, i_lin, h):
        g.dma_start(CSRC[128 * h:128 * (h + 1), 0:1024],
                    sc["pT"][i_lin % 2][:]).then_inc(ctx.sem["csrc"].h, 16)

    build_headproj(ctx, sc, WBF["k"], list(range(4)), kdst, 0,
                   sc["ropeC_t"][:], sc["ropeS_t"][:], tag + "K",
                   csrc_dma=kdma, war_gap=1, rope_mode="xor1")
    build_vstage(ctx, sc, WBF["v"], CSRC, tag, ndim=512, csrc_row0=512)
    build_gather(ctx, sc, CSRC_t, CDST_t, 4, tag, norm_row=1536)

    for phase in range(4):
        heads = list(range(4 * phase, 4 * phase + 4))

        def qdst(i_lin, h):
            return sc["pT"][i_lin % 2][:]

        def qpad(i_lin):
            kvh = 0 if i_lin < 2 else 64
            qa = sc["qbuf"][:, TLOC * (2 * i_lin):TLOC * (2 * i_lin + 1)]
            qb = sc["qbuf"][:, TLOC * (2 * i_lin + 1):TLOC * (2 * i_lin + 2)]
            return (qa, kvh), (qb, kvh)

        build_headproj(ctx, sc, WBF["q"], heads, qdst, 32 + 4 * phase,
                       sc["ropeC_t"][:], sc["ropeS_t"][:], f"{tag}Q{phase}",
                       rope_mode="xor1", qpad=qpad)
        build_bias(ctx, sc, CDST, phase, 8, 64, tag,
                   kn_of_head=[phase] * 8,
                   qn_of_head=[32 + 4 * phase + i // 2 for i in range(8)],
                   nrow0=1536, nrow1=1537 + 1536, col0=0)
        build_attn_phase_gqa(ctx, sc, CDST, phase, tag, 1537)
        build_outproj(ctx, sc, WBF["o"], phase, tag, nchunk=4)
    return



def build_moe(ctx, sc, W, XH, tag):
    nc = ctx.nc
    sem = ctx.sem
    ps_a, ps_b, ps_c, ps_d = sc["ps_a"], sc["ps_b"], sc["ps_c"], sc["ps_d"]
    build_ln(ctx, sc, tag, to_hbm=XH)
    build_bounce(ctx, sc, XH, tag)
    wsem = [sem["w0"], sem["w1"]]
    pslist = [ps_a, ps_b, ps_c, ps_d]

    def psb(tb, width):
        return pslist[tb // 2][:, 512 * (tb % 2):512 * (tb % 2) + width]

    pe0 = sem["pe"].n
    dve0 = sem["dve"].n
    act0 = sem["act"].n
    msc0 = sem["msc"].n
    gts, gtsb, gtmp, gtT = sc["gts"], sc["gtsb"], sc["gtmp"], sc["gtT"]
    with nc.Block(f"mg_{tag}") as blk:
        @blk.sync
        def _(sy):
            wait(sy, sem["q0"], sem["q1"], sem["p0"])
            sem["msb"].inc(sy.dma_start(
                sc["wbuf"][0][:, 0:128].rearrange("p (c e) -> p c e", e=8),
                W["gate"].rearrange("(c p) e -> p c e", p=128)), 16)

        @blk.tensor
        def _(t):
            wait(t, sem["msb"], sem["xt"], sem["act"], sem["dve"])
            for tb in range(NT):
                mm = None
                for f in range(NF):
                    mm = t.matmul(psb(tb, 8),
                                  sc["xhatT"][:, TLOC * f + 128 * tb:TLOC * f + 128 * (tb + 1)],
                                  sc["wbuf"][0][:, 8 * f:8 * f + 8],
                                  start=(f == 0), stop=(f == NF - 1))
                mm.then_inc(sem["pe"].h, 1)
                t.wait_ge(sem["pe"].h, pe0 + tb + 1)
            for tb in range(NT):
                t.wait_ge(sem["dve"].h, dve0 + 12 * (tb + 1))
                mm = t.matmul(psb(tb, 128)[0:8, :], gtsb[:, 8 * tb:8 * tb + 8],
                              sc["id128_bf"][:], start=True, stop=True)
                mm.then_inc(sem["pe"].h, 1)
                t.wait_ge(sem["pe"].h, pe0 + NT + tb + 1)
        sem["pe"].n = pe0 + 2 * NT

        @blk.vector
        def _(v):
            def vf(instr):
                sem["dve"].inc(instr)
                v.wait_ge(sem["dve"].h, sem["dve"].n)
            for tb in range(NT):
                v.wait_ge(sem["pe"].h, pe0 + tb + 1)
                gl = gts[:, 8 * tb:8 * tb + 8]
                sm = sc["smallf"]
                vf(v.tensor_copy(gl, psb(tb, 8)))
                vf(v.tensor_reduce(sm[:, 0:1], gl, axis=mybir.AxisListType.X, op=OP.max))
                vf(v.tensor_scalar(gtmp[:, 0:8], gl, sm[:, 0:1], None, OP.is_ge))
                vf(v.scalar_tensor_tensor(gtmp[:, 8:16], gtmp[:, 0:8], -1e30, gl,
                                          OP.mult, OP.add))
                vf(v.tensor_reduce(sm[:, 1:2], gtmp[:, 8:16], axis=mybir.AxisListType.X, op=OP.max))
                vf(v.tensor_tensor(sm[:, 2:3], sm[:, 0:1], sm[:, 1:2], OP.subtract))
                v.wait_ge(sem["act"].h, act0 + tb + 1)
                vf(v.tensor_scalar(sm[:, 4:5], sm[:, 3:4], -1.0, 1.0, OP.mult, OP.add))
                vf(v.tensor_scalar(gtmp[:, 16:24], gtmp[:, 8:16], sm[:, 1:2], None, OP.is_ge))
                vf(v.tensor_scalar(gtmp[:, 0:8], gtmp[:, 0:8], sm[:, 3:4], None, OP.mult))
                vf(v.tensor_scalar(gtmp[:, 16:24], gtmp[:, 16:24], sm[:, 4:5], None, OP.mult))
                vf(v.tensor_tensor(gl, gtmp[:, 0:8], gtmp[:, 16:24], OP.add))
                vf(v.tensor_copy(gtsb[:, 8 * tb:8 * tb + 8], gl))
        assert sem["dve"].n == dve0 + 12 * NT

        @blk.scalar
        def _(s):
            for tb in range(NT):
                s.wait_ge(sem["dve"].h, dve0 + 12 * tb + 6)
                s.activation(sc["smallf"][:, 3:4], sc["smallf"][:, 2:3],
                             AF.Sigmoid).then_inc(sem["act"].h, 1)
                s.wait_ge(sem["act"].h, act0 + tb + 1)
            for tb in range(NT):
                s.wait_ge(sem["pe"].h, pe0 + NT + tb + 1)
                s.copy(gtT[:, 128 * tb:128 * (tb + 1)],
                       psb(tb, 128)[0:8, :]).then_inc(sem["act"].h, 1)
                s.wait_ge(sem["act"].h, act0 + NT + tb + 1)
        sem["act"].n = act0 + 2 * NT

    def hstage(src_w, n_tiles, row0, bias_sb, bias_col0, dst_fn, stag):
        pe0 = sem["pe"].n
        act0 = sem["act"].n
        w0n = [wsem[0].n, wsem[1].n]
        with nc.Block(f"mh_{stag}") as blk:
            @blk.sync
            def _(sy):
                wait(sy, sem["q0"], sem["q1"])
                for i in range(n_tiles):
                    if i >= 2:
                        sy.wait_ge(sem["pe"].h, pe0 + i - 1)
                    for f in range(NF):
                        if f % 2 == 0:
                            sy.dma_start(
                                sc["wbuf"][i % 2][:, 128 * f:128 * (f + 1)],
                                src_w[128 * f:128 * (f + 1), row0 + 128 * i:row0 + 128 * (i + 1)]
                                ).then_inc(wsem[i % 2].h, 16)

            @blk.scalar
            def _(s):
                for i in range(n_tiles):
                    if i >= 2:
                        s.wait_ge(sem["pe"].h, pe0 + i - 1)
                    for f in range(NF):
                        if f % 2 == 1:
                            s.dma_start(
                                sc["wbuf"][i % 2][:, 128 * f:128 * (f + 1)],
                                src_w[128 * f:128 * (f + 1), row0 + 128 * i:row0 + 128 * (i + 1)]
                                ).then_inc(wsem[i % 2].h, 16)
                    s.wait_ge(sem["pe"].h, pe0 + i + 1)
                    ps = ps_a if i % 2 == 0 else ps_b
                    s.activation(dst_fn(i), ps[:], AF.Gelu,
                                 bias=bias_sb[:, bias_col0 + i:bias_col0 + i + 1]
                                 ).then_inc(sem["act"].h, 1)
                    s.wait_ge(sem["act"].h, act0 + i + 1)
            for i in range(n_tiles):
                wsem[i % 2].n += 16 * 16

            @blk.tensor
            def _(t):
                for i in range(n_tiles):
                    t.wait_ge(wsem[i % 2].h, w0n[i % 2] + (i // 2 + 1) * 16 * 16)
                    if i >= 2:
                        t.wait_ge(sem["act"].h, act0 + i - 1)
                    ps = ps_a if i % 2 == 0 else ps_b
                    mm = None
                    for f in range(NF):
                        mm = mm512(t, ps, sc["wbuf"][i % 2][:, 128 * f:128 * (f + 1)],
                                   sc["xhatT"][:, TLOC * f:TLOC * (f + 1)],
                                   start=(f == 0), stop=(f == NF - 1))
                    mm.then_inc(sem["pe"].h, 1)
                    t.wait_ge(sem["pe"].h, pe0 + i + 1)
            sem["pe"].n = pe0 + n_tiles
        sem["act"].n = act0 + n_tiles

    def ostage(src_w, nhc, wrow0, hsrc_fn, brow_src, evac_kind, e_idx, stag):
        pe0 = sem["pe"].n
        dve0 = sem["dve"].n
        msc0 = sem["msc"].n
        w0n = [wsem[0].n, wsem[1].n]
        per = nhc + (1 if brow_src is not None else 0)

        def mm_done(it):
            o2, h2 = divmod(it, nhc)
            return pe0 + o2 * per + h2 + 1

        with nc.Block(f"mo_{stag}") as blk:
            @blk.sync
            def _(sy):
                wait(sy, sem["q0"], sem["q1"])
                for ocg in range(4):
                    for hc in range(nhc):
                        it = ocg * nhc + hc
                        if it >= 2:
                            sy.wait_ge(sem["pe"].h, mm_done(it - 2))
                        sy.dma_start(sc["wbuf"][it % 2][:, 0:256],
                                     src_w[wrow0 + 128 * hc:wrow0 + 128 * (hc + 1),
                                           512 * ocg:512 * ocg + 256]
                                     ).then_inc(wsem[it % 2].h, 16)

            @blk.scalar
            def _(s):
                for ocg in range(4):
                    for hc in range(nhc):
                        it = ocg * nhc + hc
                        if it >= 2:
                            s.wait_ge(sem["pe"].h, mm_done(it - 2))
                        s.dma_start(sc["wbuf"][it % 2][:, 256:512],
                                    src_w[wrow0 + 128 * hc:wrow0 + 128 * (hc + 1),
                                          512 * ocg + 256:512 * (ocg + 1)]
                                    ).then_inc(wsem[it % 2].h, 16)
            for it in range(4 * nhc):
                wsem[it % 2].n += 32

            @blk.gpsimd
            def _(g):
                if brow_src is not None:
                    wait(g, sem["pe"])
                    for ocg in range(4):
                        if ocg >= 1:
                            g.wait_ge(sem["pe"].h, pe0 + ocg * per)
                        g.dma_start(sc["b2row"][0:1, 0:512],
                                    brow_src[0:1, 512 * ocg:512 * (ocg + 1)]
                                    ).then_inc(sem["msc"].h, 16)
            if brow_src is not None:
                sem["msc"].n += 4 * 16

            @blk.tensor
            def _(t):
                for ocg in range(4):
                    for hc in range(nhc):
                        it = ocg * nhc + hc
                        t.wait_ge(wsem[it % 2].h, w0n[it % 2] + (it // 2 + 1) * 32)
                        if hc == 0 and ocg >= 1:
                            t.wait_ge(sem["dve"].h, dve0 + ocg * 8)
                        mm = None
                        for tb in range(NT):
                            mm = t.matmul(psb(tb, 512), hsrc_fn(hc, tb),
                                          sc["wbuf"][it % 2][:, 0:512],
                                          start=(hc == 0), stop=(hc == nhc - 1 and brow_src is None))
                        mm.then_inc(sem["pe"].h, 1)
                        t.wait_ge(sem["pe"].h, mm_done(it))
                    if brow_src is not None:
                        t.wait_ge(sem["msc"].h, msc0 + (ocg + 1) * 16)
                        mm = None
                        for tb in range(NT):
                            mm = t.matmul(psb(tb, 512), sc["ones_row_bf"][:],
                                          sc["b2row"][0:1, 0:512], start=False, stop=True)
                        mm.then_inc(sem["pe"].h, 1)
                        t.wait_ge(sem["pe"].h, pe0 + (ocg + 1) * per)
            sem["pe"].n = pe0 + 4 * per

            @blk.vector
            def _(v):
                for ocg in range(4):
                    v.wait_ge(sem["pe"].h, pe0 + (ocg + 1) * per)
                    for tb in range(NT):
                        xt = sc["x"][tb]
                        if evac_kind == "add":
                            sem["dve"].inc(v.tensor_tensor(
                                xt[:, 512 * ocg:512 * (ocg + 1)],
                                xt[:, 512 * ocg:512 * (ocg + 1)], psb(tb, 512), OP.add))
                        else:
                            sem["dve"].inc(v.scalar_tensor_tensor(
                                xt[:, 512 * ocg:512 * (ocg + 1)], psb(tb, 512),
                                gts[:, 8 * tb + e_idx:8 * tb + e_idx + 1],
                                xt[:, 512 * ocg:512 * (ocg + 1)], OP.mult, OP.add))
                        v.wait_ge(sem["dve"].h, sem["dve"].n)
            assert sem["dve"].n == dve0 + 4 * 8

    with nc.Block(f"mb_{tag}") as blk:
        @blk.sync
        def _(sy):
            sem["msb"].inc(sy.dma_start(
                sc["b1sb"][:, 0:64].rearrange("p (e ht) -> p e ht", e=8),
                W["b1f"].rearrange("e (ht p) -> p e ht", p=128),
                allow_slow_non_contiguous=True), 16)
            sem["msb"].inc(sy.dma_start(
                sc["sb1sb"][:, 0:16].rearrange("p c -> p c"),
                W["sb1f"].rearrange("o (c p) -> p (o c)", p=128),
                allow_slow_non_contiguous=True), 16)
            wait(sy, sem["msb"])

    def shdst(i):
        if i < 8:
            return sc["ctxh"][:, TLOC * i:TLOC * (i + 1)]
        return sc["qbuf"][:, TLOC * (i - 8):TLOC * (i - 7)]

    hstage(W["sw1"], 16, 0, sc["sb1sb"], 0, shdst, tag + "sh")

    def sh_src(hc, tb):
        base = shdst(hc)
        return base[:, 128 * tb:128 * (tb + 1)]

    ostage(W["sw2"], 16, 0, sh_src, W["sb2"], "add", 0, tag + "so")

    def edst(i):
        return sc["qbuf"][:, TLOC * i:TLOC * (i + 1)]

    def e_src(hc, tb):
        return sc["qbuf"][:, TLOC * hc + 128 * tb:TLOC * hc + 128 * (tb + 1)]

    for e in range(8):
        hstage(W["w1"][2048 * e:2048 * (e + 1), :], 8, 0, sc["b1sb"], 8 * e,
               edst, f"{tag}e{e}h")
        ostage(W["w2"][1024 * e:1024 * (e + 1), :], 8, 0, e_src, None,
               "gate", e, f"{tag}e{e}o")

    pe0b = sem["pe"].n
    dve0b = sem["dve"].n
    msc0b = sem["msc"].n
    with nc.Block(f"mb2_{tag}") as blk:
        @blk.gpsimd
        def _(g):
            wait(g, sem["pe"])
            for ocg in range(4):
                if ocg >= 1:
                    g.wait_ge(sem["pe"].h, pe0b + ocg * NT)
                g.dma_start(sc["b2row"][:, :],
                            W["b2"][:, 512 * ocg:512 * (ocg + 1)]
                            ).then_inc(sem["msc"].h, 16)
        sem["msc"].n += 4 * 16

        @blk.tensor
        def _(t):
            wait(t, sem["act"], sem["dve"])
            for ocg in range(4):
                t.wait_ge(sem["msc"].h, msc0b + (ocg + 1) * 16)
                if ocg >= 1:
                    t.wait_ge(sem["dve"].h, dve0b + ocg * NT)
                for tb in range(NT):
                    mm = t.matmul(psb(tb, 512), sc["gtT"][:, 128 * tb:128 * (tb + 1)],
                                  sc["b2row"][:, :], start=True, stop=True)
                    mm.then_inc(sem["pe"].h, 1)
                    t.wait_ge(sem["pe"].h, pe0b + ocg * NT + tb + 1)
        sem["pe"].n = pe0b + 4 * NT

        @blk.vector
        def _(v):
            for ocg in range(4):
                for tb in range(NT):
                    v.wait_ge(sem["pe"].h, pe0b + ocg * NT + tb + 1)
                    xt = sc["x"][tb]
                    sem["dve"].inc(v.tensor_tensor(
                        xt[:, 512 * ocg:512 * (ocg + 1)],
                        xt[:, 512 * ocg:512 * (ocg + 1)], psb(tb, 512), OP.add))
                    v.wait_ge(sem["dve"].h, sem["dve"].n)
        assert sem["dve"].n == dve0b + 4 * NT
    return



def build_headproj_l1b(ctx, sc, WNAT, wcol0, heads, dst_fn, norm_col0,
                       bias_sb, bias_col0, tag, csrc_dma=None, war_gap=0):
    nc = ctx.nc
    sem = ctx.sem
    ps_a, ps_b, ps_c, ps_d = sc["ps_a"], sc["ps_b"], sc["ps_c"], sc["ps_d"]
    npair = len(heads) // 2
    pairs = [(heads[2 * i], heads[2 * i + 1]) for i in range(npair)]

    pe0 = sem["pe"].n
    act0 = sem["act"].n
    dve0 = sem["dve"].n
    csrc0 = sem["csrc"].n
    wsem = [sem["w0"], sem["w1"]]
    w0n = [wsem[0].n, wsem[1].n]
    PPP = 4 + 8
    PPH = 4
    DL = 14
    rcp = sc["rcp"]
    row2 = sc["row2"]

    def pe_h(i_lin, k):
        j, o = divmod(i_lin, 2)
        return pe0 + 12 * j + 2 + 5 * o + k

    with nc.Block(f"hb_{tag}") as blk:
        @blk.sync
        def _(sy):
            wait(sy, sem["q0"], sem["q1"], sem["p0"], sem["msd"])
            for j, (ha, hb) in enumerate(pairs):
                if j >= 2:
                    sy.wait_ge(sem["pe"].h, pe0 + (j - 1) * 12)
                for f in range(NF):
                    if f % 2 == 0:
                        sy.dma_start(
                            sc["wbuf"][j % 2][:, 256 * f:256 * (f + 1)],
                            WNAT[128 * f:128 * (f + 1),
                                 wcol0 + 256 * j:wcol0 + 256 * (j + 1)]
                            ).then_inc(wsem[j % 2].h, 16)

        @blk.scalar
        def _(s):
            wait(s, sem["q0"], sem["q1"], sem["p0"])
            for j, (ha, hb) in enumerate(pairs):
                if j >= 2:
                    s.wait_ge(sem["pe"].h, pe0 + (j - 1) * 12)
                for f in range(NF):
                    if f % 2 == 1:
                        s.dma_start(
                            sc["wbuf"][j % 2][:, 256 * f:256 * (f + 1)],
                            WNAT[128 * f:128 * (f + 1),
                                 wcol0 + 256 * j:wcol0 + 256 * (j + 1)]
                            ).then_inc(wsem[j % 2].h, 16)
                for o in range(2):
                    i_lin = 2 * j + o
                    s.wait_ge(sem["pe"].h, pe0 + 12 * j + o + 1)
                    if i_lin >= 1:
                        s.wait_ge(sem["dve"].h, dve0 + DL * (i_lin - 1) + 3)
                    if i_lin == 0:
                        wait(s, sem["msd"])
                    ps = ps_a if o == 0 else ps_b
                    h = (ha, hb)[o]
                    s.activation(sc["tA"][:], ps[:], AF.Identity,
                                 bias=bias_sb[:, bias_col0 + i_lin:bias_col0 + i_lin + 1]
                                 ).then_inc(sem["act"].h, 1)
                    s.wait_ge(sem["act"].h, act0 + 2 * i_lin + 1)
                    s.wait_ge(sem["dve"].h, dve0 + DL * i_lin + 8)
                    s.sqrt(row2[0:1, :], row2[0:1, :]).then_inc(sem["act"].h, 1)
                    s.wait_ge(sem["act"].h, act0 + 2 * i_lin + 2)
        sem["act"].n = act0 + npair * 4
        for j in range(npair):
            wsem[j % 2].n += 16 * 16

        @blk.tensor
        def _(t):
            wait(t, sem["xt"])
            for j, (ha, hb) in enumerate(pairs):
                t.wait_ge(wsem[j % 2].h, w0n[j % 2] + (j // 2 + 1) * 16 * 16)
                for o in range(2):
                    ps = ps_a if o == 0 else ps_b
                    if j >= 1:
                        t.wait_ge(sem["act"].h, act0 + 2 * (2 * (j - 1) + o) + 1)
                    mm = None
                    for f in range(NF):
                        mm = mm512(t, ps, sc["wbuf"][j % 2][:, 256 * f + 128 * o:256 * f + 128 * (o + 1)],
                                   sc["xhatT"][:, TLOC * f:TLOC * (f + 1)],
                                   start=(f == 0), stop=(f == NF - 1))
                    mm.then_inc(sem["pe"].h, 1)
                    t.wait_ge(sem["pe"].h, pe0 + 12 * j + o + 1)
                for o in range(2):
                    i_lin = 2 * j + o
                    t.wait_ge(sem["dve"].h, dve0 + DL * i_lin + 3)
                    mm = mm512(t, ps_c[0:1, :], sc["ones_col_bf"][:], sc["tB"][:],
                               start=True, stop=True)
                    mm.then_inc(sem["pe"].h, 1)
                    t.wait_ge(sem["pe"].h, pe_h(i_lin, 1))
                    t.wait_ge(sem["dve"].h, dve0 + DL * i_lin + 4)
                    mm = mm512(t, ps_d[0:1, :], sc["ones_col_bf"][:], sc["tC"][:],
                               start=True, stop=True)
                    mm.then_inc(sem["pe"].h, 1)
                    t.wait_ge(sem["pe"].h, pe_h(i_lin, 2))
                    t.wait_ge(sem["dve"].h, dve0 + DL * i_lin + 8)
                    mm = mm512(t, ps_c, sc["ones_row_f"][:], rcp[0:1, :],
                               start=True, stop=True)
                    mm.then_inc(sem["pe"].h, 1)
                    t.wait_ge(sem["pe"].h, pe_h(i_lin, 3))
                    t.wait_ge(sem["dve"].h, dve0 + DL * i_lin + 9)
                    mm = mm512(t, ps_d, sc["ones_row_f"][:], row2[0:1, :],
                               start=True, stop=True)
                    mm.then_inc(sem["pe"].h, 1)
                    t.wait_ge(sem["pe"].h, pe_h(i_lin, 4))
                    t.wait_ge(sem["dve"].h, dve0 + DL * i_lin + 13)
                    mm = mm512(t, ps_c[0:1, :], sc["ones_col_bf"][:], sc["tC"][:],
                               start=True, stop=True)
                    mm.then_inc(sem["pe"].h, 1)
                    t.wait_ge(sem["pe"].h, pe_h(i_lin, 5))
        sem["pe"].n = pe0 + npair * PPP

        @blk.vector
        def _(v):
            wait(v, sem["msd"])
            def vf(instr):
                sem["dve"].inc(instr)
                v.wait_ge(sem["dve"].h, sem["dve"].n)
            for j, (ha, hb) in enumerate(pairs):
                for o in range(2):
                    h = (ha, hb)[o]
                    i_lin = 2 * j + o
                    base = sem["dve"].n
                    v.wait_ge(sem["act"].h, act0 + 2 * i_lin + 1)
                    if csrc_dma is not None and i_lin >= 2:
                        v.wait_ge(sem["csrc"].h, csrc0 + i_lin * war_gap * 16)
                    dst = dst_fn(i_lin, h)
                    vf(v.stream_shuffle(sc["tB"][:], sc["tA"][:], XOR1_MASK))
                    v.tensor_tensor(sc["tA"][:], sc["tA"][:], sc["ropeC_t"][:], OP.mult)
                    vf(v.tensor_tensor(sc["tB"][:], sc["tB"][:], sc["ropeS_t"][:], OP.mult))
                    vf(v.tensor_tensor(sc["tB"][:], sc["tA"][:], sc["tB"][:], OP.add))
                    vf(v.tensor_tensor(sc["tC"][:], sc["tB"][:], sc["tB"][:], OP.mult))
                    v.wait_ge(sem["pe"].h, pe_h(i_lin, 1))
                    vf(v.tensor_scalar(rcp[0:1, :], ps_c[0:1, :], 1.0 / 128.0, None, OP.mult))
                    v.wait_ge(sem["pe"].h, pe_h(i_lin, 2))
                    vf(v.tensor_scalar(row2[0:1, :], ps_d[0:1, :], 1.0 / 128.0, None, OP.mult))
                    vf(v.tensor_tensor(sc["ps_c"][0:1, :], rcp[0:1, :], rcp[0:1, :], OP.mult))
                    vf(v.scalar_tensor_tensor(row2[0:1, :], row2[0:1, :], 1e-5,
                                              ps_c[0:1, :], OP.add, OP.subtract))
                    v.wait_ge(sem["act"].h, act0 + 2 * i_lin + 2)
                    vf(v.reciprocal(row2[0:1, :], row2[0:1, :]))
                    v.wait_ge(sem["pe"].h, pe_h(i_lin, 3))
                    vf(v.tensor_tensor(sc["tB"][:], sc["tB"][:], ps_c[:], OP.subtract))
                    v.wait_ge(sem["pe"].h, pe_h(i_lin, 4))
                    vf(v.tensor_tensor(sc["tB"][:], sc["tB"][:], ps_d[:], OP.mult))
                    vf(v.tensor_scalar(dst, sc["tB"][:], sc["lnhg"][:, 0:1],
                                       sc["lnhg"][:, 1:2], OP.mult, OP.add))
                    vf(v.tensor_tensor(sc["tC"][:], dst, dst, OP.mult))
                    v.wait_ge(sem["pe"].h, pe_h(i_lin, 5))
                    vf(v.tensor_reduce(sc["nrm"][0:1, norm_col0 + i_lin:norm_col0 + i_lin + 1],
                                       ps_c[0:1, 0:1024], axis=mybir.AxisListType.X,
                                       op=OP.max))
                    assert sem["dve"].n == base + DL
        assert sem["dve"].n == dve0 + DL * 2 * npair

        if csrc_dma is not None:
            @blk.gpsimd
            def _(g):
                for j in range(npair):
                    for o in range(2):
                        i_lin = 2 * j + o
                        h = pairs[j][o]
                        g.wait_ge(sem["dve"].h, dve0 + DL * i_lin + 12)
                        csrc_dma(g, i_lin, h)
            sem["csrc"].n = csrc0 + npair * 2 * war_gap * 16
    return



def build_attn_b1(ctx, sc, W, Wf, XH, XQ, XK, MROW, CSRC_t, CDST_t, tag):
    nc = ctx.nc
    sem = ctx.sem
    ps_a, ps_b, ps_c, ps_d = sc["ps_a"], sc["ps_b"], sc["ps_c"], sc["ps_d"]
    CSRC = CSRC_t.ap()
    CDST = CDST_t.ap()
    build_ln(ctx, sc, tag, to_hbm=XH)
    build_loadrope(ctx, sc, ctx.CT, "ropeB_c", "ropeB_s", tag)

    reps = [("lnqg", sc["kbuf"][0]), ("lnqb", sc["kbuf"][1]),
            ("lnkg", sc["vbuf"][0]), ("lnkb", sc["vbuf"][1])]
    pe0 = sem["pe"].n
    act0 = sem["act"].n
    msb0 = sem["msb"].n
    with nc.Block(f"bp_{tag}") as blk:
        @blk.sync
        def _(sy):
            wait(sy, sem["pe"], sem["act"], sem["dve"])
            for ri, (nm, dstt) in enumerate(reps):
                for hf in range(2):
                    it = 2 * ri + hf
                    if it >= 1:
                        sy.wait_ge(sem["pe"].h, pe0 + it)
                    sy.dma_start(sc["rcp"][0:1, :],
                                 Wf[nm][0:1, 1024 * hf:1024 * (hf + 1)]
                                 ).then_inc(sem["msb"].h, 16)
            sy.dma_start(sc["bqsb"][:, 0:16],
                         Wf["bq"].rearrange("o (c p) -> p (o c)", p=128),
                         allow_slow_non_contiguous=True).then_inc(sem["msd"].h, 16)
            sy.dma_start(sc["bksb"][:, 0:16],
                         Wf["bkv"][0:1, 0:2048].rearrange("o (c p) -> p (o c)", p=128),
                         allow_slow_non_contiguous=True).then_inc(sem["msd"].h, 16)
            sy.dma_start(sc["bvsb"][:, 0:16],
                         Wf["bkv"][0:1, 2048:4096].rearrange("o (c p) -> p (o c)", p=128),
                         allow_slow_non_contiguous=True).then_inc(sem["msd"].h, 16)
            sy.dma_start(sc["lnhg"][:, 0:1],
                         Wf["lnhg"].rearrange("o p -> p o"),
                         allow_slow_non_contiguous=True).then_inc(sem["msd"].h, 16)
            sy.dma_start(sc["lnhg"][:, 1:2],
                         Wf["lnhb"].rearrange("o p -> p o"),
                         allow_slow_non_contiguous=True).then_inc(sem["msd"].h, 16)
        sem["msb"].n = msb0 + 8 * 16
        sem["msd"].n += 5 * 16

        @blk.tensor
        def _(t):
            for it in range(8):
                t.wait_ge(sem["msb"].h, msb0 + (it + 1) * 16)
                if it >= 2:
                    t.wait_ge(sem["act"].h, act0 + it - 1)
                mm = mm512(t, ps_a if it % 2 == 0 else ps_b,
                           sc["ones_row_f"][:], sc["rcp"][0:1, :],
                           start=True, stop=True)
                mm.then_inc(sem["pe"].h, 1)
                t.wait_ge(sem["pe"].h, pe0 + it + 1)
        sem["pe"].n = pe0 + 8

        @blk.scalar
        def _(s):
            for it in range(8):
                ri, hf = divmod(it, 2)
                s.wait_ge(sem["pe"].h, pe0 + it + 1)
                dstt = reps[ri][1]
                s.copy(dstt[:, 1024 * hf:1024 * (hf + 1)],
                       (ps_a if it % 2 == 0 else ps_b)[:]).then_inc(sem["act"].h, 1)
                s.wait_ge(sem["act"].h, act0 + it + 1)
        sem["act"].n = act0 + 8

    pe0 = sem["pe"].n
    act0 = sem["act"].n
    dve0 = sem["dve"].n
    msc0 = sem["msc"].n
    xqs = [sem["q0"], sem["q1"]]
    xks = [sem["vr0"], sem["vr1"]]
    xq0 = [xqs[0].n, xqs[1].n]
    xk0 = [xks[0].n, xks[1].n]
    NVT = 27
    bn = sc["smallf"]
    with nc.Block(f"px_{tag}") as blk:
        @blk.gpsimd
        def _(g):
            wait(g, sem["xh"], sem["act"])
            for i in range(NT):
                if i >= 2:
                    g.wait_ge(sem["dve"].h, dve0 + (i - 1) * NVT)
                g.dma_start(sc["xhat"][:, D * (i % 2):D * (i % 2 + 1)],
                            XH[128 * i:128 * (i + 1), :]).then_inc(sem["msc"].h, 16)
                g.wait_ge(sem["dve"].h, dve0 + i * NVT + NVT)
                g.dma_start(XQ[128 * i:128 * (i + 1), :],
                            sc["ctxh"][:, 2048 * (i % 2):2048 * (i % 2 + 1)]
                            ).then_inc(xqs[i % 2].h, 16)
                g.dma_start(XK[128 * i:128 * (i + 1), :],
                            sc["ctxh"][:, 4096 + 2048 * (i % 2):4096 + 2048 * (i % 2 + 1)]
                            ).then_inc(xks[i % 2].h, 16)
        sem["msc"].n = msc0 + NT * 16
        for p in range(2):
            xqs[p].n = xq0[p] + (NT // 2) * 16
            xks[p].n = xk0[p] + (NT // 2) * 16

        @blk.vector
        def _(v):
            def vf(instr):
                sem["dve"].inc(instr)
                v.wait_ge(sem["dve"].h, sem["dve"].n)
            wait(v, sem["act"])
            for i in range(NT):
                base = sem["dve"].n
                v.wait_ge(sem["msc"].h, msc0 + (i + 1) * 16)
                if i >= 2:
                    p = i % 2
                    v.wait_ge(xqs[p].h, xq0[p] + (i // 2) * 16)
                    v.wait_ge(xks[p].h, xk0[p] + (i // 2) * 16)
                xt = sc["xhat"][:, D * (i % 2):D * (i % 2 + 1)]
                st = None
                for c4 in range(4):
                    st = v.bn_stats(bn[:, 6 * c4:6 * c4 + 6], xt[:, 512 * c4:512 * (c4 + 1)])
                vf(st)
                vf(v.bn_aggr(bn[:, 24:26], bn[:, 0:24]))
                vf(v.tensor_scalar(bn[:, 26:27], bn[:, 25:26], 1e-5, None, OP.add))
                vf(v.tensor_copy(bn[:, 30:31], bn[:, 26:27]))
                v.wait_ge(sem["act"].h, act0 + 2 * i + 1)
                vf(v.reciprocal(bn[:, 27:28], bn[:, 30:31]))
                vf(v.tensor_tensor(bn[:, 28:29], bn[:, 24:25], bn[:, 27:28], OP.mult))
                vf(v.tensor_scalar(bn[:, 28:29], bn[:, 28:29], -1.0, None, OP.mult))
                vf(v.tensor_tensor(bn[:, 31:32], bn[:, 24:25], bn[:, 24:25], OP.mult))
                vf(v.tensor_tensor(bn[:, 31:32], bn[:, 31:32], bn[:, 25:26], OP.add))
                vf(v.tensor_scalar(bn[:, 31:32], bn[:, 31:32], 2048.0, None, OP.mult))
                v.wait_ge(sem["act"].h, act0 + 2 * i + 2)
                vf(v.tensor_copy(sc["qsc"][:, i:i + 1], bn[:, 31:32]))
                for kind in range(2):
                    grep = (sc["kbuf"] if kind == 0 else sc["vbuf"])[0]
                    brep = (sc["kbuf"] if kind == 0 else sc["vbuf"])[1]
                    dst0 = 2048 * (i % 2) if kind == 0 else 4096 + 2048 * (i % 2)
                    for hf in range(2):
                        vf(v.tensor_scalar(sc["tA"][:], xt[:, 1024 * hf:1024 * (hf + 1)],
                                           bn[:, 27:28], bn[:, 28:29], OP.mult, OP.add))
                        vf(v.tensor_tensor(sc["tA"][:], sc["tA"][:],
                                           grep[:, 1024 * hf:1024 * (hf + 1)], OP.mult))
                        vf(v.tensor_tensor(sc["ctxh"][:, dst0 + 1024 * hf:dst0 + 1024 * (hf + 1)],
                                           sc["tA"][:], brep[:, 1024 * hf:1024 * (hf + 1)],
                                           OP.add))
                assert sem["dve"].n == base + 23, sem["dve"].n - base
                while sem["dve"].n < base + NVT:
                    sem["dve"].inc(v.nop())
                    v.wait_ge(sem["dve"].h, sem["dve"].n)

        @blk.scalar
        def _(s):
            for i in range(NT):
                s.wait_ge(sem["dve"].h, dve0 + i * NVT + 4)
                s.sqrt(bn[:, 30:31], bn[:, 30:31]).then_inc(sem["act"].h, 1)
                s.wait_ge(sem["act"].h, act0 + 2 * i + 1)
                s.wait_ge(sem["dve"].h, dve0 + i * NVT + 10)
                s.sqrt(bn[:, 31:32], bn[:, 31:32]).then_inc(sem["act"].h, 1)
                s.wait_ge(sem["act"].h, act0 + 2 * i + 2)
        sem["act"].n = act0 + 2 * NT

    pe0 = sem["pe"].n
    act0 = sem["act"].n
    dve0 = sem["dve"].n
    msc0 = sem["msc"].n
    with nc.Block(f"pm_{tag}") as blk:
        @blk.tensor
        def _(t):
            wait(t, sem["dve"], sem["act"])
            mm = t.matmul(ps_a[0:8, 0:128], sc["qsc"][:, 0:8], sc["id128_bf"][:],
                          start=True, stop=True)
            mm.then_inc(sem["pe"].h, 1)
            t.wait_ge(sem["pe"].h, pe0 + 1)
            t.wait_ge(sem["msc"].h, msc0 + 32)
            mm = mm512(t, ps_c, sc["ones_row_f"][:], sc["rcp"][0:1, :],
                       start=True, stop=True)
            mm.then_inc(sem["pe"].h, 1)
            t.wait_ge(sem["pe"].h, pe0 + 2)
        sem["pe"].n = pe0 + 2

        @blk.scalar
        def _(s):
            s.wait_ge(sem["pe"].h, pe0 + 1)
            s.copy(sc["b2row"][0:8, 0:128], ps_a[0:8, 0:128]).then_inc(sem["act"].h, 1)
            s.wait_ge(sem["act"].h, act0 + 1)
        sem["act"].n = act0 + 1

        @blk.gpsimd
        def _(g):
            g.wait_ge(sem["act"].h, act0 + 1)
            g.dma_start(MROW[:, :], sc["b2row"][0:8, 0:128]).then_inc(sem["msc"].h, 16)
            g.wait_ge(sem["msc"].h, msc0 + 16)
            g.dma_start(sc["rcp"][0:1, :],
                        MROW.rearrange("a b -> (a b)")).then_inc(sem["msc"].h, 16)
        sem["msc"].n = msc0 + 32

        @blk.vector
        def _(v):
            v.wait_ge(sem["pe"].h, pe0 + 2)
            wait(v, sem["tab"])
            sem["dve"].inc(v.tensor_tensor(sc["ropeC_t"][:], sc["ropeC_t"][:],
                                           ps_c[:], OP.mult))
            v.wait_ge(sem["dve"].h, sem["dve"].n)
            sem["dve"].inc(v.tensor_tensor(sc["ropeS_t"][:], sc["ropeS_t"][:],
                                           ps_c[:], OP.mult))
            v.wait_ge(sem["dve"].h, sem["dve"].n)

    with nc.Block(f"xk_{tag}") as blk:
        @blk.sync
        def _(sy):
            wait(sy, sem["vr0"], sem["vr1"], sem["dve"])
            for j in range(NF):
                sem["xt"].inc(sy.dma_start(sc["xhatT"][:, TLOC * j:TLOC * (j + 1)],
                                           XK[:, 128 * j:128 * (j + 1)], transpose=True), 16)

    def kdst(i_lin, h):
        return sc["pT"][i_lin % 2][:]

    def kdma(g, i_lin, h):
        g.dma_start(CSRC[128 * h:128 * (h + 1), 0:1024],
                    sc["pT"][i_lin % 2][:]).then_inc(ctx.sem["csrc"].h, 16)

    build_headproj_l1b(ctx, sc, W["wkv"], 0, list(range(16)), kdst, 0,
                       sc["bksb"], 0, tag + "K", csrc_dma=kdma, war_gap=1)
    build_vstage(ctx, sc, W["wkv"], CSRC, tag, natural=True, nat_col0=2048)
    build_gather(ctx, sc, CSRC_t, CDST_t, 16, tag)

    with nc.Block(f"xq_{tag}") as blk:
        @blk.sync
        def _(sy):
            wait(sy, sem["q0"], sem["q1"], sem["dve"], sem["pe"])
            for j in range(NF):
                sem["xt"].inc(sy.dma_start(sc["xhatT"][:, TLOC * j:TLOC * (j + 1)],
                                           XQ[:, 128 * j:128 * (j + 1)], transpose=True), 16)

    for phase in range(2):
        heads = list(range(8 * phase, 8 * phase + 8))

        def qdst(i_lin, h):
            return sc["qbuf"][:, TLOC * i_lin:TLOC * (i_lin + 1)]

        build_headproj_l1b(ctx, sc, W["wq"], 1024 * phase, heads, qdst,
                           32 + 8 * phase, sc["bqsb"], 8 * phase, f"{tag}Q{phase}")
        build_bias(ctx, sc, CDST, phase, 8, 128, tag, bcol_key="bcolr0")
        build_attn_phase(ctx, sc, CDST, phase, heads, 128, tag,
                         bv_sb=sc["bvsb"], bv_col0=0, causal=False)
        build_outproj_nat(ctx, sc, W["wo"], phase, tag)

    pe0 = sem["pe"].n
    dve0 = sem["dve"].n
    msc0 = sem["msc"].n
    with nc.Block(f"bo_{tag}") as blk:
        @blk.gpsimd
        def _(g):
            wait(g, sem["pe"])
            for ocg in range(4):
                if ocg >= 1:
                    g.wait_ge(sem["pe"].h, pe0 + ocg)
                g.dma_start(sc["b2row"][0:1, 0:512],
                            W["bo"][0:1, 512 * ocg:512 * (ocg + 1)]).then_inc(sem["msc"].h, 16)
        sem["msc"].n = msc0 + 4 * 16

        @blk.tensor
        def _(t):
            wait(t, sem["dve"], sem["act"])
            for ocg in range(4):
                t.wait_ge(sem["msc"].h, msc0 + (ocg + 1) * 16)
                if ocg >= 1:
                    t.wait_ge(sem["dve"].h, dve0 + ocg * NT)
                mm = mm512(t, ps_a[:, 0:512], sc["ones_row_bf"][:],
                           sc["b2row"][0:1, 0:512], start=True, stop=True, width=512)
                mm.then_inc(sem["pe"].h, 1)
                t.wait_ge(sem["pe"].h, pe0 + ocg + 1)
        sem["pe"].n = pe0 + 4

        @blk.vector
        def _(v):
            for ocg in range(4):
                v.wait_ge(sem["pe"].h, pe0 + ocg + 1)
                for tb in range(NT):
                    xt = sc["x"][tb]
                    sem["dve"].inc(v.tensor_tensor(
                        xt[:, 512 * ocg:512 * (ocg + 1)],
                        xt[:, 512 * ocg:512 * (ocg + 1)], ps_a[:, 0:512], OP.add))
                    v.wait_ge(sem["dve"].h, sem["dve"].n)
        assert sem["dve"].n == dve0 + 4 * NT
    return


def build_outproj_nat(ctx, sc, WO, phase, tag):
    def rhs_load(eng, oc, f, dest):
        dglob = phase * 8 + f
        return eng.dma_start(dest,
                             WO[128 * dglob:128 * (dglob + 1), 256 * oc:256 * (oc + 1)])

    def lhs(f, tb):
        return sc["ctxh"][:, TLOC * f + 128 * tb:TLOC * f + 128 * (tb + 1)]

    def evac(v, ps_ap, tb, oc):
        xt = sc["x"][tb]
        return v.tensor_tensor(xt[:, 256 * oc:256 * (oc + 1)],
                               xt[:, 256 * oc:256 * (oc + 1)], ps_ap, OP.add)

    build_tokmaj_mm(ctx, sc, lhs, 8, rhs_load, 8, 256, evac,
                    tag + "on" + str(phase), evac_eng="vector")
    return


# version 04 (final)
# speedup vs baseline: 1.3251x; 1.0909x over previous
import math
import os
import contextlib
import numpy as np
import ml_dtypes
import concourse.bass as bass
import concourse.mybir as mybir
from concourse.bass_utils import run_bass_kernel_spmd
from concourse.alu_op_type import AluOpType as OP

F32 = mybir.dt.float32
BF16 = mybir.dt.bfloat16
AF = mybir.ActivationFunctionType

B, S, D = 4, 2048, 2048
NCORE = 8
TLOC = 1024
NT = TLOC // 128
NF = D // 128
NKC = S // 128
LN_EPS = 1e-5
NEG_BIG = -80.0
BF16NP = ml_dtypes.bfloat16

STAGES = int(os.environ.get("KSTAGES", "99"))



def host_constants(h):
    c = {}
    tpos = np.arange(TLOC, dtype=np.float64) + 1024.0 * h

    tri = np.ones((NKC, 128, 128), dtype=np.float32)
    lt = (np.arange(128)[:, None] <= np.arange(128)[None, :]).astype(np.float32)
    for kc in range(8 * h, 8 * h + 8):
        tri[kc] = lt
    c["tri"] = np.ascontiguousarray(tri.reshape(NKC * 128, 128)).astype(BF16NP)

    bcol = np.zeros((1, NKC), dtype=np.float32)
    for kc in range(NKC):
        if kc * 128 > (1024 * h + TLOC - 1):
            bcol[0, kc] = NEG_BIG
    c["bcol"] = bcol

    sc = np.ones((1, NKC), dtype=np.float32)
    if h == 0:
        for kc in range(1, 9):
            sc[0, kc] = 0.0
    c["sc"] = sc
    sc2 = np.ones((1, NKC), dtype=np.float32)
    if h == 1:
        for kc in range(9, NKC):
            sc2[0, kc] = 0.0
    c["sc2"] = sc2

    j = np.arange(64, dtype=np.float64)
    theta = 1.0 / (10000.0 ** (j / 64.0))
    ang = tpos[None, :] * theta[:, None]
    ca, sa = np.cos(ang), np.sin(ang)
    c["ropeA_c"] = np.concatenate([ca, ca], 0).astype(np.float32)
    c["ropeA_s"] = np.concatenate([-sa, sa], 0).astype(np.float32)

    jg = np.arange(32, dtype=np.float64)
    thg = 1.0 / (10000.0 ** (2.0 * jg / 64.0))
    angg = tpos[None, :] * thg[:, None]
    cg, sg = np.cos(angg), np.sin(angg)
    C2 = np.repeat(cg, 2, axis=0)
    S2 = np.empty((64, TLOC))
    S2[0::2] = -sg
    S2[1::2] = sg
    c["ropeG_c"] = np.concatenate([C2, C2], 0).astype(np.float32)
    c["ropeG_s"] = np.concatenate([S2, S2], 0).astype(np.float32)

    u = np.linspace(0.0, 1.0, 64)
    fr = (8000.0 / 200.0) ** u * (200.0 / 1000.0)
    freq = 200.0 * fr / 1000.0
    angb = tpos[None, :] * freq[:, None]
    cb, sb = np.cos(angb), np.sin(angb)
    qsc4 = 128.0 ** (-0.25)
    c["ropeB_c"] = (np.repeat(cb, 2, axis=0) * qsc4).astype(np.float32)
    SB = np.empty((128, TLOC))
    SB[0::2] = -sb
    SB[1::2] = sb
    c["ropeB_s"] = (SB * qsc4).astype(np.float32)
    c["id128"] = np.eye(128, dtype=np.float32)
    return c


CONST_SPECS = [
    ("tri", [NKC * 128, 128], BF16),
    ("bcol", [1, NKC], F32),
    ("sc", [1, NKC], F32),
    ("sc2", [1, NKC], F32),
    ("ropeA_c", [128, TLOC], F32),
    ("ropeA_s", [128, TLOC], F32),
    ("ropeG_c", [128, TLOC], F32),
    ("ropeG_s", [128, TLOC], F32),
    ("ropeB_c", [128, TLOC], F32),
    ("ropeB_s", [128, TLOC], F32),
    ("id128", [128, 128], F32),
]



class Sem:
    __slots__ = ("h", "n")
    def __init__(self, stack, nc, name):
        self.h = stack.enter_context(nc.semaphore(name))
        self.n = 0
    def inc(self, instr, k=1):
        instr.then_inc(self.h, k)
        self.n += k
        return instr
    def incw(self, eng, instr, k=1):
        instr.then_inc(self.h, k)
        self.n += k
        eng.wait_ge(self.h, self.n)
        return instr


class Ctx:
    def __init__(self, nc, stack):
        self.nc = nc
        self.stack = stack
        names = ["pe", "act", "dve", "cc", "xin", "tab", "p0", "xh", "xt",
                 "w0", "w1", "kv0", "kv1", "csrc", "vr0", "vr1", "vr2", "vr3",
                 "msc", "msb", "msd", "out", "q0", "q1"]
        self.sem = {n: Sem(stack, nc, "s_" + n) for n in names}

    def sb(self, name, shape, dtype):
        return self.stack.enter_context(self.nc.sbuf_tensor(name, shape, dtype))

    def ps(self, name, shape, dtype=F32):
        return self.stack.enter_context(self.nc.psum_tensor(name, shape, dtype))


XOR1_MASK = [i ^ 1 for i in range(32)]


def wait(eng, *sems):
    for s in sems:
        if s.n > 0:
            eng.wait_ge(s.h, s.n)


def mm512(t, ps_ap, lhsT, rhs_ap, start, stop, width=1024):
    mm = None
    for hw in range(0, width, 512):
        w = min(512, width - hw)
        mm = t.matmul(ps_ap[:, hw:hw + w], lhsT, rhs_ap[:, hw:hw + w],
                      start=start, stop=stop)
    return mm


def alloc_scratch(ctx):
    sc = {}
    sc["x"] = [ctx.sb(f"x{i}", [128, D], F32) for i in range(NT)]
    sc["xhat"] = ctx.sb("xhat", [128, 2 * D], BF16)
    sc["xhatT"] = ctx.sb("xhatT", [128, NF * TLOC], BF16)
    sc["qbuf"] = ctx.sb("qbuf", [128, 8 * TLOC], BF16)
    sc["ctxh"] = ctx.sb("ctxh", [128, 8 * TLOC], BF16)
    sc["kbuf"] = [ctx.sb(f"kbuf{i}", [128, 2048], BF16) for i in range(2)]
    sc["vbuf"] = [ctx.sb(f"vbuf{i}", [128, 2048], BF16) for i in range(2)]
    sc["wbuf"] = [ctx.sb(f"wbuf{i}", [128, NF * 256], BF16) for i in range(2)]
    sc["tA"] = ctx.sb("tA", [128, 1024], BF16)
    sc["tB"] = ctx.sb("tB", [128, 1024], BF16)
    sc["tC"] = ctx.sb("tC", [128, 1024], BF16)
    sc["tD"] = ctx.sb("tD", [128, 1024], BF16)
    sc["pT"] = [ctx.sb(f"pT{i}", [128, 1024], BF16) for i in range(2)]
    sc["ones_col_bf"] = ctx.sb("ones_col_bf", [128, 1], BF16)
    sc["ones_row_f"] = ctx.sb("ones_row_f", [1, 128], F32)
    sc["nrm"] = ctx.sb("nrm", [1, 128], F32)
    sc["nrm2"] = ctx.sb("nrm2", [1, 128], F32)
    sc["bcolr"] = ctx.sb("bcolr", [128, NKC], F32)
    sc["bcolr0"] = ctx.sb("bcolr0", [128, NKC], F32)
    sc["scr"] = ctx.sb("scr", [128, NKC], F32)
    sc["scr2"] = ctx.sb("scr2", [128, NKC], F32)
    sc["biasmat"] = ctx.sb("biasmat", [128, 16 * NKC], F32)
    sc["rcp"] = ctx.sb("rcp", [1, 1024], F32)
    sc["smallf"] = ctx.sb("smallf", [128, 64], F32)
    sc["smallg"] = ctx.sb("smallg", [1, 16], F32)
    sc["qsc"] = ctx.sb("qsc", [128, 16], BF16)
    sc["qscf"] = ctx.sb("qscf", [128, 32], F32)
    sc["gts"] = ctx.sb("gts", [128, 64], F32)
    sc["gtsb"] = ctx.sb("gtsb", [128, 64], BF16)
    sc["gtmp"] = ctx.sb("gtmp", [128, 24], F32)
    sc["gtT"] = ctx.sb("gtT", [8, 1024], BF16)
    sc["b2row"] = ctx.sb("b2row", [8, 512], BF16)
    sc["brow"] = sc["b2row"]
    sc["ones_row_bf"] = ctx.sb("ones_row_bf", [1, 128], BF16)
    sc["b1sb"] = ctx.sb("b1sb", [128, 64], F32)
    sc["sb1sb"] = ctx.sb("sb1sb", [128, 16], F32)
    sc["id128_bf"] = ctx.sb("id128_bf", [128, 128], BF16)
    sc["row2"] = ctx.sb("row2", [1, 1024], F32)
    sc["lnhg"] = ctx.sb("lnhg", [128, 2], F32)
    sc["bqsb"] = ctx.sb("bqsb", [128, 16], F32)
    sc["bksb"] = ctx.sb("bksb", [128, 16], F32)
    sc["bvsb"] = ctx.sb("bvsb", [128, 16], F32)
    sc["ropeC_t"] = ctx.sb("ropeC_t", [128, TLOC], BF16)
    sc["ropeS_t"] = ctx.sb("ropeS_t", [128, TLOC], BF16)
    sc["tri_sb"] = ctx.sb("tri_sb", [128, NKC * 128], BF16)
    sc["ps_a"] = ctx.ps("ps_a", [128, 1024])
    sc["ps_b"] = ctx.ps("ps_b", [128, 1024])
    sc["ps_c"] = ctx.ps("ps_c", [128, 1024])
    sc["ps_d"] = ctx.ps("ps_d", [128, 1024])
    return sc



def build_startup(ctx, sc, XIN, CT):
    nc = ctx.nc
    sem = ctx.sem
    with nc.Block("startup") as blk:
        @blk.sync
        def _(sy):
            for i in range(NT):
                sem["xin"].inc(sy.dma_start(sc["x"][i][:], XIN[128 * i:128 * (i + 1), :]), 16)
            sem["xin"].inc(sy.dma_start(
                sc["tri_sb"][:].rearrange("p (kc q) -> p kc q", q=128),
                CT["tri"].rearrange("(kc p) q -> p kc q", p=128)), 16)

        @blk.gpsimd
        def _(g):
            sem["tab"].inc(g.dma_start(sc["id128_bf"][:], CT["id128"][:, :]), 16)
            sem["tab"].inc(g.dma_start(sc["nrm"][0:1, 0:NKC], CT["bcol"][:, :]), 16)
            sem["tab"].inc(g.dma_start(sc["nrm2"][0:1, 0:NKC], CT["sc"][:, :]), 16)
            sem["tab"].inc(g.dma_start(sc["smallg"][0:1, 0:NKC], CT["sc2"][:, :]), 16)

        @blk.vector
        def _(v):
            v.memset(sc["ones_col_bf"][:], 1.0)
            v.memset(sc["ones_row_bf"][:], 1.0)
            v.memset(sc["bcolr0"][:], 0.0)
            sem["dve"].incw(v, v.memset(sc["ones_row_f"][:], 1.0))

        @blk.tensor
        def _(t):
            wait(t, sem["tab"], sem["dve"])
            t.matmul(sc["ps_a"][:, 0:NKC], sc["ones_row_f"][:],
                     sc["nrm"][0:1, 0:NKC], start=True, stop=True)
            t.matmul(sc["ps_b"][:, 0:NKC], sc["ones_row_f"][:],
                     sc["nrm2"][0:1, 0:NKC], start=True, stop=True)
            sem["pe"].incw(t, t.matmul(sc["ps_c"][:, 0:NKC], sc["ones_row_f"][:],
                                   sc["smallg"][0:1, 0:NKC], start=True, stop=True))

        @blk.scalar
        def _(s):
            wait(s, sem["pe"])
            s.copy(sc["bcolr"][:], sc["ps_a"][:, 0:NKC])
            s.copy(sc["scr"][:], sc["ps_b"][:, 0:NKC])
            sem["act"].incw(s, s.copy(sc["scr2"][:], sc["ps_c"][:, 0:NKC]))


def build_loadrope(ctx, sc, CT, cname, sname, tag):
    nc = ctx.nc
    sem = ctx.sem
    with nc.Block(f"lr_{tag}") as blk:
        @blk.gpsimd
        def _(g):
            wait(g, sem["dve"], sem["csrc"])
            sem["tab"].inc(g.dma_start(sc["ropeC_t"][:], CT[cname][:, :]), 16)
            sem["tab"].inc(g.dma_start(sc["ropeS_t"][:], CT[sname][:, :]), 16)


def build_phase0_cast(ctx, pairs, tag):
    nc = ctx.nc
    sem = ctx.sem
    with nc.Block(f"p0_{tag}") as blk:
        @blk.gpsimd
        def _(g):
            for srcap, dstap in pairs:
                sem["p0"].inc(g.dma_start(dstap, srcap), 16)



def build_ln(ctx, sc, tag, want_stats=False, eps=LN_EPS, to_hbm=None):
    nc = ctx.nc
    sem = ctx.sem
    bn = sc["smallf"]
    xhat = sc["xhat"]
    act0 = sem["act"].n
    dve0 = sem["dve"].n
    xh0 = sem["xh"].n
    NV = 9
    with nc.Block(f"ln_{tag}") as blk:
        @blk.vector
        def _(v):
            wait(v, sem["xin"], sem["pe"], sem["act"])

            def vf(instr):
                sem["dve"].inc(instr)
                v.wait_ge(sem["dve"].h, sem["dve"].n)

            for i in range(NT):
                base = sem["dve"].n
                xt = sc["x"][i]
                st = None
                for c4 in range(4):
                    st = v.bn_stats(bn[:, 6 * c4:6 * c4 + 6], xt[:, 512 * c4:512 * (c4 + 1)])
                vf(st)
                vf(v.bn_aggr(bn[:, 24:26], bn[:, 0:24]))
                if want_stats:
                    v.tensor_copy(sc["smallf"][:, 48 + 2 * i:50 + 2 * i], bn[:, 24:26])
                vf(v.tensor_scalar(bn[:, 26:27], bn[:, 25:26], eps, None, OP.add))
                vf(v.tensor_copy(bn[:, 30:31], bn[:, 26:27]))
                v.wait_ge(sem["act"].h, act0 + i + 1)
                vf(v.reciprocal(bn[:, 27:28], bn[:, 30:31]))
                vf(v.tensor_tensor(bn[:, 28:29], bn[:, 24:25], bn[:, 27:28], OP.mult))
                vf(v.tensor_scalar(bn[:, 28:29], bn[:, 28:29], -1.0, None, OP.mult))
                if to_hbm is not None and i >= 2:
                    v.wait_ge(sem["xh"].h, xh0 + i * 16)
                vf(v.tensor_scalar(xhat[:, D * (i % 2):D * (i % 2 + 1)], xt[:],
                                   bn[:, 27:28], bn[:, 28:29], OP.mult, OP.add))
                while sem["dve"].n < base + NV:
                    sem["dve"].inc(v.nop())
                assert sem["dve"].n == base + NV

        @blk.scalar
        def _(s):
            for i in range(NT):
                s.wait_ge(sem["dve"].h, dve0 + NV * i + 4)
                s.sqrt(bn[:, 30:31], bn[:, 30:31]).then_inc(sem["act"].h, 1)
                s.wait_ge(sem["act"].h, act0 + i + 1)
        sem["act"].n = act0 + NT

        if to_hbm is not None:
            @blk.gpsimd
            def _(g):
                for i in range(NT):
                    g.wait_ge(sem["dve"].h, dve0 + NV * i + 8)
                    g.dma_start(to_hbm[128 * i:128 * (i + 1), :],
                                sc["xhat"][:, D * (i % 2):D * (i % 2 + 1)]
                                ).then_inc(sem["xh"].h, 16)
            sem["xh"].n = xh0 + NT * 16


def build_bounce(ctx, sc, XH, tag):
    nc = ctx.nc
    sem = ctx.sem
    with nc.Block(f"tx_{tag}") as blk:
        @blk.sync
        def _(sy):
            wait(sy, sem["xh"], sem["msb"])
            for j in range(NF):
                sem["xt"].inc(sy.dma_start(sc["xhatT"][:, TLOC * j:TLOC * (j + 1)],
                                           XH[:, 128 * j:128 * (j + 1)], transpose=True), 16)



def build_headproj(ctx, sc, WBF, heads, dst_fn, norm_col0, ropeC, ropeS,
                   tag, scale=1.0, csrc_dma=None, war_gap=0, rope_mode="half",
                   qpad=None):
    nc = ctx.nc
    sem = ctx.sem
    ps_a, ps_b, ps_c = sc["ps_a"], sc["ps_b"], sc["ps_c"]
    npair = len(heads) // 2
    pairs = [(heads[2 * i], heads[2 * i + 1]) for i in range(npair)]

    pe0 = sem["pe"].n
    act0 = sem["act"].n
    dve0 = sem["dve"].n
    csrc0 = sem["csrc"].n
    wsem = [sem["w0"], sem["w1"]]
    w0n = [wsem[0].n, wsem[1].n]
    DPH = 3 if rope_mode == "half" else 4
    AP_ = 1 + (4 if qpad is not None else 0)

    with nc.Block(f"hp_{tag}") as blk:
        @blk.sync
        def _(sy):
            wait(sy, sem["p0"], sem["q0"], sem["q1"])
            for j, (ha, hb) in enumerate(pairs):
                if j >= 2:
                    sy.wait_ge(sem["pe"].h, pe0 + (j - 1) * 4)
                for f in range(NF):
                    sy.dma_start(
                        sc["wbuf"][j % 2][:, 256 * f:256 * (f + 1)],
                        WBF[128 * ha:128 * ha + 256, 128 * f:128 * (f + 1)],
                        transpose=True).then_inc(wsem[j % 2].h, 16)

        @blk.scalar
        def _(s):
            wait(s, sem["p0"], sem["q0"], sem["q1"])
            for j, (ha, hb) in enumerate(pairs):
                if j >= 2:
                    s.wait_ge(sem["pe"].h, pe0 + (j - 1) * 4)
                for o in range(2):
                    s.wait_ge(sem["pe"].h, pe0 + 4 * j + o + 1)
                    i_lin = 2 * j + o
                    if i_lin >= 1:
                        s.wait_ge(sem["dve"].h, dve0 + DPH * (i_lin - 1) + DPH - 1)
                    ps = ps_a if o == 0 else ps_b
                    if rope_mode == "half":
                        s.activation(sc["tA"][:], ps[:], AF.Copy, scale=scale)
                        s.activation(sc["tB"][0:64, :], ps[64:128, :], AF.Copy, scale=scale)
                        s.activation(sc["tB"][64:128, :], ps[0:64, :], AF.Copy,
                                     scale=scale).then_inc(sem["act"].h, 1)
                    else:
                        s.activation(sc["tA"][:], ps[:], AF.Copy,
                                     scale=scale).then_inc(sem["act"].h, 1)
                    s.wait_ge(sem["act"].h, act0 + AP_ * i_lin + 1)
                    if qpad is not None:
                        s.wait_ge(sem["dve"].h, dve0 + DPH * i_lin + DPH - 1)
                        stg = dst_fn(i_lin, 0)
                        (apA, kvhA), (apB, kvhB) = qpad(i_lin)
                        s.memzero(apA).then_inc(sem["act"].h, 1)
                        s.wait_ge(sem["act"].h, act0 + AP_ * i_lin + 2)
                        s.copy(apA[kvhA:kvhA + 64, :], stg[0:64, :]
                               ).then_inc(sem["act"].h, 1)
                        s.wait_ge(sem["act"].h, act0 + AP_ * i_lin + 3)
                        s.memzero(apB).then_inc(sem["act"].h, 1)
                        s.wait_ge(sem["act"].h, act0 + AP_ * i_lin + 4)
                        s.copy(apB[kvhB:kvhB + 64, :], stg[64:128, :]
                               ).then_inc(sem["act"].h, 1)
                        s.wait_ge(sem["act"].h, act0 + AP_ * i_lin + 5)

        sem["act"].n = act0 + npair * 2 * AP_
        for j in range(npair):
            wsem[j % 2].n += 16 * 16

        @blk.tensor
        def _(t):
            wait(t, sem["xt"])
            for j, (ha, hb) in enumerate(pairs):
                t.wait_ge(wsem[j % 2].h, w0n[j % 2] + (j // 2 + 1) * 16 * 16)
                for o in range(2):
                    ps = ps_a if o == 0 else ps_b
                    if j >= 1:
                        i_prev = 2 * (j - 1) + o
                        t.wait_ge(sem["act"].h, act0 + AP_ * i_prev + 1)
                    mm = None
                    for f in range(NF):
                        mm = mm512(t, ps, sc["wbuf"][j % 2][:, 256 * f + 128 * o:256 * f + 128 * (o + 1)],
                                   sc["xhatT"][:, TLOC * f:TLOC * (f + 1)],
                                   start=(f == 0), stop=(f == NF - 1))
                    mm.then_inc(sem["pe"].h, 1)
                    t.wait_ge(sem["pe"].h, pe0 + 4 * j + o + 1)
                for o in range(2):
                    t.wait_ge(sem["dve"].h, dve0 + DPH * (2 * j + o) + DPH)
                    mm = mm512(t, ps_c[0:1, :], sc["ones_col_bf"][:],
                               sc["tC"][:], start=True, stop=True)
                    mm.then_inc(sem["pe"].h, 1)
                    t.wait_ge(sem["pe"].h, pe0 + 4 * j + 2 + o + 1)
        sem["pe"].n = pe0 + npair * 4

        @blk.vector
        def _(v):
            wait(v, sem["tab"])
            for j, (ha, hb) in enumerate(pairs):
                for o in range(2):
                    h = (ha, hb)[o]
                    i_lin = 2 * j + o
                    v.wait_ge(sem["act"].h, act0 + AP_ * i_lin + 1)
                    if csrc_dma is not None and i_lin >= 2:
                        v.wait_ge(sem["csrc"].h, csrc0 + i_lin * war_gap * 16)
                    if qpad is not None and i_lin >= 2:
                        v.wait_ge(sem["act"].h, act0 + AP_ * (i_lin - 2) + AP_)
                    dst = dst_fn(i_lin, h)
                    b0 = dve0 + DPH * i_lin
                    if rope_mode == "xor1":
                        v.stream_shuffle(sc["tB"][:], sc["tA"][:], XOR1_MASK
                                         ).then_inc(sem["dve"].h, 1)
                        v.wait_ge(sem["dve"].h, b0 + 1)
                    v.tensor_tensor(sc["tA"][:], sc["tA"][:], ropeC, OP.mult)
                    v.tensor_tensor(sc["tB"][:], sc["tB"][:], ropeS,
                                    OP.mult).then_inc(sem["dve"].h, 1)
                    v.wait_ge(sem["dve"].h, b0 + DPH - 2)
                    v.tensor_tensor(dst, sc["tA"][:], sc["tB"][:], OP.add
                                    ).then_inc(sem["dve"].h, 1)
                    v.wait_ge(sem["dve"].h, b0 + DPH - 1)
                    v.tensor_tensor(sc["tC"][:], dst, dst,
                                    OP.mult).then_inc(sem["dve"].h, 1)
                    v.wait_ge(sem["dve"].h, b0 + DPH)
                    v.wait_ge(sem["pe"].h, pe0 + 4 * j + 2 + o + 1)
                    v.tensor_reduce(sc["nrm"][0:1, norm_col0 + i_lin:norm_col0 + i_lin + 1],
                                    ps_c[0:1, 0:1024], axis=mybir.AxisListType.X,
                                    op=OP.max)
        sem["dve"].n = dve0 + DPH * 2 * npair

        if csrc_dma is not None:
            @blk.gpsimd
            def _(g):
                for j in range(npair):
                    for o in range(2):
                        i_lin = 2 * j + o
                        h = pairs[j][o]
                        g.wait_ge(sem["dve"].h, dve0 + DPH * i_lin + DPH - 1)
                        csrc_dma(g, i_lin, h)
            sem["csrc"].n = csrc0 + npair * 2 * war_gap * 16
    return



def build_vstage(ctx, sc, WVBF, CSRC, tag, ndim=2048, csrc_row0=2048,
                 natural=False, nat_col0=0):
    nc = ctx.nc
    sem = ctx.sem
    ps_a, ps_b = sc["ps_a"], sc["ps_b"]
    n_oc = ndim // 256
    pe0 = sem["pe"].n
    act0 = sem["act"].n
    wsem = [sem["w0"], sem["w1"]]
    w0n = [wsem[0].n, wsem[1].n]
    vrsem = [sem["vr0"], sem["vr1"], sem["vr2"], sem["vr3"]]
    vr0 = [sme.n for sme in vrsem]
    ring = 4
    inc_per = 16 if ndim == 2048 else 32

    with nc.Block(f"vs_{tag}") as blk:
        @blk.sync
        def _(sy):
            for oc in range(n_oc):
                if oc >= 2:
                    sy.wait_ge(sem["pe"].h, pe0 + (oc - 1) * NT)
                for f in range(NF):
                    if natural:
                        sy.dma_start(
                            sc["wbuf"][oc % 2][:, 256 * f:256 * f + 256],
                            WVBF[128 * f:128 * (f + 1),
                                 nat_col0 + 256 * oc:nat_col0 + 256 * (oc + 1)]
                            ).then_inc(wsem[oc % 2].h, 16)
                    else:
                        sy.dma_start(
                            sc["wbuf"][oc % 2][:, 256 * f:256 * f + 256],
                            WVBF[256 * oc:256 * (oc + 1), 128 * f:128 * (f + 1)],
                            transpose=True).then_inc(wsem[oc % 2].h, 16)

        @blk.scalar
        def _(s):
            for oc in range(n_oc):
                if oc >= 2:
                    s.wait_ge(sem["pe"].h, pe0 + (oc - 1) * NT)
                for tb in range(NT):
                    idx = oc * NT + tb
                    r = idx % ring
                    s.wait_ge(sem["pe"].h, pe0 + idx + 1)
                    if idx >= ring:
                        s.wait_ge(vrsem[r].h, vr0[r] + (idx // ring) * inc_per)
                    ps = ps_a if tb % 2 == 0 else ps_b
                    s.copy(sc["tD"][:, 256 * r:256 * (r + 1)],
                           ps[:, 0:256]).then_inc(sem["act"].h, 1)
                    s.wait_ge(sem["act"].h, act0 + idx + 1)
        sem["act"].n = act0 + n_oc * NT
        for oc in range(n_oc):
            wsem[oc % 2].n += 16 * 16

        @blk.tensor
        def _(t):
            for oc in range(n_oc):
                t.wait_ge(wsem[oc % 2].h, w0n[oc % 2] + (oc // 2 + 1) * 16 * 16)
                for tb in range(NT):
                    idx = oc * NT + tb
                    ps = ps_a if tb % 2 == 0 else ps_b
                    if idx >= 2:
                        t.wait_ge(sem["act"].h, act0 + idx - 1)
                    mm = None
                    for f in range(NF):
                        mm = t.matmul(ps[:, 0:256],
                                 sc["xhatT"][:, TLOC * f + 128 * tb:TLOC * f + 128 * (tb + 1)],
                                 sc["wbuf"][oc % 2][:, 256 * f:256 * f + 256],
                                 start=(f == 0), stop=(f == NF - 1))
                    mm.then_inc(sem["pe"].h, 1)
                    t.wait_ge(sem["pe"].h, pe0 + idx + 1)
        sem["pe"].n = pe0 + n_oc * NT

        @blk.gpsimd
        def _(g):
            for oc in range(n_oc):
                d0 = oc * 256
                for tb in range(NT):
                    idx = oc * NT + tb
                    r = idx % ring
                    g.wait_ge(sem["act"].h, act0 + idx + 1)
                    srcap = sc["tD"][:, 256 * r:256 * (r + 1)]
                    if ndim == 2048:
                        dhalf = d0 // 1024
                        dcol = d0 % 1024
                        dst = CSRC[csrc_row0 + 2 * 128 * tb + dhalf:
                                   csrc_row0 + 2 * 128 * (tb + 1) + dhalf:2,
                                   dcol:dcol + 256]
                        g.dma_start(dst, srcap).then_inc(vrsem[r].h, 16)
                    else:
                        dst = CSRC[csrc_row0 + 128 * tb:csrc_row0 + 128 * (tb + 1),
                                   d0:d0 + 256]
                        g.dma_start(dst, srcap).then_inc(vrsem[r].h, 16)
                        dst2 = CSRC[csrc_row0 + 128 * tb:csrc_row0 + 128 * (tb + 1),
                                    512 + d0:512 + d0 + 256]
                        g.dma_start(dst2, srcap).then_inc(vrsem[r].h, 16)
        for idx in range(n_oc * NT):
            vrsem[idx % 4].n += inc_per
    return



def build_gather(ctx, sc, CSRC_t, CDST_t, nheads_k, tag, norm_row=4096):
    nc = ctx.nc
    sem = ctx.sem
    with nc.Block(f"cc_{tag}") as blk:
        @blk.vector
        def _(v):
            wait(v, sem["dve"])
            sem["dve"].inc(v.memset(sc["rcp"][0:1, :], 0.0))
            v.wait_ge(sem["dve"].h, sem["dve"].n)
            sem["dve"].incw(v, v.tensor_copy(sc["rcp"][0:1, 0:nheads_k],
                                         sc["nrm"][0:1, 0:nheads_k]))

        @blk.gpsimd
        def _(g):
            wait(g, sem["dve"])
            sem["csrc"].inc(g.dma_start(
                CSRC_t.ap()[norm_row:norm_row + 1, 0:1024], sc["rcp"][0:1, :]), 16)
            wait(g, sem["csrc"], sem["vr0"], sem["vr1"], sem["vr2"], sem["vr3"])
            g.collective_compute(
                "AllGather",
                mybir.AluOpType.bypass,
                replica_groups=[[0, 1], [2, 3], [4, 5], [6, 7]],
                ins=[CSRC_t.ap().opt()],
                outs=[CDST_t.ap().opt()],
            ).then_inc(sem["cc"].h, 1)
            sem["cc"].n += 1
            g.wait_ge(sem["cc"].h, sem["cc"].n)
    return



def build_bias(ctx, sc, CDST, phase, nh_phase, hd, tag, kn_of_head=None,
               qn_of_head=None, nrow0=4096, nrow1=8193, col0=None,
               bcol_key="bcolr"):
    nc = ctx.nc
    sem = ctx.sem
    if kn_of_head is None:
        kn_of_head = [phase * nh_phase + i for i in range(nh_phase)]
    if qn_of_head is None:
        qn_of_head = [32 + phase * nh_phase + i for i in range(nh_phase)]
    if col0 is None:
        col0 = phase * nh_phase * NKC
    nk = max(kn_of_head) + 1
    with nc.Block(f"bias_{tag}{phase}") as blk:
        @blk.gpsimd
        def _(g):
            wait(g, sem["cc"])
            sem["msc"].inc(g.dma_start(sc["nrm2"][0:1, 0:nk], CDST[nrow0:nrow0 + 1, 0:nk]), 16)
            sem["msc"].inc(g.dma_start(sc["nrm2"][0:1, 32:32 + nk], CDST[nrow1:nrow1 + 1, 0:nk]), 16)

        @blk.vector
        def _(v):
            wait(v, sem["msc"], sem["dve"], sem["pe"])

            def vf(instr):
                sem["dve"].inc(instr)
                v.wait_ge(sem["dve"].h, sem["dve"].n)

            vf(v.tensor_tensor(sc["nrm2"][0:1, 64:64 + nk], sc["nrm2"][0:1, 0:nk],
                               sc["nrm2"][0:1, 32:32 + nk], OP.max))
            cp = None
            for i in range(nh_phase):
                cp = v.tensor_copy(sc["nrm2"][0:1, 96 + i:97 + i],
                                   sc["nrm2"][0:1, 64 + kn_of_head[i]:65 + kn_of_head[i]])
            vf(cp)
            cp = None
            for i in range(nh_phase):
                cp = v.tensor_copy(sc["nrm2"][0:1, 112 + i:113 + i],
                                   sc["nrm"][0:1, qn_of_head[i]:qn_of_head[i] + 1])
            vf(cp)
            vf(v.tensor_tensor(sc["smallg"][0:1, 0:nh_phase],
                               sc["nrm2"][0:1, 112:112 + nh_phase],
                               sc["nrm2"][0:1, 96:96 + nh_phase], OP.mult))

        @blk.scalar
        def _(s):
            wait(s, sem["dve"])
            sem["act"].incw(s, s.sqrt(sc["smallg"][0:1, 0:nh_phase], sc["smallg"][0:1, 0:nh_phase]))

        @blk.vector
        def _(v2):
            wait(v2, sem["act"])
            sem["dve"].incw(v2, v2.tensor_scalar(sc["smallg"][0:1, 0:nh_phase],
                           sc["smallg"][0:1, 0:nh_phase], -1.0 / math.sqrt(hd), None, OP.mult))

        @blk.tensor
        def _(t):
            wait(t, sem["dve"], sem["act"])
            sem["pe"].incw(t, t.matmul(sc["ps_c"][:, 0:nh_phase], sc["ones_row_f"][:],
                                   sc["smallg"][0:1, 0:nh_phase], start=True, stop=True))

        @blk.scalar
        def _(s2):
            wait(s2, sem["pe"])
            sem["act"].incw(s2, s2.copy(sc["smallf"][:, 0:nh_phase], sc["ps_c"][:, 0:nh_phase]))

        @blk.vector
        def _(v3):
            wait(v3, sem["act"])
            ins = None
            for i in range(nh_phase):
                ins = v3.tensor_scalar(
                    sc["biasmat"][:, col0 + i * NKC:col0 + (i + 1) * NKC],
                    sc[bcol_key][:, 0:NKC], sc["smallf"][:, i:i + 1], None, OP.add)
            sem["dve"].incw(v3, ins)
    return



def build_attn_phase(ctx, sc, CDST, phase, heads, hd, tag, bv_sb=None,
                     bv_col0=0, causal=True):
    nc = ctx.nc
    sem = ctx.sem
    ps_a, ps_b, ps_c, ps_d = sc["ps_a"], sc["ps_b"], sc["ps_c"], sc["ps_d"]
    NHP = len(heads)
    pe0 = sem["pe"].n
    act0 = sem["act"].n
    dve0 = sem["dve"].n
    kvsem = [sem["kv0"], sem["kv1"]]
    kv0n = [kvsem[0].n, kvsem[1].n]
    escale = 1.0 / math.sqrt(hd)
    AH = 34 + (1 if bv_sb is not None else 0)

    def pe_s(i, kc):
        return pe0 + 33 * i + (1 if kc == 0 else 2 * kc)

    def pe_cd(i, kc):
        return pe0 + 33 * i + (2 * kc + 3 if kc <= 14 else 32)

    with nc.Block(f"at_{tag}{phase}") as blk:
        @blk.gpsimd
        def _(g):
            wait(g, sem["cc"])
            for i, h in enumerate(heads):
                if i >= 2:
                    g.wait_ge(sem["pe"].h, pe0 + 33 * (i - 1))
                kb = sc["kbuf"][i % 2]
                vb = sc["vbuf"][i % 2]
                for r in range(2):
                    g.dma_start(kb[:, 1024 * r:1024 * (r + 1)],
                                CDST[4097 * r + 128 * h:4097 * r + 128 * (h + 1), 0:1024]
                                ).then_inc(kvsem[i % 2].h, 16)
                    vsrc = CDST[4097 * r + 2048:4097 * r + 4096, :]
                    vsrc = vsrc.rearrange("(kc i two) j -> i kc two j", kc=8, i=128, two=2)
                    dhalf = (h * 128) // 1024
                    dcol = (h * 128) % 1024
                    g.dma_start(
                        vb[:, 1024 * r:1024 * (r + 1)].rearrange("i (kc j) -> i kc j", kc=8),
                        vsrc[:, :, dhalf, dcol:dcol + 128]).then_inc(kvsem[i % 2].h, 16)
        for i in range(NHP):
            kvsem[i % 2].n += 64

        @blk.tensor
        def _(t):
            for i, h in enumerate(heads):
                kb = sc["kbuf"][i % 2]
                vb = sc["vbuf"][i % 2]
                t.wait_ge(kvsem[i % 2].h, kv0n[i % 2] + (i // 2 + 1) * 64)
                if i >= 1:
                    t.wait_ge(sem["act"].h, act0 + 17 * i)
                    t.wait_ge(sem["dve"].h, dve0 + AH * i)
                for kc in range(NKC):
                    ps = ps_a if kc % 2 == 0 else ps_b
                    if kc >= 2:
                        t.wait_ge(sem["act"].h, act0 + 17 * i + (kc - 1))
                    mm = mm512(t, ps, kb[:, 128 * kc:128 * (kc + 1)],
                               sc["qbuf"][:, TLOC * i:TLOC * (i + 1)], start=True, stop=True)
                    mm.then_inc(sem["pe"].h, 1)
                    t.wait_ge(sem["pe"].h, pe_s(i, kc))
                    if kc >= 1:
                        kp = kc - 1
                        t.wait_ge(sem["dve"].h, dve0 + AH * i + 2 * (kp + 1))
                        pt = sc["pT"][kp % 2]
                        mm512(t, ps_c, vb[:, 128 * kp:128 * (kp + 1)], pt[:],
                              start=(kp == 0), stop=False)
                        mm = mm512(t, ps_d[0:1, :], sc["ones_col_bf"][:], pt[:],
                                   start=(kp == 0), stop=False)
                        mm.then_inc(sem["pe"].h, 1)
                        t.wait_ge(sem["pe"].h, pe_cd(i, kp))
                kp = NKC - 1
                t.wait_ge(sem["dve"].h, dve0 + AH * i + 2 * (kp + 1))
                pt = sc["pT"][kp % 2]
                mm512(t, ps_c, vb[:, 128 * kp:128 * (kp + 1)], pt[:],
                      start=False, stop=True)
                mm = mm512(t, ps_d[0:1, :], sc["ones_col_bf"][:], pt[:],
                           start=False, stop=True)
                mm.then_inc(sem["pe"].h, 1)
                t.wait_ge(sem["pe"].h, pe_cd(i, 15))
                t.wait_ge(sem["act"].h, act0 + 17 * i + 16)
                t.wait_ge(sem["dve"].h, dve0 + AH * i + 33)
                mm = mm512(t, ps_a, sc["ones_row_f"][:], sc["rcp"][0:1, :],
                           start=True, stop=True)
                mm.then_inc(sem["pe"].h, 1)
                t.wait_ge(sem["pe"].h, pe0 + 33 * i + 33)
        sem["pe"].n = pe0 + NHP * 33

        @blk.scalar
        def _(s):
            for i, h in enumerate(heads):
                hb = phase * 8 + i
                for kc in range(NKC):
                    ps = ps_a if kc % 2 == 0 else ps_b
                    s.wait_ge(sem["pe"].h, pe_s(i, kc))
                    s.activation(sc["pT"][kc % 2][:], ps[:], AF.Exp,
                                 scale=escale,
                                 bias=sc["biasmat"][:, hb * NKC + kc:hb * NKC + kc + 1]
                                 ).then_inc(sem["act"].h, 1)
                    s.wait_ge(sem["act"].h, act0 + 17 * i + kc + 1)
                s.wait_ge(sem["pe"].h, pe_cd(i, 15))
                s.copy(sc["tD"][:], ps_c[:]).then_inc(sem["act"].h, 1)
                s.wait_ge(sem["act"].h, act0 + 17 * i + 17)
        sem["act"].n = act0 + NHP * 17

        @blk.vector
        def _(v):
            for i, h in enumerate(heads):
                for kc in range(NKC):
                    v.wait_ge(sem["act"].h, act0 + 17 * i + kc + 1)
                    pt = sc["pT"][kc % 2]
                    if causal and 1 <= kc <= 8:
                        sem["dve"].inc(v.tensor_scalar(pt[:, 0:128 * kc], pt[:, 0:128 * kc],
                                       sc["scr"][:, kc:kc + 1], None, OP.mult))
                        v.wait_ge(sem["dve"].h, sem["dve"].n)
                    elif causal and kc >= 9:
                        w = 128 * (kc - 8)
                        sem["dve"].inc(v.tensor_scalar(pt[:, 0:w], pt[:, 0:w],
                                       sc["scr2"][:, kc:kc + 1], None, OP.mult))
                        v.wait_ge(sem["dve"].h, sem["dve"].n)
                    else:
                        sem["dve"].inc(v.nop())
                        v.wait_ge(sem["dve"].h, sem["dve"].n)
                    off = (kc % 8) * 128
                    if causal:
                        sem["dve"].inc(v.tensor_tensor(pt[:, off:off + 128], pt[:, off:off + 128],
                                       sc["tri_sb"][:, 128 * kc:128 * (kc + 1)], OP.mult))
                    else:
                        sem["dve"].inc(v.nop())
                    v.wait_ge(sem["dve"].h, sem["dve"].n)
                v.wait_ge(sem["pe"].h, pe_cd(i, 15))
                sem["dve"].inc(v.reciprocal(sc["rcp"][0:1, :], ps_d[0:1, :]))
                v.wait_ge(sem["dve"].h, sem["dve"].n)
                v.wait_ge(sem["act"].h, act0 + 17 * i + 17)
                v.wait_ge(sem["pe"].h, pe0 + 33 * i + 33)
                sem["dve"].inc(v.tensor_tensor(sc["ctxh"][:, TLOC * i:TLOC * (i + 1)],
                               sc["tD"][:], ps_a[:], OP.mult))
                v.wait_ge(sem["dve"].h, sem["dve"].n)
                if bv_sb is not None:
                    if i == 0:
                        wait(v, sem["msd"])
                    sem["dve"].inc(v.tensor_scalar(
                        sc["ctxh"][:, TLOC * i:TLOC * (i + 1)],
                        sc["ctxh"][:, TLOC * i:TLOC * (i + 1)],
                        bv_sb[:, bv_col0 + phase * 8 + i:bv_col0 + phase * 8 + i + 1],
                        None, OP.add))
                    v.wait_ge(sem["dve"].h, sem["dve"].n)
        assert sem["dve"].n == dve0 + NHP * AH
    return



def build_tokmaj_mm(ctx, sc, lhs_src, nchunk_f, rhs_load, n_oc, oc_w, evac_fn,
                    tag, evac_eng="scalar"):
    nc = ctx.nc
    sem = ctx.sem
    ps_a, ps_b = sc["ps_a"], sc["ps_b"]
    pe0 = sem["pe"].n
    act0 = sem["act"].n
    wsem = [sem["w0"], sem["w1"]]
    w0n = [wsem[0].n, wsem[1].n]
    assert oc_w <= 256

    with nc.Block(f"tm_{tag}") as blk:
        @blk.sync
        def _(sy):
            for oc in range(n_oc):
                if oc >= 2:
                    sy.wait_ge(sem["pe"].h, pe0 + (oc - 1) * NT)
                for f in range(nchunk_f):
                    rhs_load(sy, oc, f,
                             sc["wbuf"][oc % 2][:, 256 * f:256 * f + oc_w]
                             ).then_inc(wsem[oc % 2].h, 16)

        @blk.scalar
        def _(s):
            for oc in range(n_oc):
                if oc >= 2:
                    s.wait_ge(sem["pe"].h, pe0 + (oc - 1) * NT)
                if evac_eng == "scalar":
                    for tb in range(NT):
                        s.wait_ge(sem["pe"].h, pe0 + oc * NT + tb + 1)
                        ps = ps_a if tb % 2 == 0 else ps_b
                        evac_fn(s, ps[:, 0:oc_w], tb, oc).then_inc(sem["act"].h, 1)
                        s.wait_ge(sem["act"].h, act0 + oc * NT + tb + 1)
        if evac_eng == "scalar":
            sem["act"].n = act0 + n_oc * NT
        for oc in range(n_oc):
            wsem[oc % 2].n += nchunk_f * 16

        @blk.tensor
        def _(t):
            for oc in range(n_oc):
                t.wait_ge(wsem[oc % 2].h, w0n[oc % 2] + (oc // 2 + 1) * nchunk_f * 16)
                for tb in range(NT):
                    ps = ps_a if tb % 2 == 0 else ps_b
                    if oc * NT + tb >= 2:
                        t.wait_ge(sem["act"].h, act0 + oc * NT + tb - 1)
                    mm = None
                    for f in range(nchunk_f):
                        mm = t.matmul(ps[:, 0:oc_w], lhs_src(f, tb),
                                 sc["wbuf"][oc % 2][:, 256 * f:256 * f + oc_w],
                                 start=(f == 0), stop=(f == nchunk_f - 1))
                    mm.then_inc(sem["pe"].h, 1)
                    t.wait_ge(sem["pe"].h, pe0 + oc * NT + tb + 1)
        sem["pe"].n = pe0 + n_oc * NT

        if evac_eng == "vector":
            @blk.vector
            def _(v):
                for oc in range(n_oc):
                    for tb in range(NT):
                        v.wait_ge(sem["pe"].h, pe0 + oc * NT + tb + 1)
                        ps = ps_a if tb % 2 == 0 else ps_b
                        evac_fn(v, ps[:, 0:oc_w], tb, oc).then_inc(sem["act"].h, 1)
                        v.wait_ge(sem["act"].h, act0 + oc * NT + tb + 1)
            sem["act"].n = act0 + n_oc * NT
    return


def build_outproj(ctx, sc, WOBF, phase, tag, nchunk=8):
    def rhs_load(eng, oc, f, dest):
        dglob = phase * nchunk + f
        return eng.dma_start(dest,
                             WOBF[256 * oc:256 * (oc + 1), 128 * dglob:128 * (dglob + 1)],
                             transpose=True)

    def lhs(f, tb):
        return sc["ctxh"][:, TLOC * f + 128 * tb:TLOC * f + 128 * (tb + 1)]

    def evac(v, ps_ap, tb, oc):
        xt = sc["x"][tb]
        return v.tensor_tensor(xt[:, 256 * oc:256 * (oc + 1)],
                               xt[:, 256 * oc:256 * (oc + 1)], ps_ap, OP.add)

    build_tokmaj_mm(ctx, sc, lhs, nchunk, rhs_load, 8, 256, evac,
                    tag + f"op{phase}", evac_eng="vector")
    return



def build_attn_a(ctx, sc, WBF, XH, CSRC_t, CDST_t, tag):
    CSRC = CSRC_t.ap()
    CDST = CDST_t.ap()
    if STAGES < 1:
        return
    build_ln(ctx, sc, tag, to_hbm=XH)
    if STAGES < 2:
        return
    build_loadrope(ctx, sc, ctx.CT, "ropeA_c", "ropeA_s", tag)
    build_bounce(ctx, sc, XH, tag)
    if STAGES < 3:
        return

    def kdst(i_lin, h):
        return sc["pT"][i_lin % 2][:]

    def kdma(g, i_lin, h):
        g.dma_start(CSRC[128 * h:128 * (h + 1), 0:1024],
                    sc["pT"][i_lin % 2][:]).then_inc(ctx.sem["csrc"].h, 16)

    build_headproj(ctx, sc, WBF["k"], list(range(16)), kdst, 0,
                   sc["ropeC_t"][:], sc["ropeS_t"][:], tag + "K",
                   csrc_dma=kdma, war_gap=1)
    if STAGES < 4:
        return
    build_vstage(ctx, sc, WBF["v"], CSRC, tag)
    if STAGES < 5:
        return
    build_gather(ctx, sc, CSRC_t, CDST_t, 16, tag)
    if STAGES < 6:
        return

    for phase in range(2):
        heads = list(range(8 * phase, 8 * phase + 8))

        def qdst(i_lin, h):
            return sc["qbuf"][:, TLOC * i_lin:TLOC * (i_lin + 1)]

        build_headproj(ctx, sc, WBF["q"], heads, qdst, 32 + 8 * phase,
                       sc["ropeC_t"][:], sc["ropeS_t"][:], f"{tag}Q{phase}")
        if STAGES < 7:
            return
        build_bias(ctx, sc, CDST, phase, 8, 128, tag)
        if STAGES < 8:
            return
        build_attn_phase(ctx, sc, CDST, phase, heads, 128, tag)
        if STAGES < 9:
            return
        build_outproj(ctx, sc, WBF["o"], phase, tag)
        if STAGES < 10:
            return
    return



def build_store(ctx, sc, XOUT):
    nc = ctx.nc
    sem = ctx.sem
    with nc.Block("store") as blk:
        @blk.sync
        def _(sy):
            wait(sy, sem["dve"], sem["act"], sem["pe"], sem["cc"],
                 sem["xh"], sem["xt"], sem["csrc"], sem["msc"], sem["msb"], sem["msd"],
                 sem["vr0"], sem["vr1"], sem["vr2"], sem["vr3"],
                 sem["w0"], sem["w1"], sem["kv0"], sem["kv1"], sem["p0"],
                 sem["q0"], sem["q1"])
            for i in range(NT):
                sem["out"].inc(sy.dma_start(XOUT[128 * i:128 * (i + 1), :], sc["x"][i][:]), 16)
            wait(sy, sem["out"])



SUBLAYERS = [
    ("attn_a", "L0A"),
    ("ffn_l0", "L0F"),
    ("gqa", "L0B"),
    ("ffn_l0", "L0F"),
    ("attn_a", "L1A"),
    ("moe", "L1F"),
    ("attn_b1", "L1B"),
    ("moe", "L1F"),
]

WEIGHT_INPUTS = {
    "L0A": [("q", "L0_attn_a_Wq", [2048, 2048]), ("k", "L0_attn_a_Wk", [2048, 2048]),
            ("v", "L0_attn_a_Wv", [2048, 2048]), ("o", "L0_attn_a_Wo", [2048, 2048])],
    "L1A": [("q", "L1_attn_a_Wq", [2048, 2048]), ("k", "L1_attn_a_Wk", [2048, 2048]),
            ("v", "L1_attn_a_Wv", [2048, 2048]), ("o", "L1_attn_a_Wo", [2048, 2048])],
    "L0B": [("q", "L0_attn_b_wq", [2048, 2048]), ("k", "L0_attn_b_wk", [512, 2048]),
            ("v", "L0_attn_b_wv", [512, 2048]), ("o", "L0_attn_b_wo", [2048, 2048])],
    "L0F": [("gu", "L0_ffn_w_gate_up", [16384, 2048]), ("pr", "L0_ffn_w_proj", [2048, 8192])],
    "L1F": [("gate", "L1_ffn_gate_w", [2048, 8]), ("w1", "L1_ffn_w1", [8 * 2048, 1024]),
            ("b1", "L1_ffn_b1", [8, 1024]), ("w2", "L1_ffn_w2", [8 * 1024, 2048]),
            ("b2", "L1_ffn_b2", [8, 2048]),
            ("sw1", "L1_ffn_shared_w1", [2048, 2048]), ("sb1", "L1_ffn_shared_b1", [1, 2048]),
            ("sw2", "L1_ffn_shared_w2", [2048, 2048]), ("sb2", "L1_ffn_shared_b2", [1, 2048])],
    "L1B": [("wq", "L1_attn_b_Wq", [2048, 2048]), ("bq", "L1_attn_b_bq", [1, 2048]),
            ("wkv", "L1_attn_b_Wkv", [2048, 4096]), ("bkv", "L1_attn_b_bkv", [1, 4096]),
            ("lnqg", "L1_attn_b_ln_q_g", [1, 2048]), ("lnqb", "L1_attn_b_ln_q_b", [1, 2048]),
            ("lnkg", "L1_attn_b_ln_kv_g", [1, 2048]), ("lnkb", "L1_attn_b_ln_kv_b", [1, 2048]),
            ("lnhg", "L1_attn_b_ln_h_g", [1, 128]), ("lnhb", "L1_attn_b_ln_h_b", [1, 128]),
            ("wo", "L1_attn_b_Wo", [2048, 2048]), ("bo", "L1_attn_b_bo", [1, 2048])],
}


def build_program(sub_idx_list):
    nc = bass.Bass("TRN2", target_bir_lowering=False, debug=False)
    stack = contextlib.ExitStack()
    ctx = Ctx(nc, stack)

    XIN = nc.dram_tensor("XIN", [TLOC, D], F32, kind="ExternalInput").ap()
    XOUT = nc.dram_tensor("XOUT", [TLOC, D], F32, kind="ExternalOutput").ap()
    CT = {}
    for name, shape, dt in CONST_SPECS:
        CT[name] = nc.dram_tensor("c_" + name, shape, dt, kind="ExternalInput").ap()

    kinds = [SUBLAYERS[i] for i in sub_idx_list]
    wnames = sorted({wkey for _, wkey in kinds})
    weights = {}
    input_names = ["XIN"] + ["c_" + n for n, _, _ in CONST_SPECS]
    for wkey in wnames:
        weights[wkey] = {}
        for short, iname, shape in WEIGHT_INPUTS[wkey]:
            weights[wkey][short] = nc.dram_tensor(iname, shape, F32, kind="ExternalInput").ap()
            input_names.append(iname)

    XH = nc.dram_tensor("XH", [TLOC, D], BF16, kind="Internal").ap()

    wbf = {}
    for wkey in wnames:
        if wkey in ("L0A", "L1A", "L0B"):
            wbf[wkey] = {}
            for short, iname, shape in WEIGHT_INPUTS[wkey]:
                wbf[wkey][short] = nc.dram_tensor(
                    f"{wkey}_{short}_bf", shape, BF16, kind="Internal").ap()
        elif wkey == "L1F":
            wbf[wkey] = {}
            for short in ["gate", "w1", "w2", "sw1", "sw2", "b2", "sb2"]:
                shape = dict((s, sh) for s, n, sh in WEIGHT_INPUTS["L1F"])[short]
                wbf[wkey][short] = nc.dram_tensor(
                    f"L1F_{short}_bf", shape, BF16, kind="Internal").ap()
        elif wkey == "L1B":
            wbf[wkey] = {}
            for short in ["wq", "wkv", "wo", "bo"]:
                shape = dict((s, sh) for s, n, sh in WEIGHT_INPUTS["L1B"])[short]
                wbf[wkey][short] = nc.dram_tensor(
                    f"L1B_{short}_bf", shape, BF16, kind="Internal").ap()

    cds = {}
    for i in sub_idx_list:
        kind, wkey = SUBLAYERS[i]
        if kind in ("attn_a", "attn_b1"):
            cds[i] = (
                nc.dram_tensor(f"CSRC{i}", [4097, 1024], BF16, kind="Internal"),
                nc.dram_tensor(f"CDST{i}", [8194, 1024], BF16, kind="Internal"),
            )
        elif kind == "gqa":
            cds[i] = (
                nc.dram_tensor(f"CSRC{i}", [1537, 1024], BF16, kind="Internal"),
                nc.dram_tensor(f"CDST{i}", [3074, 1024], BF16, kind="Internal"),
            )
    have_ffn = any(SUBLAYERS[i][0] == "ffn_l0" for i in sub_idx_list)
    if have_ffn:
        WGU_Q = nc.dram_tensor("WGU_Q", [16384, 2048], BF16, kind="Internal").ap()
        WPR_Q = nc.dram_tensor("WPR_Q", [2048, 8192], BF16, kind="Internal").ap()
        HHBM = nc.dram_tensor("HHBM", [8192, TLOC], BF16, kind="Internal").ap()
    have_b1 = any(SUBLAYERS[i][0] == "attn_b1" for i in sub_idx_list)
    if have_b1:
        XQ = nc.dram_tensor("XQ", [TLOC, D], BF16, kind="Internal").ap()
        XK = nc.dram_tensor("XK", [TLOC, D], BF16, kind="Internal").ap()
        MROW = nc.dram_tensor("MROW", [8, 128], BF16, kind="Internal").ap()

    ctx.CT = CT
    with stack:
        sc = alloc_scratch(ctx)
        build_startup(ctx, sc, XIN, CT)
        pairs = []
        for wkey in wnames:
            if wkey in wbf:
                for short in wbf[wkey]:
                    pairs.append((weights[wkey][short][:, :], wbf[wkey][short][:, :]))
        if pairs:
            build_phase0_cast(ctx, pairs, "w")
        if have_ffn:
            build_quant(ctx, sc, weights["L0F"]["gu"], WGU_Q, 16384, 2048, "gu",
                        row_map=lambda rt: 256 * rt if rt < 64 else 256 * (rt - 64) + 128)
            build_quant(ctx, sc, weights["L0F"]["pr"], WPR_Q, 2048, 8192, "pr")

        for i in sub_idx_list:
            kind, wkey = SUBLAYERS[i]
            if kind == "attn_a":
                build_attn_a(ctx, sc, wbf[wkey], XH, cds[i][0], cds[i][1], f"s{i}")
            elif kind == "ffn_l0":
                build_ffn_l0(ctx, sc, WGU_Q, WPR_Q, XH, HHBM, f"s{i}")
            elif kind == "gqa":
                build_attn_gqa(ctx, sc, wbf[wkey], XH, cds[i][0], cds[i][1], f"s{i}")
            elif kind == "moe":
                Wm = dict(wbf["L1F"])
                Wm["b1f"] = weights["L1F"]["b1"]
                Wm["sb1f"] = weights["L1F"]["sb1"]
                build_moe(ctx, sc, Wm, XH, f"s{i}")
            elif kind == "attn_b1":
                build_attn_b1(ctx, sc, wbf["L1B"], weights["L1B"], XH, XQ, XK,
                              MROW, cds[i][0], cds[i][1], f"s{i}")
            else:
                raise NotImplementedError(kind)
        build_store(ctx, sc, XOUT)
    return nc, input_names


def run_sublayers(sub_idx_list, x_full, inputs):
    nc, input_names = build_program(sub_idx_list)
    in_maps = []
    for c in range(NCORE):
        b, h = c // 2, c % 2
        m = {"XIN": np.ascontiguousarray(x_full[b, 1024 * h:1024 * (h + 1), :])}
        consts = host_constants(h)
        for name, shape, dt in CONST_SPECS:
            m["c_" + name] = np.ascontiguousarray(consts[name].reshape(shape))
        for iname in input_names:
            if iname.startswith(("XIN", "c_")):
                continue
            arr = np.asarray(inputs[iname], dtype=np.float32)
            if arr.ndim == 1:
                arr = arr.reshape(1, -1)
            elif arr.ndim == 3:
                arr = arr.reshape(-1, arr.shape[-1])
            m[iname] = np.ascontiguousarray(arr)
        in_maps.append(m)
    res = run_bass_kernel_spmd(nc, in_maps, core_ids=list(range(NCORE)))
    out = np.empty_like(x_full)
    for c in range(NCORE):
        b, h = c // 2, c % 2
        out[b, 1024 * h:1024 * (h + 1), :] = res.results[c]["XOUT"]
    return out


def kernel(**inputs):
    x = np.ascontiguousarray(np.asarray(inputs["L0_attn_a_hidden_states"], dtype=np.float32))
    for group in [[0, 1, 2, 3, 4, 5, 6, 7]]:
        x = run_sublayers(group, x, inputs)
    return x



def build_quant(ctx, sc, WSRC, WQ, R, Cc, tag, row_map=None):
    nc = ctx.nc
    sem = ctx.sem
    nrt = R // 128
    ncc = Cc // 1024
    qs0 = sem["msc"].n
    qsem = [sem["q0"], sem["q1"]]
    qo0 = [qsem[0].n, qsem[1].n]
    act0 = sem["act"].n
    dve0 = sem["dve"].n
    qsc = sc["qsc"]
    qscf = sc["qscf"]
    with nc.Block(f"qz_{tag}") as blk:
        @blk.gpsimd
        def _(g):
            for it in range(nrt * ncc):
                rt, cc = divmod(it, ncc)
                g.wait_ge(qsem[it % 2].h, qo0[it % 2] + (it // 2) * 16)
                g.dma_start(sc["wbuf"][it % 2][:, 0:1024],
                            WSRC[128 * rt:128 * (rt + 1), 1024 * cc:1024 * (cc + 1)]
                            ).then_inc(sem["msc"].h, 16)
                g.wait_ge(sem["dve"].h, dve0 + (it + 1) * 8)
                r0 = row_map(rt) if row_map is not None else 128 * rt
                g.dma_start(WQ[r0:r0 + 128, 1024 * cc:1024 * (cc + 1)],
                            sc["wbuf"][it % 2][:, 1024:2048]
                            ).then_inc(qsem[it % 2].h, 16)
        sem["msc"].n = qs0 + nrt * ncc * 16
        for it in range(nrt * ncc):
            qsem[it % 2].n += 16

        @blk.scalar
        def _(s):
            for it in range(nrt * ncc):
                s.wait_ge(sem["msc"].h, qs0 + (it + 1) * 16)
                s.wait_ge(sem["dve"].h, dve0 + (8 * (it - 1) + 7 if it >= 1 else 0))
                wb = sc["wbuf"][it % 2][:, 0:1024]
                s.activation(sc["tA"][:], wb, AF.Abs)
                s.activation(sc["tB"][:], wb, AF.Sign).then_inc(sem["act"].h, 1)
                s.wait_ge(sem["act"].h, act0 + it + 1)
        sem["act"].n = act0 + nrt * ncc

        @blk.vector
        def _(v):
            for it in range(nrt * ncc):
                v.wait_ge(sem["act"].h, act0 + it + 1)
                v.wait_ge(qsem[it % 2].h, qo0[it % 2] + (it // 2) * 16)

                def vf(instr):
                    sem["dve"].inc(instr)
                    v.wait_ge(sem["dve"].h, sem["dve"].n)

                base = sem["dve"].n
                whq = sc["wbuf"][it % 2][:, 1024:2048]
                vf(v.tensor_reduce(sc["smallf"][:, 0:16],
                                   sc["tA"][:].rearrange("p (g k) -> p g k", k=64),
                                   axis=mybir.AxisListType.X, op=OP.add))
                vf(v.tensor_scalar(qsc[:], sc["smallf"][:, 0:16], 1.0 / 64.0, None, OP.mult))
                vf(v.tensor_scalar(qsc[:], qsc[:], 1e-8, None, OP.max))
                vf(v.tensor_copy(qscf[:, 0:16], qsc[:]))
                vf(v.tensor_scalar(qscf[:, 16:32], qscf[:, 0:16], 0.5, None, OP.mult))
                cp = None
                for gg in range(16):
                    cp = v.tensor_scalar(sc["tC"][:, 64 * gg:64 * (gg + 1)],
                                         sc["tA"][:, 64 * gg:64 * (gg + 1)],
                                         qscf[:, 16 + gg:17 + gg], None, OP.is_gt)
                vf(cp)
                vf(v.tensor_tensor(sc["tC"][:], sc["tC"][:], sc["tB"][:], OP.mult))
                cp = None
                for gg in range(16):
                    cp = v.tensor_scalar(whq[:, 64 * gg:64 * (gg + 1)],
                                         sc["tC"][:, 64 * gg:64 * (gg + 1)],
                                         qscf[:, gg:gg + 1], None, OP.mult)
                assert sem["dve"].n == base + 7
                vf(cp)
                assert sem["dve"].n == base + 8
        assert sem["dve"].n == dve0 + nrt * ncc * 8
    return



def build_ffn_l0(ctx, sc, WGU_Q, WPR_Q, XH, HHBM, tag):
    nc = ctx.nc
    sem = ctx.sem
    ps_a, ps_b, ps_c, ps_d = sc["ps_a"], sc["ps_b"], sc["ps_c"], sc["ps_d"]
    build_ln(ctx, sc, tag, to_hbm=XH)
    build_bounce(ctx, sc, XH, tag)

    wsem = [sem["w0"], sem["w1"]]
    vrsem = [sem["vr0"], sem["vr1"]]
    NPAIR = 64

    pe0 = sem["pe"].n
    act0 = sem["act"].n
    dve0 = sem["dve"].n
    w0n = [wsem[0].n, wsem[1].n]
    vr0n = [vrsem[0].n, vrsem[1].n]
    with nc.Block(f"f1_{tag}") as blk:
        @blk.sync
        def _(sy):
            wait(sy, sem["p0"], sem["q0"], sem["q1"])
            for i in range(NPAIR):
                if i >= 2:
                    sy.wait_ge(sem["pe"].h, pe0 + (i - 1) * 2)
                for f in range(NF):
                    sy.dma_start(
                        sc["wbuf"][i % 2][:, 256 * f:256 * f + 256],
                        WGU_Q[256 * i:256 * (i + 1), 128 * f:128 * (f + 1)],
                        transpose=True).then_inc(wsem[i % 2].h, 16)

        @blk.scalar
        def _(s):
            wait(s, sem["p0"], sem["q0"], sem["q1"])
            for i in range(NPAIR):
                s.wait_ge(sem["pe"].h, pe0 + 2 * i + 1)
                if i >= 1:
                    s.wait_ge(sem["dve"].h, dve0 + 2 * (i - 1) + 1)
                s.activation(sc["tA"][:], ps_a[:], AF.Silu)
                s.copy(sc["tB"][:], ps_b[:]).then_inc(sem["act"].h, 1)
                s.wait_ge(sem["act"].h, act0 + i + 1)
        sem["act"].n = act0 + NPAIR
        for i in range(NPAIR):
            wsem[i % 2].n += 16 * 16

        @blk.tensor
        def _(t):
            wait(t, sem["xt"])
            for i in range(NPAIR):
                t.wait_ge(wsem[i % 2].h, w0n[i % 2] + (i // 2 + 1) * 16 * 16)
                if i >= 1:
                    t.wait_ge(sem["act"].h, act0 + i)
                mm = None
                for f in range(NF):
                    mm512(t, ps_a, sc["wbuf"][i % 2][:, 256 * f:256 * f + 128],
                          sc["xhatT"][:, TLOC * f:TLOC * (f + 1)],
                          start=(f == 0), stop=(f == NF - 1))
                    mm = mm512(t, ps_b, sc["wbuf"][i % 2][:, 256 * f + 128:256 * f + 256],
                               sc["xhatT"][:, TLOC * f:TLOC * (f + 1)],
                               start=(f == 0), stop=(f == NF - 1))
                mm.then_inc(sem["pe"].h, 1)
                t.wait_ge(sem["pe"].h, pe0 + 2 * i + 1)
                t.wait_ge(sem["dve"].h, dve0 + 2 * i + 2)
                mm = mm512(t, ps_c[0:1, :], sc["ones_col_bf"][:], sc["tC"][:],
                           start=(i == 0), stop=(i == NPAIR - 1))
                mm.then_inc(sem["pe"].h, 1)
                t.wait_ge(sem["pe"].h, pe0 + 2 * i + 2)
        sem["pe"].n = pe0 + NPAIR * 2

        @blk.vector
        def _(v):
            for i in range(NPAIR):
                v.wait_ge(sem["act"].h, act0 + i + 1)
                u = i // 2
                if u >= 1:
                    v.wait_ge(vrsem[i % 2].h, vr0n[i % 2] + u * 16)
                ht = sc["pT"][i % 2]
                sem["dve"].inc(v.tensor_tensor(ht[:], sc["tA"][:], sc["tB"][:], OP.mult))
                v.wait_ge(sem["dve"].h, sem["dve"].n)
                sem["dve"].inc(v.tensor_tensor(sc["tC"][:], ht[:], ht[:], OP.mult))
                v.wait_ge(sem["dve"].h, sem["dve"].n)
        assert sem["dve"].n == dve0 + NPAIR * 2

        @blk.gpsimd
        def _(g):
            for i in range(NPAIR):
                g.wait_ge(sem["dve"].h, dve0 + 2 * i + 1)
                g.dma_start(HHBM[128 * i:128 * (i + 1), :],
                            sc["pT"][i % 2][:]).then_inc(vrsem[i % 2].h, 16)
        for i in range(NPAIR):
            vrsem[i % 2].n += 16

    EPS32 = float(np.finfo(np.float32).eps)
    with nc.Block(f"f2_{tag}") as blk:
        @blk.vector
        def _(v):
            wait(v, sem["pe"], sem["dve"])
            sem["dve"].incw(v, v.tensor_scalar(sc["rcp"][0:1, :], ps_c[0:1, :],
                            1.0 / 8192.0, EPS32, OP.mult, OP.add))

        @blk.tensor
        def _(t):
            wait(t, sem["dve"])
            mm = None
            pslist = [ps_a, ps_b, ps_c, ps_d]
            for tb in range(NT):
                mm = t.matmul(pslist[tb // 2][:, 512 * (tb % 2):512 * (tb % 2) + 1],
                              sc["rcp"][0:1, 128 * tb:128 * (tb + 1)],
                              sc["ones_row_f"][0:1, 0:1], start=True, stop=True)
            sem["pe"].incw(t, mm)

        @blk.scalar
        def _(s):
            wait(s, sem["pe"])
            pslist = [ps_a, ps_b, ps_c, ps_d]
            ins = None
            for pi in range(4):
                ins = s.sqrt(sc["smallf"][:, 32 + 2 * pi:34 + 2 * pi],
                             pslist[pi][:].rearrange("p (b c) -> p b c", b=2)[:, :, 0])
            sem["act"].incw(s, ins)

        @blk.vector
        def _(v2):
            wait(v2, sem["act"])
            sem["dve"].incw(v2, v2.reciprocal(sc["smallf"][:, 40:48], sc["smallf"][:, 32:40]))

    pe0 = sem["pe"].n
    act0 = sem["act"].n
    dve0 = sem["dve"].n
    w0n = [wsem[0].n, wsem[1].n]
    vr0n = [vrsem[0].n, vrsem[1].n]
    NH2 = 64

    def psreg(tb):
        base = [ps_a, ps_b, ps_c, ps_d][tb // 2]
        return base[:, 512 * (tb % 2):512 * (tb % 2 + 1)]

    with nc.Block(f"f3_{tag}") as blk:
        @blk.gpsimd
        def _(g):
            wait(g, vrsem[0], vrsem[1])
            for ocg in range(4):
                for hc in range(NH2):
                    it = ocg * NH2 + hc
                    u = it // 2
                    if u >= 1:
                        g.wait_ge(sem["pe"].h, pe0 + (it - 1))
                    g.dma_start(sc["pT"][it % 2][:], HHBM[128 * hc:128 * (hc + 1), :]
                                ).then_inc(vrsem[it % 2].h, 16)
        for it in range(4 * NH2):
            vrsem[it % 2].n += 16

        @blk.sync
        def _(sy):
            for ocg in range(4):
                for hc in range(NH2):
                    it = ocg * NH2 + hc
                    if it >= 2:
                        sy.wait_ge(sem["pe"].h, pe0 + (it - 1))
                    sy.dma_start(sc["wbuf"][it % 2][:, 0:512],
                                 WPR_Q[512 * ocg:512 * (ocg + 1),
                                       128 * hc:128 * (hc + 1)],
                                 transpose=True).then_inc(wsem[it % 2].h, 16)
        for it in range(4 * NH2):
            wsem[it % 2].n += 16

        @blk.tensor
        def _(t):
            for ocg in range(4):
                for hc in range(NH2):
                    it = ocg * NH2 + hc
                    t.wait_ge(wsem[it % 2].h, w0n[it % 2] + (it // 2 + 1) * 16)
                    t.wait_ge(vrsem[it % 2].h, vr0n[it % 2] + (it // 2 + 1) * 16)
                    if hc == 0 and ocg >= 1:
                        t.wait_ge(sem["dve"].h, dve0 + ocg * 8)
                    mm = None
                    for tb in range(NT):
                        mm = t.matmul(psreg(tb),
                                      sc["pT"][it % 2][:, 128 * tb:128 * (tb + 1)],
                                      sc["wbuf"][it % 2][:, 0:512],
                                      start=(hc == 0), stop=(hc == NH2 - 1))
                    mm.then_inc(sem["pe"].h, 1)
                    t.wait_ge(sem["pe"].h, pe0 + it + 1)
        sem["pe"].n = pe0 + 4 * NH2

        @blk.vector
        def _(v):
            for ocg in range(4):
                v.wait_ge(sem["pe"].h, pe0 + (ocg + 1) * NH2)
                for tb in range(NT):
                    xt = sc["x"][tb]
                    sem["dve"].inc(v.scalar_tensor_tensor(
                        xt[:, 512 * ocg:512 * (ocg + 1)], psreg(tb),
                        sc["smallf"][:, 40 + tb:41 + tb],
                        xt[:, 512 * ocg:512 * (ocg + 1)], OP.mult, OP.add))
                    v.wait_ge(sem["dve"].h, sem["dve"].n)
        assert sem["dve"].n == dve0 + 4 * 8
    return



def build_attn_phase_gqa(ctx, sc, CDST, phase, tag, rowsz):
    nc = ctx.nc
    sem = ctx.sem
    ps_a, ps_b, ps_c, ps_d = sc["ps_a"], sc["ps_b"], sc["ps_c"], sc["ps_d"]
    NHP = 8
    AH = 34
    pe0 = sem["pe"].n
    act0 = sem["act"].n
    dve0 = sem["dve"].n
    kvsem = [sem["kv0"], sem["kv1"]]
    kv0n = kvsem[phase % 2].n
    escale = 1.0 / math.sqrt(64.0)

    def pe_s(i, kc):
        return pe0 + 33 * i + (1 if kc == 0 else 2 * kc)

    def pe_cd(i, kc):
        return pe0 + 33 * i + (2 * kc + 3 if kc <= 14 else 32)

    kb = sc["kbuf"][phase % 2]
    vb = sc["vbuf"][phase % 2]

    with nc.Block(f"ag_{tag}{phase}") as blk:
        @blk.gpsimd
        def _(g):
            wait(g, sem["cc"])
            if phase >= 2:
                g.wait_ge(sem["pe"].h, pe0)
            for r in range(2):
                g.dma_start(kb[:, 1024 * r:1024 * (r + 1)],
                            CDST[rowsz * r + 128 * phase:rowsz * r + 128 * (phase + 1), 0:1024]
                            ).then_inc(kvsem[phase % 2].h, 16)
                g.dma_start(
                    vb[:, 1024 * r:1024 * (r + 1)].rearrange("i (kc j) -> i kc j", kc=8),
                    CDST[rowsz * r + 512:rowsz * r + 1536, 128 * phase:128 * (phase + 1)]
                    .rearrange("(kc i) j -> i kc j", kc=8)).then_inc(kvsem[phase % 2].h, 16)
        kvsem[phase % 2].n += 64

        @blk.tensor
        def _(t):
            t.wait_ge(kvsem[phase % 2].h, kv0n + 64)
            for i in range(NHP):
                kvh = (i // 4) * 64
                if i >= 1:
                    t.wait_ge(sem["act"].h, act0 + 18 * i)
                    t.wait_ge(sem["dve"].h, dve0 + AH * i)
                for kc in range(NKC):
                    ps = ps_a if kc % 2 == 0 else ps_b
                    if kc >= 2:
                        t.wait_ge(sem["act"].h, act0 + 18 * i + (kc - 1))
                    mm = mm512(t, ps, kb[:, 128 * kc:128 * (kc + 1)],
                               sc["qbuf"][:, TLOC * i:TLOC * (i + 1)],
                               start=True, stop=True)
                    mm.then_inc(sem["pe"].h, 1)
                    t.wait_ge(sem["pe"].h, pe_s(i, kc))
                    if kc >= 1:
                        kp = kc - 1
                        t.wait_ge(sem["dve"].h, dve0 + AH * i + 2 * (kp + 1))
                        pt = sc["pT"][kp % 2]
                        mm512(t, ps_c[0:64, :], vb[:, 128 * kp + kvh:128 * kp + kvh + 64],
                              pt[:], start=(kp == 0), stop=False)
                        mm = mm512(t, ps_d[0:1, :], sc["ones_col_bf"][:], pt[:],
                                   start=(kp == 0), stop=False)
                        mm.then_inc(sem["pe"].h, 1)
                        t.wait_ge(sem["pe"].h, pe_cd(i, kp))
                kp = NKC - 1
                t.wait_ge(sem["dve"].h, dve0 + AH * i + 2 * (kp + 1))
                pt = sc["pT"][kp % 2]
                mm512(t, ps_c[0:64, :], vb[:, 128 * kp + kvh:128 * kp + kvh + 64],
                      pt[:], start=False, stop=True)
                mm = mm512(t, ps_d[0:1, :], sc["ones_col_bf"][:], pt[:],
                           start=False, stop=True)
                mm.then_inc(sem["pe"].h, 1)
                t.wait_ge(sem["pe"].h, pe_cd(i, 15))
                t.wait_ge(sem["act"].h, act0 + 18 * i + 16)
                t.wait_ge(sem["dve"].h, dve0 + AH * i + 33)
                mm = mm512(t, ps_a[0:64, :], sc["ones_row_f"][0:1, 0:64],
                           sc["rcp"][0:1, :], start=True, stop=True)
                mm.then_inc(sem["pe"].h, 1)
                t.wait_ge(sem["pe"].h, pe0 + 33 * i + 33)
        sem["pe"].n = pe0 + NHP * 33

        @blk.scalar
        def _(s):
            for i in range(NHP):
                hb = i
                for kc in range(NKC):
                    ps = ps_a if kc % 2 == 0 else ps_b
                    s.wait_ge(sem["pe"].h, pe_s(i, kc))
                    s.activation(sc["pT"][kc % 2][:], ps[:], AF.Exp,
                                 scale=escale,
                                 bias=sc["biasmat"][:, hb * NKC + kc:hb * NKC + kc + 1]
                                 ).then_inc(sem["act"].h, 1)
                    s.wait_ge(sem["act"].h, act0 + 18 * i + kc + 1)
                s.wait_ge(sem["pe"].h, pe_cd(i, 15))
                s.copy(sc["tD"][0:64, :], ps_c[0:64, :]).then_inc(sem["act"].h, 1)
                s.wait_ge(sem["act"].h, act0 + 18 * i + 17)
                s.wait_ge(sem["dve"].h, dve0 + AH * i + 34)
                qt, qr = i // 2, (i % 2) * 64
                s.copy(sc["ctxh"][qr:qr + 64, TLOC * qt:TLOC * (qt + 1)],
                       sc["tC"][0:64, :]).then_inc(sem["act"].h, 1)
                s.wait_ge(sem["act"].h, act0 + 18 * i + 18)
        sem["act"].n = act0 + NHP * 18

        @blk.vector
        def _(v):
            for i in range(NHP):
                for kc in range(NKC):
                    v.wait_ge(sem["act"].h, act0 + 18 * i + kc + 1)
                    pt = sc["pT"][kc % 2]
                    if 1 <= kc <= 8:
                        sem["dve"].inc(v.tensor_scalar(pt[:, 0:128 * kc], pt[:, 0:128 * kc],
                                       sc["scr"][:, kc:kc + 1], None, OP.mult))
                        v.wait_ge(sem["dve"].h, sem["dve"].n)
                    elif kc >= 9:
                        w = 128 * (kc - 8)
                        sem["dve"].inc(v.tensor_scalar(pt[:, 0:w], pt[:, 0:w],
                                       sc["scr2"][:, kc:kc + 1], None, OP.mult))
                        v.wait_ge(sem["dve"].h, sem["dve"].n)
                    else:
                        sem["dve"].inc(v.nop())
                        v.wait_ge(sem["dve"].h, sem["dve"].n)
                    off = (kc % 8) * 128
                    sem["dve"].inc(v.tensor_tensor(pt[:, off:off + 128], pt[:, off:off + 128],
                                   sc["tri_sb"][:, 128 * kc:128 * (kc + 1)], OP.mult))
                    v.wait_ge(sem["dve"].h, sem["dve"].n)
                v.wait_ge(sem["pe"].h, pe_cd(i, 15))
                sem["dve"].inc(v.reciprocal(sc["rcp"][0:1, :], ps_d[0:1, :]))
                v.wait_ge(sem["dve"].h, sem["dve"].n)
                v.wait_ge(sem["act"].h, act0 + 18 * i + 17)
                v.wait_ge(sem["pe"].h, pe0 + 33 * i + 33)
                sem["dve"].inc(v.tensor_tensor(sc["tC"][0:64, :], sc["tD"][0:64, :],
                               ps_a[0:64, :], OP.mult))
                v.wait_ge(sem["dve"].h, sem["dve"].n)
        assert sem["dve"].n == dve0 + NHP * AH
    return



def build_attn_gqa(ctx, sc, WBF, XH, CSRC_t, CDST_t, tag):
    nc = ctx.nc
    sem = ctx.sem
    ROWSZ = 1537
    CSRC = CSRC_t.ap()
    CDST = CDST_t.ap()
    build_ln(ctx, sc, tag, to_hbm=XH)
    build_loadrope(ctx, sc, ctx.CT, "ropeG_c", "ropeG_s", tag)
    build_bounce(ctx, sc, XH, tag)

    def kdst(i_lin, h):
        return sc["pT"][i_lin % 2][:]

    def kdma(g, i_lin, h):
        g.dma_start(CSRC[128 * h:128 * (h + 1), 0:1024],
                    sc["pT"][i_lin % 2][:]).then_inc(ctx.sem["csrc"].h, 16)

    build_headproj(ctx, sc, WBF["k"], list(range(4)), kdst, 0,
                   sc["ropeC_t"][:], sc["ropeS_t"][:], tag + "K",
                   csrc_dma=kdma, war_gap=1, rope_mode="xor1")
    build_vstage(ctx, sc, WBF["v"], CSRC, tag, ndim=512, csrc_row0=512)
    build_gather(ctx, sc, CSRC_t, CDST_t, 4, tag, norm_row=1536)

    for phase in range(4):
        heads = list(range(4 * phase, 4 * phase + 4))

        def qdst(i_lin, h):
            return sc["pT"][i_lin % 2][:]

        def qpad(i_lin):
            kvh = 0 if i_lin < 2 else 64
            qa = sc["qbuf"][:, TLOC * (2 * i_lin):TLOC * (2 * i_lin + 1)]
            qb = sc["qbuf"][:, TLOC * (2 * i_lin + 1):TLOC * (2 * i_lin + 2)]
            return (qa, kvh), (qb, kvh)

        build_headproj(ctx, sc, WBF["q"], heads, qdst, 32 + 4 * phase,
                       sc["ropeC_t"][:], sc["ropeS_t"][:], f"{tag}Q{phase}",
                       rope_mode="xor1", qpad=qpad)
        build_bias(ctx, sc, CDST, phase, 8, 64, tag,
                   kn_of_head=[phase] * 8,
                   qn_of_head=[32 + 4 * phase + i // 2 for i in range(8)],
                   nrow0=1536, nrow1=1537 + 1536, col0=0)
        build_attn_phase_gqa(ctx, sc, CDST, phase, tag, 1537)
        build_outproj(ctx, sc, WBF["o"], phase, tag, nchunk=4)
    return



def build_moe(ctx, sc, W, XH, tag):
    nc = ctx.nc
    sem = ctx.sem
    ps_a, ps_b, ps_c, ps_d = sc["ps_a"], sc["ps_b"], sc["ps_c"], sc["ps_d"]
    build_ln(ctx, sc, tag, to_hbm=XH)
    build_bounce(ctx, sc, XH, tag)
    wsem = [sem["w0"], sem["w1"]]
    pslist = [ps_a, ps_b, ps_c, ps_d]

    def psb(tb, width):
        return pslist[tb // 2][:, 512 * (tb % 2):512 * (tb % 2) + width]

    pe0 = sem["pe"].n
    dve0 = sem["dve"].n
    act0 = sem["act"].n
    msc0 = sem["msc"].n
    gts, gtsb, gtmp, gtT = sc["gts"], sc["gtsb"], sc["gtmp"], sc["gtT"]
    with nc.Block(f"mg_{tag}") as blk:
        @blk.sync
        def _(sy):
            wait(sy, sem["q0"], sem["q1"], sem["p0"])
            sem["msb"].inc(sy.dma_start(
                sc["wbuf"][0][:, 0:128].rearrange("p (c e) -> p c e", e=8),
                W["gate"].rearrange("(c p) e -> p c e", p=128)), 16)

        @blk.tensor
        def _(t):
            wait(t, sem["msb"], sem["xt"], sem["act"], sem["dve"])
            for tb in range(NT):
                mm = None
                for f in range(NF):
                    mm = t.matmul(psb(tb, 8),
                                  sc["xhatT"][:, TLOC * f + 128 * tb:TLOC * f + 128 * (tb + 1)],
                                  sc["wbuf"][0][:, 8 * f:8 * f + 8],
                                  start=(f == 0), stop=(f == NF - 1))
                mm.then_inc(sem["pe"].h, 1)
                t.wait_ge(sem["pe"].h, pe0 + tb + 1)
            for tb in range(NT):
                t.wait_ge(sem["dve"].h, dve0 + 12 * (tb + 1))
                mm = t.matmul(psb(tb, 128)[0:8, :], gtsb[:, 8 * tb:8 * tb + 8],
                              sc["id128_bf"][:], start=True, stop=True)
                mm.then_inc(sem["pe"].h, 1)
                t.wait_ge(sem["pe"].h, pe0 + NT + tb + 1)
        sem["pe"].n = pe0 + 2 * NT

        @blk.vector
        def _(v):
            def vf(instr):
                sem["dve"].inc(instr)
                v.wait_ge(sem["dve"].h, sem["dve"].n)
            for tb in range(NT):
                v.wait_ge(sem["pe"].h, pe0 + tb + 1)
                gl = gts[:, 8 * tb:8 * tb + 8]
                sm = sc["smallf"]
                vf(v.tensor_copy(gl, psb(tb, 8)))
                vf(v.tensor_reduce(sm[:, 0:1], gl, axis=mybir.AxisListType.X, op=OP.max))
                vf(v.tensor_scalar(gtmp[:, 0:8], gl, sm[:, 0:1], None, OP.is_ge))
                vf(v.scalar_tensor_tensor(gtmp[:, 8:16], gtmp[:, 0:8], -1e30, gl,
                                          OP.mult, OP.add))
                vf(v.tensor_reduce(sm[:, 1:2], gtmp[:, 8:16], axis=mybir.AxisListType.X, op=OP.max))
                vf(v.tensor_tensor(sm[:, 2:3], sm[:, 0:1], sm[:, 1:2], OP.subtract))
                v.wait_ge(sem["act"].h, act0 + tb + 1)
                vf(v.tensor_scalar(sm[:, 4:5], sm[:, 3:4], -1.0, 1.0, OP.mult, OP.add))
                vf(v.tensor_scalar(gtmp[:, 16:24], gtmp[:, 8:16], sm[:, 1:2], None, OP.is_ge))
                vf(v.tensor_scalar(gtmp[:, 0:8], gtmp[:, 0:8], sm[:, 3:4], None, OP.mult))
                vf(v.tensor_scalar(gtmp[:, 16:24], gtmp[:, 16:24], sm[:, 4:5], None, OP.mult))
                vf(v.tensor_tensor(gl, gtmp[:, 0:8], gtmp[:, 16:24], OP.add))
                vf(v.tensor_copy(gtsb[:, 8 * tb:8 * tb + 8], gl))
        assert sem["dve"].n == dve0 + 12 * NT

        @blk.scalar
        def _(s):
            for tb in range(NT):
                s.wait_ge(sem["dve"].h, dve0 + 12 * tb + 6)
                s.activation(sc["smallf"][:, 3:4], sc["smallf"][:, 2:3],
                             AF.Sigmoid).then_inc(sem["act"].h, 1)
                s.wait_ge(sem["act"].h, act0 + tb + 1)
            for tb in range(NT):
                s.wait_ge(sem["pe"].h, pe0 + NT + tb + 1)
                s.copy(gtT[:, 128 * tb:128 * (tb + 1)],
                       psb(tb, 128)[0:8, :]).then_inc(sem["act"].h, 1)
                s.wait_ge(sem["act"].h, act0 + NT + tb + 1)
        sem["act"].n = act0 + 2 * NT

    def hstage(src_w, n_tiles, row0, bias_sb, bias_col0, dst_fn, stag):
        pe0 = sem["pe"].n
        act0 = sem["act"].n
        w0n = [wsem[0].n, wsem[1].n]
        with nc.Block(f"mh_{stag}") as blk:
            @blk.sync
            def _(sy):
                wait(sy, sem["q0"], sem["q1"])
                for i in range(n_tiles):
                    if i >= 2:
                        sy.wait_ge(sem["pe"].h, pe0 + i - 1)
                    for f in range(NF):
                        if f % 2 == 0:
                            sy.dma_start(
                                sc["wbuf"][i % 2][:, 128 * f:128 * (f + 1)],
                                src_w[128 * f:128 * (f + 1), row0 + 128 * i:row0 + 128 * (i + 1)]
                                ).then_inc(wsem[i % 2].h, 16)

            @blk.scalar
            def _(s):
                for i in range(n_tiles):
                    if i >= 2:
                        s.wait_ge(sem["pe"].h, pe0 + i - 1)
                    for f in range(NF):
                        if f % 2 == 1:
                            s.dma_start(
                                sc["wbuf"][i % 2][:, 128 * f:128 * (f + 1)],
                                src_w[128 * f:128 * (f + 1), row0 + 128 * i:row0 + 128 * (i + 1)]
                                ).then_inc(wsem[i % 2].h, 16)
                    s.wait_ge(sem["pe"].h, pe0 + i + 1)
                    ps = ps_a if i % 2 == 0 else ps_b
                    s.activation(dst_fn(i), ps[:], AF.Gelu,
                                 bias=bias_sb[:, bias_col0 + i:bias_col0 + i + 1]
                                 ).then_inc(sem["act"].h, 1)
                    s.wait_ge(sem["act"].h, act0 + i + 1)
            for i in range(n_tiles):
                wsem[i % 2].n += 16 * 16

            @blk.tensor
            def _(t):
                for i in range(n_tiles):
                    t.wait_ge(wsem[i % 2].h, w0n[i % 2] + (i // 2 + 1) * 16 * 16)
                    if i >= 2:
                        t.wait_ge(sem["act"].h, act0 + i - 1)
                    ps = ps_a if i % 2 == 0 else ps_b
                    mm = None
                    for f in range(NF):
                        mm = mm512(t, ps, sc["wbuf"][i % 2][:, 128 * f:128 * (f + 1)],
                                   sc["xhatT"][:, TLOC * f:TLOC * (f + 1)],
                                   start=(f == 0), stop=(f == NF - 1))
                    mm.then_inc(sem["pe"].h, 1)
                    t.wait_ge(sem["pe"].h, pe0 + i + 1)
            sem["pe"].n = pe0 + n_tiles
        sem["act"].n = act0 + n_tiles

    def ostage(src_w, nhc, wrow0, hsrc_fn, brow_src, evac_kind, e_idx, stag):
        pe0 = sem["pe"].n
        dve0 = sem["dve"].n
        msc0 = sem["msc"].n
        w0n = [wsem[0].n, wsem[1].n]
        per = nhc + (1 if brow_src is not None else 0)

        def mm_done(it):
            o2, h2 = divmod(it, nhc)
            return pe0 + o2 * per + h2 + 1

        with nc.Block(f"mo_{stag}") as blk:
            @blk.sync
            def _(sy):
                wait(sy, sem["q0"], sem["q1"])
                for ocg in range(4):
                    for hc in range(nhc):
                        it = ocg * nhc + hc
                        if it >= 2:
                            sy.wait_ge(sem["pe"].h, mm_done(it - 2))
                        sy.dma_start(sc["wbuf"][it % 2][:, 0:256],
                                     src_w[wrow0 + 128 * hc:wrow0 + 128 * (hc + 1),
                                           512 * ocg:512 * ocg + 256]
                                     ).then_inc(wsem[it % 2].h, 16)

            @blk.scalar
            def _(s):
                for ocg in range(4):
                    for hc in range(nhc):
                        it = ocg * nhc + hc
                        if it >= 2:
                            s.wait_ge(sem["pe"].h, mm_done(it - 2))
                        s.dma_start(sc["wbuf"][it % 2][:, 256:512],
                                    src_w[wrow0 + 128 * hc:wrow0 + 128 * (hc + 1),
                                          512 * ocg + 256:512 * (ocg + 1)]
                                    ).then_inc(wsem[it % 2].h, 16)
            for it in range(4 * nhc):
                wsem[it % 2].n += 32

            @blk.gpsimd
            def _(g):
                if brow_src is not None:
                    wait(g, sem["pe"])
                    for ocg in range(4):
                        if ocg >= 1:
                            g.wait_ge(sem["pe"].h, pe0 + ocg * per)
                        g.dma_start(sc["b2row"][0:1, 0:512],
                                    brow_src[0:1, 512 * ocg:512 * (ocg + 1)]
                                    ).then_inc(sem["msc"].h, 16)
            if brow_src is not None:
                sem["msc"].n += 4 * 16

            @blk.tensor
            def _(t):
                for ocg in range(4):
                    for hc in range(nhc):
                        it = ocg * nhc + hc
                        t.wait_ge(wsem[it % 2].h, w0n[it % 2] + (it // 2 + 1) * 32)
                        if hc == 0 and ocg >= 1:
                            t.wait_ge(sem["dve"].h, dve0 + ocg * 8)
                        mm = None
                        for tb in range(NT):
                            mm = t.matmul(psb(tb, 512), hsrc_fn(hc, tb),
                                          sc["wbuf"][it % 2][:, 0:512],
                                          start=(hc == 0), stop=(hc == nhc - 1 and brow_src is None))
                        mm.then_inc(sem["pe"].h, 1)
                        t.wait_ge(sem["pe"].h, mm_done(it))
                    if brow_src is not None:
                        t.wait_ge(sem["msc"].h, msc0 + (ocg + 1) * 16)
                        mm = None
                        for tb in range(NT):
                            mm = t.matmul(psb(tb, 512), sc["ones_row_bf"][:],
                                          sc["b2row"][0:1, 0:512], start=False, stop=True)
                        mm.then_inc(sem["pe"].h, 1)
                        t.wait_ge(sem["pe"].h, pe0 + (ocg + 1) * per)
            sem["pe"].n = pe0 + 4 * per

            @blk.vector
            def _(v):
                for ocg in range(4):
                    v.wait_ge(sem["pe"].h, pe0 + (ocg + 1) * per)
                    for tb in range(NT):
                        xt = sc["x"][tb]
                        if evac_kind == "add":
                            sem["dve"].inc(v.tensor_tensor(
                                xt[:, 512 * ocg:512 * (ocg + 1)],
                                xt[:, 512 * ocg:512 * (ocg + 1)], psb(tb, 512), OP.add))
                        else:
                            sem["dve"].inc(v.scalar_tensor_tensor(
                                xt[:, 512 * ocg:512 * (ocg + 1)], psb(tb, 512),
                                gts[:, 8 * tb + e_idx:8 * tb + e_idx + 1],
                                xt[:, 512 * ocg:512 * (ocg + 1)], OP.mult, OP.add))
                        v.wait_ge(sem["dve"].h, sem["dve"].n)
            assert sem["dve"].n == dve0 + 4 * 8

    with nc.Block(f"mb_{tag}") as blk:
        @blk.sync
        def _(sy):
            sem["msb"].inc(sy.dma_start(
                sc["b1sb"][:, 0:64].rearrange("p (e ht) -> p e ht", e=8),
                W["b1f"].rearrange("e (ht p) -> p e ht", p=128),
                allow_slow_non_contiguous=True), 16)
            sem["msb"].inc(sy.dma_start(
                sc["sb1sb"][:, 0:16].rearrange("p c -> p c"),
                W["sb1f"].rearrange("o (c p) -> p (o c)", p=128),
                allow_slow_non_contiguous=True), 16)
            wait(sy, sem["msb"])

    def shdst(i):
        if i < 8:
            return sc["ctxh"][:, TLOC * i:TLOC * (i + 1)]
        return sc["qbuf"][:, TLOC * (i - 8):TLOC * (i - 7)]

    hstage(W["sw1"], 16, 0, sc["sb1sb"], 0, shdst, tag + "sh")

    def sh_src(hc, tb):
        base = shdst(hc)
        return base[:, 128 * tb:128 * (tb + 1)]

    ostage(W["sw2"], 16, 0, sh_src, W["sb2"], "add", 0, tag + "so")

    def edst(i):
        return sc["qbuf"][:, TLOC * i:TLOC * (i + 1)]

    def e_src(hc, tb):
        return sc["qbuf"][:, TLOC * hc + 128 * tb:TLOC * hc + 128 * (tb + 1)]

    for e in range(8):
        hstage(W["w1"][2048 * e:2048 * (e + 1), :], 8, 0, sc["b1sb"], 8 * e,
               edst, f"{tag}e{e}h")
        ostage(W["w2"][1024 * e:1024 * (e + 1), :], 8, 0, e_src, None,
               "gate", e, f"{tag}e{e}o")

    pe0b = sem["pe"].n
    dve0b = sem["dve"].n
    msc0b = sem["msc"].n
    with nc.Block(f"mb2_{tag}") as blk:
        @blk.gpsimd
        def _(g):
            wait(g, sem["pe"])
            for ocg in range(4):
                if ocg >= 1:
                    g.wait_ge(sem["pe"].h, pe0b + ocg * NT)
                g.dma_start(sc["b2row"][:, :],
                            W["b2"][:, 512 * ocg:512 * (ocg + 1)]
                            ).then_inc(sem["msc"].h, 16)
        sem["msc"].n += 4 * 16

        @blk.tensor
        def _(t):
            wait(t, sem["act"], sem["dve"])
            for ocg in range(4):
                t.wait_ge(sem["msc"].h, msc0b + (ocg + 1) * 16)
                if ocg >= 1:
                    t.wait_ge(sem["dve"].h, dve0b + ocg * NT)
                for tb in range(NT):
                    mm = t.matmul(psb(tb, 512), sc["gtT"][:, 128 * tb:128 * (tb + 1)],
                                  sc["b2row"][:, :], start=True, stop=True)
                    mm.then_inc(sem["pe"].h, 1)
                    t.wait_ge(sem["pe"].h, pe0b + ocg * NT + tb + 1)
        sem["pe"].n = pe0b + 4 * NT

        @blk.vector
        def _(v):
            for ocg in range(4):
                for tb in range(NT):
                    v.wait_ge(sem["pe"].h, pe0b + ocg * NT + tb + 1)
                    xt = sc["x"][tb]
                    sem["dve"].inc(v.tensor_tensor(
                        xt[:, 512 * ocg:512 * (ocg + 1)],
                        xt[:, 512 * ocg:512 * (ocg + 1)], psb(tb, 512), OP.add))
                    v.wait_ge(sem["dve"].h, sem["dve"].n)
        assert sem["dve"].n == dve0b + 4 * NT
    return



def build_headproj_l1b(ctx, sc, WNAT, wcol0, heads, dst_fn, norm_col0,
                       bias_sb, bias_col0, tag, csrc_dma=None, war_gap=0):
    nc = ctx.nc
    sem = ctx.sem
    ps_a, ps_b, ps_c, ps_d = sc["ps_a"], sc["ps_b"], sc["ps_c"], sc["ps_d"]
    npair = len(heads) // 2
    pairs = [(heads[2 * i], heads[2 * i + 1]) for i in range(npair)]

    pe0 = sem["pe"].n
    act0 = sem["act"].n
    dve0 = sem["dve"].n
    csrc0 = sem["csrc"].n
    wsem = [sem["w0"], sem["w1"]]
    w0n = [wsem[0].n, wsem[1].n]
    PPP = 4 + 8
    PPH = 4
    DL = 14
    rcp = sc["rcp"]
    row2 = sc["row2"]

    def pe_h(i_lin, k):
        j, o = divmod(i_lin, 2)
        return pe0 + 12 * j + 2 + 5 * o + k

    with nc.Block(f"hb_{tag}") as blk:
        @blk.sync
        def _(sy):
            wait(sy, sem["q0"], sem["q1"], sem["p0"], sem["msd"])
            for j, (ha, hb) in enumerate(pairs):
                if j >= 2:
                    sy.wait_ge(sem["pe"].h, pe0 + (j - 1) * 12)
                for f in range(NF):
                    if f % 2 == 0:
                        sy.dma_start(
                            sc["wbuf"][j % 2][:, 256 * f:256 * (f + 1)],
                            WNAT[128 * f:128 * (f + 1),
                                 wcol0 + 256 * j:wcol0 + 256 * (j + 1)]
                            ).then_inc(wsem[j % 2].h, 16)

        @blk.scalar
        def _(s):
            wait(s, sem["q0"], sem["q1"], sem["p0"])
            for j, (ha, hb) in enumerate(pairs):
                if j >= 2:
                    s.wait_ge(sem["pe"].h, pe0 + (j - 1) * 12)
                for f in range(NF):
                    if f % 2 == 1:
                        s.dma_start(
                            sc["wbuf"][j % 2][:, 256 * f:256 * (f + 1)],
                            WNAT[128 * f:128 * (f + 1),
                                 wcol0 + 256 * j:wcol0 + 256 * (j + 1)]
                            ).then_inc(wsem[j % 2].h, 16)
                for o in range(2):
                    i_lin = 2 * j + o
                    s.wait_ge(sem["pe"].h, pe0 + 12 * j + o + 1)
                    if i_lin >= 1:
                        s.wait_ge(sem["dve"].h, dve0 + DL * (i_lin - 1) + 3)
                    if i_lin == 0:
                        wait(s, sem["msd"])
                    ps = ps_a if o == 0 else ps_b
                    h = (ha, hb)[o]
                    s.activation(sc["tA"][:], ps[:], AF.Identity,
                                 bias=bias_sb[:, bias_col0 + i_lin:bias_col0 + i_lin + 1]
                                 ).then_inc(sem["act"].h, 1)
                    s.wait_ge(sem["act"].h, act0 + 2 * i_lin + 1)
                    s.wait_ge(sem["dve"].h, dve0 + DL * i_lin + 8)
                    s.sqrt(row2[0:1, :], row2[0:1, :]).then_inc(sem["act"].h, 1)
                    s.wait_ge(sem["act"].h, act0 + 2 * i_lin + 2)
        sem["act"].n = act0 + npair * 4
        for j in range(npair):
            wsem[j % 2].n += 16 * 16

        @blk.tensor
        def _(t):
            wait(t, sem["xt"])
            for j, (ha, hb) in enumerate(pairs):
                t.wait_ge(wsem[j % 2].h, w0n[j % 2] + (j // 2 + 1) * 16 * 16)
                for o in range(2):
                    ps = ps_a if o == 0 else ps_b
                    if j >= 1:
                        t.wait_ge(sem["act"].h, act0 + 2 * (2 * (j - 1) + o) + 1)
                    mm = None
                    for f in range(NF):
                        mm = mm512(t, ps, sc["wbuf"][j % 2][:, 256 * f + 128 * o:256 * f + 128 * (o + 1)],
                                   sc["xhatT"][:, TLOC * f:TLOC * (f + 1)],
                                   start=(f == 0), stop=(f == NF - 1))
                    mm.then_inc(sem["pe"].h, 1)
                    t.wait_ge(sem["pe"].h, pe0 + 12 * j + o + 1)
                for o in range(2):
                    i_lin = 2 * j + o
                    t.wait_ge(sem["dve"].h, dve0 + DL * i_lin + 3)
                    mm = mm512(t, ps_c[0:1, :], sc["ones_col_bf"][:], sc["tB"][:],
                               start=True, stop=True)
                    mm.then_inc(sem["pe"].h, 1)
                    t.wait_ge(sem["pe"].h, pe_h(i_lin, 1))
                    t.wait_ge(sem["dve"].h, dve0 + DL * i_lin + 4)
                    mm = mm512(t, ps_d[0:1, :], sc["ones_col_bf"][:], sc["tC"][:],
                               start=True, stop=True)
                    mm.then_inc(sem["pe"].h, 1)
                    t.wait_ge(sem["pe"].h, pe_h(i_lin, 2))
                    t.wait_ge(sem["dve"].h, dve0 + DL * i_lin + 8)
                    mm = mm512(t, ps_c, sc["ones_row_f"][:], rcp[0:1, :],
                               start=True, stop=True)
                    mm.then_inc(sem["pe"].h, 1)
                    t.wait_ge(sem["pe"].h, pe_h(i_lin, 3))
                    t.wait_ge(sem["dve"].h, dve0 + DL * i_lin + 9)
                    mm = mm512(t, ps_d, sc["ones_row_f"][:], row2[0:1, :],
                               start=True, stop=True)
                    mm.then_inc(sem["pe"].h, 1)
                    t.wait_ge(sem["pe"].h, pe_h(i_lin, 4))
                    t.wait_ge(sem["dve"].h, dve0 + DL * i_lin + 13)
                    mm = mm512(t, ps_c[0:1, :], sc["ones_col_bf"][:], sc["tC"][:],
                               start=True, stop=True)
                    mm.then_inc(sem["pe"].h, 1)
                    t.wait_ge(sem["pe"].h, pe_h(i_lin, 5))
        sem["pe"].n = pe0 + npair * PPP

        @blk.vector
        def _(v):
            wait(v, sem["msd"])
            def vf(instr):
                sem["dve"].inc(instr)
                v.wait_ge(sem["dve"].h, sem["dve"].n)
            for j, (ha, hb) in enumerate(pairs):
                for o in range(2):
                    h = (ha, hb)[o]
                    i_lin = 2 * j + o
                    base = sem["dve"].n
                    v.wait_ge(sem["act"].h, act0 + 2 * i_lin + 1)
                    if csrc_dma is not None and i_lin >= 2:
                        v.wait_ge(sem["csrc"].h, csrc0 + i_lin * war_gap * 16)
                    dst = dst_fn(i_lin, h)
                    vf(v.stream_shuffle(sc["tB"][:], sc["tA"][:], XOR1_MASK))
                    v.tensor_tensor(sc["tA"][:], sc["tA"][:], sc["ropeC_t"][:], OP.mult)
                    vf(v.tensor_tensor(sc["tB"][:], sc["tB"][:], sc["ropeS_t"][:], OP.mult))
                    vf(v.tensor_tensor(sc["tB"][:], sc["tA"][:], sc["tB"][:], OP.add))
                    vf(v.tensor_tensor(sc["tC"][:], sc["tB"][:], sc["tB"][:], OP.mult))
                    v.wait_ge(sem["pe"].h, pe_h(i_lin, 1))
                    vf(v.tensor_scalar(rcp[0:1, :], ps_c[0:1, :], 1.0 / 128.0, None, OP.mult))
                    v.wait_ge(sem["pe"].h, pe_h(i_lin, 2))
                    vf(v.tensor_scalar(row2[0:1, :], ps_d[0:1, :], 1.0 / 128.0, None, OP.mult))
                    vf(v.tensor_tensor(sc["ps_c"][0:1, :], rcp[0:1, :], rcp[0:1, :], OP.mult))
                    vf(v.scalar_tensor_tensor(row2[0:1, :], row2[0:1, :], 1e-5,
                                              ps_c[0:1, :], OP.add, OP.subtract))
                    v.wait_ge(sem["act"].h, act0 + 2 * i_lin + 2)
                    vf(v.reciprocal(row2[0:1, :], row2[0:1, :]))
                    v.wait_ge(sem["pe"].h, pe_h(i_lin, 3))
                    vf(v.tensor_tensor(sc["tB"][:], sc["tB"][:], ps_c[:], OP.subtract))
                    v.wait_ge(sem["pe"].h, pe_h(i_lin, 4))
                    vf(v.tensor_tensor(sc["tB"][:], sc["tB"][:], ps_d[:], OP.mult))
                    vf(v.tensor_scalar(dst, sc["tB"][:], sc["lnhg"][:, 0:1],
                                       sc["lnhg"][:, 1:2], OP.mult, OP.add))
                    vf(v.tensor_tensor(sc["tC"][:], dst, dst, OP.mult))
                    v.wait_ge(sem["pe"].h, pe_h(i_lin, 5))
                    vf(v.tensor_reduce(sc["nrm"][0:1, norm_col0 + i_lin:norm_col0 + i_lin + 1],
                                       ps_c[0:1, 0:1024], axis=mybir.AxisListType.X,
                                       op=OP.max))
                    assert sem["dve"].n == base + DL
        assert sem["dve"].n == dve0 + DL * 2 * npair

        if csrc_dma is not None:
            @blk.gpsimd
            def _(g):
                for j in range(npair):
                    for o in range(2):
                        i_lin = 2 * j + o
                        h = pairs[j][o]
                        g.wait_ge(sem["dve"].h, dve0 + DL * i_lin + 12)
                        csrc_dma(g, i_lin, h)
            sem["csrc"].n = csrc0 + npair * 2 * war_gap * 16
    return



def build_attn_b1(ctx, sc, W, Wf, XH, XQ, XK, MROW, CSRC_t, CDST_t, tag):
    nc = ctx.nc
    sem = ctx.sem
    ps_a, ps_b, ps_c, ps_d = sc["ps_a"], sc["ps_b"], sc["ps_c"], sc["ps_d"]
    CSRC = CSRC_t.ap()
    CDST = CDST_t.ap()
    build_ln(ctx, sc, tag, to_hbm=XH)
    build_loadrope(ctx, sc, ctx.CT, "ropeB_c", "ropeB_s", tag)

    reps = [("lnqg", sc["kbuf"][0]), ("lnqb", sc["kbuf"][1]),
            ("lnkg", sc["vbuf"][0]), ("lnkb", sc["vbuf"][1])]
    pe0 = sem["pe"].n
    act0 = sem["act"].n
    msb0 = sem["msb"].n
    with nc.Block(f"bp_{tag}") as blk:
        @blk.sync
        def _(sy):
            wait(sy, sem["pe"], sem["act"], sem["dve"])
            for ri, (nm, dstt) in enumerate(reps):
                for hf in range(2):
                    it = 2 * ri + hf
                    if it >= 1:
                        sy.wait_ge(sem["pe"].h, pe0 + it)
                    sy.dma_start(sc["rcp"][0:1, :],
                                 Wf[nm][0:1, 1024 * hf:1024 * (hf + 1)]
                                 ).then_inc(sem["msb"].h, 16)
            sy.dma_start(sc["bqsb"][:, 0:16],
                         Wf["bq"].rearrange("o (c p) -> p (o c)", p=128),
                         allow_slow_non_contiguous=True).then_inc(sem["msd"].h, 16)
            sy.dma_start(sc["bksb"][:, 0:16],
                         Wf["bkv"][0:1, 0:2048].rearrange("o (c p) -> p (o c)", p=128),
                         allow_slow_non_contiguous=True).then_inc(sem["msd"].h, 16)
            sy.dma_start(sc["bvsb"][:, 0:16],
                         Wf["bkv"][0:1, 2048:4096].rearrange("o (c p) -> p (o c)", p=128),
                         allow_slow_non_contiguous=True).then_inc(sem["msd"].h, 16)
            sy.dma_start(sc["lnhg"][:, 0:1],
                         Wf["lnhg"].rearrange("o p -> p o"),
                         allow_slow_non_contiguous=True).then_inc(sem["msd"].h, 16)
            sy.dma_start(sc["lnhg"][:, 1:2],
                         Wf["lnhb"].rearrange("o p -> p o"),
                         allow_slow_non_contiguous=True).then_inc(sem["msd"].h, 16)
        sem["msb"].n = msb0 + 8 * 16
        sem["msd"].n += 5 * 16

        @blk.tensor
        def _(t):
            for it in range(8):
                t.wait_ge(sem["msb"].h, msb0 + (it + 1) * 16)
                if it >= 2:
                    t.wait_ge(sem["act"].h, act0 + it - 1)
                mm = mm512(t, ps_a if it % 2 == 0 else ps_b,
                           sc["ones_row_f"][:], sc["rcp"][0:1, :],
                           start=True, stop=True)
                mm.then_inc(sem["pe"].h, 1)
                t.wait_ge(sem["pe"].h, pe0 + it + 1)
        sem["pe"].n = pe0 + 8

        @blk.scalar
        def _(s):
            for it in range(8):
                ri, hf = divmod(it, 2)
                s.wait_ge(sem["pe"].h, pe0 + it + 1)
                dstt = reps[ri][1]
                s.copy(dstt[:, 1024 * hf:1024 * (hf + 1)],
                       (ps_a if it % 2 == 0 else ps_b)[:]).then_inc(sem["act"].h, 1)
                s.wait_ge(sem["act"].h, act0 + it + 1)
        sem["act"].n = act0 + 8

    pe0 = sem["pe"].n
    act0 = sem["act"].n
    dve0 = sem["dve"].n
    msc0 = sem["msc"].n
    xqs = [sem["q0"], sem["q1"]]
    xks = [sem["vr0"], sem["vr1"]]
    xq0 = [xqs[0].n, xqs[1].n]
    xk0 = [xks[0].n, xks[1].n]
    NVT = 27
    bn = sc["smallf"]
    with nc.Block(f"px_{tag}") as blk:
        @blk.gpsimd
        def _(g):
            wait(g, sem["xh"], sem["act"])
            for i in range(NT):
                if i >= 2:
                    g.wait_ge(sem["dve"].h, dve0 + (i - 1) * NVT)
                g.dma_start(sc["xhat"][:, D * (i % 2):D * (i % 2 + 1)],
                            XH[128 * i:128 * (i + 1), :]).then_inc(sem["msc"].h, 16)
                g.wait_ge(sem["dve"].h, dve0 + i * NVT + NVT)
                g.dma_start(XQ[128 * i:128 * (i + 1), :],
                            sc["ctxh"][:, 2048 * (i % 2):2048 * (i % 2 + 1)]
                            ).then_inc(xqs[i % 2].h, 16)
                g.dma_start(XK[128 * i:128 * (i + 1), :],
                            sc["ctxh"][:, 4096 + 2048 * (i % 2):4096 + 2048 * (i % 2 + 1)]
                            ).then_inc(xks[i % 2].h, 16)
        sem["msc"].n = msc0 + NT * 16
        for p in range(2):
            xqs[p].n = xq0[p] + (NT // 2) * 16
            xks[p].n = xk0[p] + (NT // 2) * 16

        @blk.vector
        def _(v):
            def vf(instr):
                sem["dve"].inc(instr)
                v.wait_ge(sem["dve"].h, sem["dve"].n)
            wait(v, sem["act"])
            for i in range(NT):
                base = sem["dve"].n
                v.wait_ge(sem["msc"].h, msc0 + (i + 1) * 16)
                if i >= 2:
                    p = i % 2
                    v.wait_ge(xqs[p].h, xq0[p] + (i // 2) * 16)
                    v.wait_ge(xks[p].h, xk0[p] + (i // 2) * 16)
                xt = sc["xhat"][:, D * (i % 2):D * (i % 2 + 1)]
                st = None
                for c4 in range(4):
                    st = v.bn_stats(bn[:, 6 * c4:6 * c4 + 6], xt[:, 512 * c4:512 * (c4 + 1)])
                vf(st)
                vf(v.bn_aggr(bn[:, 24:26], bn[:, 0:24]))
                vf(v.tensor_scalar(bn[:, 26:27], bn[:, 25:26], 1e-5, None, OP.add))
                vf(v.tensor_copy(bn[:, 30:31], bn[:, 26:27]))
                v.wait_ge(sem["act"].h, act0 + 2 * i + 1)
                vf(v.reciprocal(bn[:, 27:28], bn[:, 30:31]))
                vf(v.tensor_tensor(bn[:, 28:29], bn[:, 24:25], bn[:, 27:28], OP.mult))
                vf(v.tensor_scalar(bn[:, 28:29], bn[:, 28:29], -1.0, None, OP.mult))
                vf(v.tensor_tensor(bn[:, 31:32], bn[:, 24:25], bn[:, 24:25], OP.mult))
                vf(v.tensor_tensor(bn[:, 31:32], bn[:, 31:32], bn[:, 25:26], OP.add))
                vf(v.tensor_scalar(bn[:, 31:32], bn[:, 31:32], 2048.0, None, OP.mult))
                v.wait_ge(sem["act"].h, act0 + 2 * i + 2)
                vf(v.tensor_copy(sc["qsc"][:, i:i + 1], bn[:, 31:32]))
                for kind in range(2):
                    grep = (sc["kbuf"] if kind == 0 else sc["vbuf"])[0]
                    brep = (sc["kbuf"] if kind == 0 else sc["vbuf"])[1]
                    dst0 = 2048 * (i % 2) if kind == 0 else 4096 + 2048 * (i % 2)
                    for hf in range(2):
                        vf(v.tensor_scalar(sc["tA"][:], xt[:, 1024 * hf:1024 * (hf + 1)],
                                           bn[:, 27:28], bn[:, 28:29], OP.mult, OP.add))
                        vf(v.tensor_tensor(sc["tA"][:], sc["tA"][:],
                                           grep[:, 1024 * hf:1024 * (hf + 1)], OP.mult))
                        vf(v.tensor_tensor(sc["ctxh"][:, dst0 + 1024 * hf:dst0 + 1024 * (hf + 1)],
                                           sc["tA"][:], brep[:, 1024 * hf:1024 * (hf + 1)],
                                           OP.add))
                assert sem["dve"].n == base + 23, sem["dve"].n - base
                while sem["dve"].n < base + NVT:
                    sem["dve"].inc(v.nop())
                    v.wait_ge(sem["dve"].h, sem["dve"].n)

        @blk.scalar
        def _(s):
            for i in range(NT):
                s.wait_ge(sem["dve"].h, dve0 + i * NVT + 4)
                s.sqrt(bn[:, 30:31], bn[:, 30:31]).then_inc(sem["act"].h, 1)
                s.wait_ge(sem["act"].h, act0 + 2 * i + 1)
                s.wait_ge(sem["dve"].h, dve0 + i * NVT + 10)
                s.sqrt(bn[:, 31:32], bn[:, 31:32]).then_inc(sem["act"].h, 1)
                s.wait_ge(sem["act"].h, act0 + 2 * i + 2)
        sem["act"].n = act0 + 2 * NT

    pe0 = sem["pe"].n
    act0 = sem["act"].n
    dve0 = sem["dve"].n
    msc0 = sem["msc"].n
    with nc.Block(f"pm_{tag}") as blk:
        @blk.tensor
        def _(t):
            wait(t, sem["dve"], sem["act"])
            mm = t.matmul(ps_a[0:8, 0:128], sc["qsc"][:, 0:8], sc["id128_bf"][:],
                          start=True, stop=True)
            mm.then_inc(sem["pe"].h, 1)
            t.wait_ge(sem["pe"].h, pe0 + 1)
            t.wait_ge(sem["msc"].h, msc0 + 32)
            mm = mm512(t, ps_c, sc["ones_row_f"][:], sc["rcp"][0:1, :],
                       start=True, stop=True)
            mm.then_inc(sem["pe"].h, 1)
            t.wait_ge(sem["pe"].h, pe0 + 2)
        sem["pe"].n = pe0 + 2

        @blk.scalar
        def _(s):
            s.wait_ge(sem["pe"].h, pe0 + 1)
            s.copy(sc["b2row"][0:8, 0:128], ps_a[0:8, 0:128]).then_inc(sem["act"].h, 1)
            s.wait_ge(sem["act"].h, act0 + 1)
        sem["act"].n = act0 + 1

        @blk.gpsimd
        def _(g):
            g.wait_ge(sem["act"].h, act0 + 1)
            g.dma_start(MROW[:, :], sc["b2row"][0:8, 0:128]).then_inc(sem["msc"].h, 16)
            g.wait_ge(sem["msc"].h, msc0 + 16)
            g.dma_start(sc["rcp"][0:1, :],
                        MROW.rearrange("a b -> (a b)")).then_inc(sem["msc"].h, 16)
        sem["msc"].n = msc0 + 32

        @blk.vector
        def _(v):
            v.wait_ge(sem["pe"].h, pe0 + 2)
            wait(v, sem["tab"])
            sem["dve"].inc(v.tensor_tensor(sc["ropeC_t"][:], sc["ropeC_t"][:],
                                           ps_c[:], OP.mult))
            v.wait_ge(sem["dve"].h, sem["dve"].n)
            sem["dve"].inc(v.tensor_tensor(sc["ropeS_t"][:], sc["ropeS_t"][:],
                                           ps_c[:], OP.mult))
            v.wait_ge(sem["dve"].h, sem["dve"].n)

    with nc.Block(f"xk_{tag}") as blk:
        @blk.sync
        def _(sy):
            wait(sy, sem["vr0"], sem["vr1"], sem["dve"])
            for j in range(NF):
                sem["xt"].inc(sy.dma_start(sc["xhatT"][:, TLOC * j:TLOC * (j + 1)],
                                           XK[:, 128 * j:128 * (j + 1)], transpose=True), 16)

    def kdst(i_lin, h):
        return sc["pT"][i_lin % 2][:]

    def kdma(g, i_lin, h):
        g.dma_start(CSRC[128 * h:128 * (h + 1), 0:1024],
                    sc["pT"][i_lin % 2][:]).then_inc(ctx.sem["csrc"].h, 16)

    build_headproj_l1b(ctx, sc, W["wkv"], 0, list(range(16)), kdst, 0,
                       sc["bksb"], 0, tag + "K", csrc_dma=kdma, war_gap=1)
    build_vstage(ctx, sc, W["wkv"], CSRC, tag, natural=True, nat_col0=2048)
    build_gather(ctx, sc, CSRC_t, CDST_t, 16, tag)

    with nc.Block(f"xq_{tag}") as blk:
        @blk.sync
        def _(sy):
            wait(sy, sem["q0"], sem["q1"], sem["dve"], sem["pe"])
            for j in range(NF):
                sem["xt"].inc(sy.dma_start(sc["xhatT"][:, TLOC * j:TLOC * (j + 1)],
                                           XQ[:, 128 * j:128 * (j + 1)], transpose=True), 16)

    for phase in range(2):
        heads = list(range(8 * phase, 8 * phase + 8))

        def qdst(i_lin, h):
            return sc["qbuf"][:, TLOC * i_lin:TLOC * (i_lin + 1)]

        build_headproj_l1b(ctx, sc, W["wq"], 1024 * phase, heads, qdst,
                           32 + 8 * phase, sc["bqsb"], 8 * phase, f"{tag}Q{phase}")
        build_bias(ctx, sc, CDST, phase, 8, 128, tag, bcol_key="bcolr0")
        build_attn_phase(ctx, sc, CDST, phase, heads, 128, tag,
                         bv_sb=sc["bvsb"], bv_col0=0, causal=False)
        build_outproj_nat(ctx, sc, W["wo"], phase, tag)

    pe0 = sem["pe"].n
    dve0 = sem["dve"].n
    msc0 = sem["msc"].n
    with nc.Block(f"bo_{tag}") as blk:
        @blk.gpsimd
        def _(g):
            wait(g, sem["pe"])
            for ocg in range(4):
                if ocg >= 1:
                    g.wait_ge(sem["pe"].h, pe0 + ocg)
                g.dma_start(sc["b2row"][0:1, 0:512],
                            W["bo"][0:1, 512 * ocg:512 * (ocg + 1)]).then_inc(sem["msc"].h, 16)
        sem["msc"].n = msc0 + 4 * 16

        @blk.tensor
        def _(t):
            wait(t, sem["dve"], sem["act"])
            for ocg in range(4):
                t.wait_ge(sem["msc"].h, msc0 + (ocg + 1) * 16)
                if ocg >= 1:
                    t.wait_ge(sem["dve"].h, dve0 + ocg * NT)
                mm = mm512(t, ps_a[:, 0:512], sc["ones_row_bf"][:],
                           sc["b2row"][0:1, 0:512], start=True, stop=True, width=512)
                mm.then_inc(sem["pe"].h, 1)
                t.wait_ge(sem["pe"].h, pe0 + ocg + 1)
        sem["pe"].n = pe0 + 4

        @blk.vector
        def _(v):
            for ocg in range(4):
                v.wait_ge(sem["pe"].h, pe0 + ocg + 1)
                for tb in range(NT):
                    xt = sc["x"][tb]
                    sem["dve"].inc(v.tensor_tensor(
                        xt[:, 512 * ocg:512 * (ocg + 1)],
                        xt[:, 512 * ocg:512 * (ocg + 1)], ps_a[:, 0:512], OP.add))
                    v.wait_ge(sem["dve"].h, sem["dve"].n)
        assert sem["dve"].n == dve0 + 4 * NT
    return


def build_outproj_nat(ctx, sc, WO, phase, tag):
    def rhs_load(eng, oc, f, dest):
        dglob = phase * 8 + f
        return eng.dma_start(dest,
                             WO[128 * dglob:128 * (dglob + 1), 256 * oc:256 * (oc + 1)])

    def lhs(f, tb):
        return sc["ctxh"][:, TLOC * f + 128 * tb:TLOC * f + 128 * (tb + 1)]

    def evac(v, ps_ap, tb, oc):
        xt = sc["x"][tb]
        return v.tensor_tensor(xt[:, 256 * oc:256 * (oc + 1)],
                               xt[:, 256 * oc:256 * (oc + 1)], ps_ap, OP.add)

    build_tokmaj_mm(ctx, sc, lhs, 8, rhs_load, 8, 256, evac,
                    tag + "on" + str(phase), evac_eng="vector")
    return
